# version 07
# baseline (speedup 1.0000x reference)
import math
from contextlib import ExitStack

import numpy as np
import ml_dtypes

import concourse.bass as bass
import concourse.mybir as mybir
from concourse.bass_utils import run_bass_kernel_spmd

F32 = mybir.dt.float32
BF16 = mybir.dt.bfloat16
U8 = mybir.dt.uint8
AF = mybir.ActivationFunctionType
ALU = mybir.AluOpType
AX = mybir.AxisListType
BF = ml_dtypes.bfloat16

D = 4096
S = 4096
TPC = 1024
NT = TPC // 128
NI = D // 128
HPC = 8
HD = 128
EPS = 1e-5
ISQ = 1.0 / math.sqrt(128.0)


def _build_MT_cols():
    n = D
    i = np.arange(n, dtype=np.float64)
    M = np.zeros((n, n))
    M[:, 0] = 1.0 / math.sqrt(n)
    f = np.arange(1, (n - 1) // 2 + 1, dtype=np.float64)
    ang = 2.0 * np.pi * np.outer(i, f) / n
    M[:, 1:-1:2] = math.sqrt(2.0 / n) * np.cos(ang)
    M[:, 2:-1:2] = math.sqrt(2.0 / n) * np.sin(ang)
    M[:, n - 1] = (1.0 / math.sqrt(n)) * np.where(i % 2 == 0, 1.0, -1.0)
    MT = M.T.astype(np.float32)
    cols = MT.reshape(n, 8, 512).transpose(1, 0, 2)
    out = np.stack([_pmaj(cols[oc]) for oc in range(8)])
    return np.ascontiguousarray(out.astype(BF))


def _rope_tables():
    hd = HD
    inv = (1.0 / (10000.0 ** (np.arange(0, hd, 2, dtype=np.float32) / hd))).astype(np.float32)
    pos = np.arange(S, dtype=np.float32)
    fr = pos[:, None] * inv[None, :]
    emb = np.concatenate([fr, fr], axis=-1).astype(np.float32)
    return (np.ascontiguousarray(np.cos(emb).T.astype(np.float32)),
            np.ascontiguousarray(np.sin(emb).T.astype(np.float32)))


def _tile_w(wT):
    wT = np.asarray(wT)
    K, N = wT.shape
    t = wT.reshape(K // 128, 128, N // 128, 128)
    return np.ascontiguousarray(t.transpose(2, 1, 0, 3).reshape(N // 128, 128, K))


def _pmaj(a):
    a = np.asarray(a)
    JP, R = a.shape
    return np.ascontiguousarray(a.reshape(JP // 128, 128, R).transpose(1, 0, 2)
                                .reshape(128, (JP // 128) * R))


def _tri_mask():
    out = np.zeros((128, 4 * 512), np.float32)
    p = np.arange(128)[:, None]
    j = np.arange(512)[None, :]
    for s_ in range(4):
        out[:, s_ * 512:(s_ + 1) * 512] = np.where(j < s_ * 128 + p, -1e30, 0.0)
    return np.ascontiguousarray(out.astype(BF))


def _l1_decays():
    HEADS_ = 32
    BLK = 256
    base = 1.0 / 2.0 ** (8.0 / HEADS_)
    slope = (base ** (np.arange(HEADS_, dtype=np.float64) + 1.0)) * (1.0 - 1 / (31 + 1e-05) + 1e-05)
    r = np.arange(BLK, dtype=np.float64) + 1.0
    qd = np.exp(-slope[:, None] * r[None, :])
    kd = np.exp(-slope[:, None] * (BLK - r)[None, :])
    diff = r[:, None] - r[None, :]
    Ddec = np.exp(np.where(diff >= 0, -slope[:, None, None] * diff, -np.inf))
    bd = np.exp(-slope * BLK)
    return (qd.astype(np.float32), kd.astype(np.float32),
            np.ascontiguousarray(Ddec.astype(BF)), bd)


class Ctx:
    def __init__(self, nc, es):
        self.nc = nc
        self.es = es
        self.sem = {}
        self.cnt = {}

    def new_sem(self, name):
        assert name not in self.sem, name
        self.sem[name] = self.es.enter_context(self.nc.semaphore(name))
        self.cnt[name] = 0
        return self.sem[name]

    def sb(self, name, shape, dt):
        return self.es.enter_context(self.nc.sbuf_tensor(name, shape, dt))

    def ps(self, name, shape, dt=F32):
        return self.es.enter_context(self.nc.psum_tensor(name, shape, dt))

    def serw(self, e, name):
        e.wait_ge(self.sem[name], self.cnt[name])

    def serm(self, inst, name):
        self.cnt[name] += 1
        inst.then_inc(self.sem[name], 1)


def new_ctx(name):
    nc = bass.Bass("TRN2", target_bir_lowering=False, name=name)
    es = ExitStack()
    return Ctx(nc, es)


class LnBufs:
    def __init__(self, ctx, nt=NT, nxn=2, xa2=True):
        self.xa2 = xa2
        self.xa = ctx.sb("lnb_xa", [128, (2 if xa2 else 1) * D], F32)
        self.st = ctx.sb("lnb_st", [128, 4 * nt], F32)
        self.nxn = nxn
        self.xn = ctx.sb("lnb_xn", [128, nxn * D], BF16)
        self.eps = ctx.sb("lnb_eps", [128, 1], F32)


def emit_ln(ctx, block, tag, x_dram, xnT_sb, ident, lnb, nt=NT, xn_bf_store=None,
            build_xnT=True, ptrans=None, wait_sync=None, wait_vec=None,
            wait_pe=None):
    sx, sv, ss, sp = (tag + c for c in "xvsp")
    sx1 = tag + "x1"
    sc2 = tag + "c2"
    for s_ in (sx, sx1, sv, ss, sp, sc2):
        ctx.new_sem(s_)
    sxp = (sx, sx1)
    so0, so1 = tag + "o0", tag + "o1"
    if xn_bf_store is not None:
        ctx.new_sem(so0)
        ctx.new_sem(so1)
    xa, st, xn = lnb.xa, lnb.st, lnb.xn
    sem, cnt = ctx.sem, ctx.cnt

    def f_sync(e):
        if wait_sync is not None:
            for ws_, wc_ in wait_sync:
                e.wait_ge(sem[ws_], wc_)
        dep = 2 if lnb.xa2 else 1
        for tt in range(nt):
            if tt >= dep:
                e.wait_ge(sem[sv], tt - dep + 1)
                e.wait_ge(sem[ss], tt - dep + 1)
            sl = (tt % 2) if lnb.xa2 else 0
            cnt[sxp[tt % 2]] += 16
            e.dma_start(out=xa[:, sl * D:(sl + 1) * D],
                        in_=x_dram[tt * 128:(tt + 1) * 128, :]
                        ).then_inc(sem[sxp[tt % 2]], 16)
    block.sync(f_sync)

    def f_vec(e):
        for tt in range(nt):
            e.wait_ge(sem[sxp[tt % 2]], 16 * (tt // 2 + 1))
            sl = (tt % 2) if lnb.xa2 else 0
            cnt[sv] += 1
            e.reduce_sum(st[:, tt:tt + 1], xa[:, sl * D:(sl + 1) * D],
                         axis=AX.X).then_inc(sem[sv], 1)
    block.vector(f_vec)

    bss0 = cnt[ss]

    def f_sc(e):
        for tt in range(nt):
            e.wait_ge(sem[sxp[tt % 2]], 16 * (tt // 2 + 1))
            if tt >= 1:
                e.wait_ge(sem[ss], bss0 + tt)
            sl = (tt % 2) if lnb.xa2 else 0
            cnt[ss] += 1
            e.activation(xn[:, 0:D], xa[:, sl * D:(sl + 1) * D], AF.Square,
                         accum_out=st[:, nt + tt:nt + tt + 1]).then_inc(sem[ss], 1)
    block.scalar(f_sc)

    def f_vstat(e):
        e.wait_ge(sem[sv], cnt[sv])
        e.wait_ge(sem[ss], cnt[ss])
        e.memset(lnb.eps[:, :], EPS)
        ctx.serm(e.tensor_scalar(out=st[:, 0:nt], in0=st[:, 0:nt], scalar1=1.0 / D,
                                 scalar2=None, op0=ALU.mult), sv)
        ctx.serw(e, sv)
        ctx.serm(e.tensor_mul(st[:, 2 * nt:3 * nt], st[:, 0:nt], st[:, 0:nt]), sv)
        ctx.serw(e, sv)
        ctx.serm(e.scalar_tensor_tensor(out=st[:, nt:2 * nt], in0=st[:, nt:2 * nt],
                                        scalar=1.0 / D, in1=st[:, 2 * nt:3 * nt],
                                        op0=ALU.mult, op1=ALU.subtract), sv)
    block.vector(f_vstat)

    def f_sstat(e):
        e.wait_ge(sem[sv], cnt[sv])
        cnt[ss] += 1
        e.activation(st[:, 2 * nt:3 * nt], st[:, nt:2 * nt], AF.Sqrt,
                     bias=lnb.eps[:, :]).then_inc(sem[ss], 1)
    block.scalar(f_sstat)

    def f_vrec(e):
        e.wait_ge(sem[ss], cnt[ss])
        cnt[sv] += 1
        e.reciprocal(st[:, 3 * nt:4 * nt], st[:, 2 * nt:3 * nt]).then_inc(sem[sv], 1)
    block.vector(f_vrec)

    bx2 = (cnt[sx], cnt[sx1])
    bv, bp = cnt[sv], cnt[sp]
    bc2 = cnt[sc2]
    per_tt = 1

    def _store(e, tt):
        e.wait_ge(sem[sv], bv + tt * per_tt + 1)
        sop = so0 if tt % 2 == 0 else so1
        cnt[sop] += 16
        e.dma_start(out=xn_bf_store[tt * 128:(tt + 1) * 128, :],
                    in_=xn[:, (tt % lnb.nxn) * D:(tt % lnb.nxn + 1) * D]).then_inc(sem[sop], 16)

    def f_sync2(e):
        e.wait_ge(sem[sv], bv)
        e.wait_ge(sem[ss], cnt[ss])
        dep = 2 if lnb.xa2 else 1
        for tt in range(nt):
            if tt >= dep:
                e.wait_ge(sem[sv], bv + (tt - dep) * per_tt + 1)
            sl = (tt % 2) if lnb.xa2 else 0
            cnt[sxp[tt % 2]] += 16
            e.dma_start(out=xa[:, sl * D:(sl + 1) * D],
                        in_=x_dram[tt * 128:(tt + 1) * 128, :]
                        ).then_inc(sem[sxp[tt % 2]], 16)
            if xn_bf_store is not None and tt >= 1:
                _store(e, tt - 1)
        if xn_bf_store is not None:
            _store(e, nt - 1)
    block.sync(f_sync2)

    def f_vec2(e):
        if wait_vec is not None:
            for ws_, wc_ in wait_vec:
                e.wait_ge(sem[ws_], wc_)
        for tt in range(nt):
            e.wait_ge(sem[sxp[tt % 2]], bx2[tt % 2] + 16 * (tt // 2 + 1))
            if tt >= 1:
                e.wait_ge(sem[sv], bv + tt * per_tt)
            if tt >= 2 and xn_bf_store is not None:
                e.wait_ge(sem[so0 if tt % 2 == 0 else so1], 16 * ((tt - 2) // 2 + 1))
            if tt >= lnb.nxn and build_xnT:
                e.wait_ge(sem[sp], bp + (tt - lnb.nxn + 1) * NI)
            sl = (tt % 2) if lnb.xa2 else 0
            cnt[sv] += 1
            e.tensor_scalar(out=xn[:, (tt % lnb.nxn) * D:(tt % lnb.nxn + 1) * D],
                            in0=xa[:, sl * D:(sl + 1) * D],
                            scalar1=st[:, tt:tt + 1],
                            scalar2=st[:, 3 * nt + tt:3 * nt + tt + 1],
                            op0=ALU.subtract, op1=ALU.mult).then_inc(sem[sv], 1)
    block.vector(f_vec2)

    if build_xnT:
        def f_sc2(e):
            if wait_vec is not None:
                for ws_, wc_ in wait_vec:
                    e.wait_ge(sem[ws_], wc_)
            for tt in range(nt):
                for i in range(NI):
                    k = tt * NI + i
                    e.wait_ge(sem[sp], bp + k + 1)
                    cnt[sc2] += 1
                    e.copy(xnT_sb[:, i * (nt * 128) + tt * 128:
                                  i * (nt * 128) + (tt + 1) * 128],
                           ptrans[:, (k % 2) * 1024:(k % 2) * 1024 + 128]
                           ).then_inc(sem[sc2], 1)
        block.scalar(f_sc2)

    if build_xnT:
        def f_pe2(e):
            if wait_pe is not None:
                for ws_, wc_ in wait_pe:
                    e.wait_ge(sem[ws_], wc_)
            for tt in range(nt):
                e.wait_ge(sem[sv], bv + tt * per_tt + 1)
                for i in range(NI):
                    k = tt * NI + i
                    if k >= 2:
                        e.wait_ge(sem[sc2], bc2 + k - 1)
                    cnt[sp] += 1
                    e.transpose(ptrans[:, (k % 2) * 1024:(k % 2) * 1024 + 128],
                                xn[:, (tt % lnb.nxn) * D + i * 128:
                                   (tt % lnb.nxn) * D + (i + 1) * 128],
                                ident[:, :]).then_inc(sem[sp], 1)
        block.tensor(f_pe2)

    if build_xnT:
        return sc2, cnt[sc2]
    return sv, cnt[sv]


def emit_fproj(ctx, block, tag, *, w_dram, nko, kt, rhs_slice, n_tc,
               tc_width=512, extras=None, wait_tensor=(), evac_mode,
               ident=None, ptrans=None,
               x_in_dram=None, x_out_dram=None, res_scale=1.0,
               evac_bias=None, out_sb=None, act=None, out_dram_bf=None,
               wait_gpsimd=None, wait_sync_first=None, wait_scalar_first=None,
               scoped=False, shared_bufs=None):
    sw0, sw1, sp, sc = tag + "w0", tag + "w1", tag + "p", tag + "c"
    for s_ in (sw0, sw1, sp, sc):
        ctx.new_sem(s_)
    sem, cnt = ctx.sem, ctx.cnt
    K = kt * 128
    TW = n_tc * tc_width
    fscope = ExitStack()
    nc_ = ctx.nc
    if shared_bufs is not None:
        w_sb, pss = shared_bufs[0], shared_bufs[1]
    else:
        w_sb = fscope.enter_context(nc_.sbuf_tensor(tag + "_w", [128, 2 * K], BF16))
        pss = fscope.enter_context(nc_.psum_tensor(tag + "_ps", [128, 2 * TW], F32))

    def _wl(w):
        if w is None:
            return []
        if isinstance(w, tuple):
            return [w]
        return list(w)

    def f_g(e):
        for ws_, wc_ in _wl(wait_gpsimd):
            e.wait_ge(sem[ws_], wc_)
        for ot in range(nko):
            if ot >= 2:
                e.wait_ge(sem[sc], ot - 1)
            swp = sw0 if ot % 2 == 0 else sw1
            cnt[swp] += 16
            e.dma_start(
                out=w_sb[:, (ot % 2) * K:(ot % 2 + 1) * K],
                in_=w_dram.ap()[ot],
            ).then_inc(sem[swp], 16)
    block.gpsimd(f_g)

    if evac_mode in ("resadd", "dram_bf"):
        if shared_bufs is not None:
            obf = shared_bufs[2]
        else:
            obf = fscope.enter_context(nc_.sbuf_tensor(tag + "_obf", [128, 2 * TW], BF16))
    if evac_mode == "resadd":
        st_, sv_ = tag + "t", tag + "v"
        for s_ in (st_, sv_):
            ctx.new_sem(s_)
        ntt = TW // 128
        sxs = [tag + "x%d" % q for q in range(3)]
        sos = [tag + "o%d" % q for q in range(3)]
        for s_ in sxs + sos:
            ctx.new_sem(s_)
        if shared_bufs is not None:
            xsl = shared_bufs[3]
        else:
            xsl = fscope.enter_context(
                nc_.sbuf_tensor(tag + "_xsl", [128, 3 * ntt * 128], F32))
    if evac_mode == "dram_bf":
        sos2 = [tag + "o0", tag + "o1"]
        for s_ in sos2:
            ctx.new_sem(s_)

    def _emit_transposes(e, ot):
        e.wait_ge(sem[sc], ot + 1)
        for tt in range(ntt):
            k = ot * ntt + tt
            if k >= 2:
                e.wait_ge(sem[sv_], k - 1)
            cnt[st_] += 1
            e.transpose(ptrans[:, (k % 2) * 1024:(k % 2) * 1024 + 128],
                        obf[:, (ot % 2) * TW + tt * 128:(ot % 2) * TW + (tt + 1) * 128],
                        ident[:, :]).then_inc(sem[st_], 1)

    def f_t(e):
        for s_, c_ in wait_tensor:
            e.wait_ge(sem[s_], c_)
        for ot in range(nko):
            e.wait_ge(sem[sw0 if ot % 2 == 0 else sw1], 16 * (ot // 2 + 1))
            if ot >= 2:
                e.wait_ge(sem[sc], ot - 1)
            for tc in range(n_tc):
                pslice = pss[:, (ot % 2) * TW + tc * tc_width:
                             (ot % 2) * TW + (tc + 1) * tc_width]
                for i in range(kt):
                    mm = e.matmul(pslice,
                                  w_sb[:, (ot % 2) * K + i * 128:(ot % 2) * K + (i + 1) * 128],
                                  rhs_slice(i, tc),
                                  start=(i == 0),
                                  stop=(i == kt - 1 and extras is None))
                if extras is not None:
                    mm = extras(e, ot, tc, pslice)
            cnt[sp] += 1
            mm.then_inc(sem[sp], 1)
            if evac_mode == "resadd" and ot >= 1:
                _emit_transposes(e, ot - 1)
        if evac_mode == "resadd":
            _emit_transposes(e, nko - 1)
    block.tensor(f_t)

    if evac_mode == "resadd":
        def f_sc_(e):
            for ws_, wc_ in _wl(wait_scalar_first):
                e.wait_ge(sem[ws_], wc_)
            for ot in range(nko):
                e.wait_ge(sem[sp], ot + 1)
                if ot >= 2:
                    e.wait_ge(sem[st_], (ot - 1) * ntt)
                cnt[sc] += 1
                if evac_bias is not None:
                    e.activation(obf[:, (ot % 2) * TW:(ot % 2 + 1) * TW],
                                 pss[:, (ot % 2) * TW:(ot % 2 + 1) * TW],
                                 AF.Identity, bias=evac_bias(ot)).then_inc(sem[sc], 1)
                else:
                    e.copy(obf[:, (ot % 2) * TW:(ot % 2 + 1) * TW],
                           pss[:, (ot % 2) * TW:(ot % 2 + 1) * TW]
                           ).then_inc(sem[sc], 1)
        block.scalar(f_sc_)

        def f_v(e):
            for k in range(nko * ntt):
                g, tt = k // ntt, k % ntt
                e.wait_ge(sem[st_], k + 1)
                e.wait_ge(sem[sxs[g % 3]], 16 * ntt * (g // 3 + 1))
                xs = xsl[:, ((g % 3) * ntt + tt) * 128:((g % 3) * ntt + tt + 1) * 128]
                cnt[sv_] += 1
                e.scalar_tensor_tensor(out=xs,
                                       in0=ptrans[:, (k % 2) * 1024:(k % 2) * 1024 + 128],
                                       scalar=res_scale,
                                       in1=xs,
                                       op0=ALU.mult, op1=ALU.add).then_inc(sem[sv_], 1)
        block.vector(f_v)

        def f_sxo(e):
            for ws_, wc_ in _wl(wait_sync_first):
                e.wait_ge(sem[ws_], wc_)
            for g in range(nko + 2):
                if g < nko:
                    if g >= 3:
                        e.wait_ge(sem[sos[g % 3]],
                                  16 * ntt * ((g - 3) // 3 + 1))
                    for tt in range(ntt):
                        cnt[sxs[g % 3]] += 16
                        e.dma_start(out=xsl[:, ((g % 3) * ntt + tt) * 128:
                                            ((g % 3) * ntt + tt + 1) * 128],
                                    in_=x_in_dram[tt * 128:(tt + 1) * 128,
                                                  g * 128:(g + 1) * 128]
                                    ).then_inc(sem[sxs[g % 3]], 16)
                if g >= 2:
                    go = g - 2
                    for tt in range(ntt):
                        ko = go * ntt + tt
                        e.wait_ge(sem[sv_], ko + 1)
                        cnt[sos[go % 3]] += 16
                        e.dma_start(out=x_out_dram[tt * 128:(tt + 1) * 128,
                                                   go * 128:(go + 1) * 128],
                                    in_=xsl[:, ((go % 3) * ntt + tt) * 128:
                                            ((go % 3) * ntt + tt + 1) * 128]
                                    ).then_inc(sem[sos[go % 3]], 16)
        block.sync(f_sxo)
        if scoped:
            fscope.close()
        else:
            ctx.es.enter_context(fscope.pop_all())
        return [(s_, cnt[s_]) for s_ in sos]

    elif evac_mode == "sbuf":
        def f_sc_(e):
            for ot in range(nko):
                e.wait_ge(sem[sp], ot + 1)
                cnt[sc] += 1
                e.activation(out_sb[:, ot * TW:(ot + 1) * TW],
                             pss[:, (ot % 2) * TW:(ot % 2 + 1) * TW],
                             act if act is not None else AF.Copy).then_inc(sem[sc], 1)
        block.scalar(f_sc_)
        if scoped:
            fscope.close()
        else:
            ctx.es.enter_context(fscope.pop_all())
        return sc, cnt[sc]

    elif evac_mode == "dram_bf":
        def f_sc_(e):
            for ot in range(nko):
                e.wait_ge(sem[sp], ot + 1)
                if ot >= 2:
                    e.wait_ge(sem[sos2[ot % 2]], 16 * ((ot - 2) // 2 + 1))
                cnt[sc] += 1
                e.activation(obf[:, (ot % 2) * TW:(ot % 2 + 1) * TW],
                             pss[:, (ot % 2) * TW:(ot % 2 + 1) * TW],
                             act if act is not None else AF.Copy).then_inc(sem[sc], 1)
        block.scalar(f_sc_)

        def f_so(e):
            for ot in range(nko):
                e.wait_ge(sem[sc], ot + 1)
                cnt[sos2[ot % 2]] += 16
                e.dma_start(out=out_dram_bf[ot * 128:(ot + 1) * 128, :],
                            in_=obf[:, (ot % 2) * TW:(ot % 2 + 1) * TW]
                            ).then_inc(sem[sos2[ot % 2]], 16)
        block.sync(f_so)
        if scoped:
            fscope.close()
        else:
            ctx.es.enter_context(fscope.pop_all())
        return [(s_, cnt[s_]) for s_ in sos2]
    raise ValueError(evac_mode)


def emit_l0ffn(ctx, block, pfx, *, x_in, x_out, wbT, MTc, c0, m0, c1, m1, encT_d,
               bb_d, ident, ptrans, lnb, build_G, G_sb, g_dram_in=None,
               g_store=None, wait_sync=None, level=9, xnT_shared=None,
               psX=None, wait_vec=None, wait_gpsimd0=None, fscoped=False,
               fshared=None, fgates=None):
    nc = ctx.nc
    sem, cnt = ctx.sem, ctx.cnt
    xnT = xnT_shared if xnT_shared is not None else ctx.sb(pfx + "_xnT", [128, NI * TPC], BF16)
    eT = ctx.sb(pfx + "_eT", [41, TPC], BF16)
    encb = ctx.sb(pfx + "_encb", [128, NI * 40], BF16)
    ones = ctx.sb(pfx + "_ones", [1, TPC], BF16)
    psG = (psX[0:40, 0:1024] if psX is not None else None)
    if psG is None:
        psG_t = ctx.ps(pfx + "_psG", [40, 1024], F32)
        psG = psG_t[0:40, 0:1024]
    for s_ in ("ld", "vb", "gm", "pg", "sg", "eo", "gb", "gs"):
        ctx.new_sem(pfx + s_)

    gscope = ExitStack()
    if build_G:
        cwork = gscope.enter_context(nc.sbuf_tensor(pfx + "_cw", [128, 3 * 1024 + 3 * 256], F32))
        mwork = gscope.enter_context(nc.sbuf_tensor(pfx + "_mw", [128, 3 * 1024 + 3 * 256], U8))
        w01 = gscope.enter_context(nc.sbuf_tensor(pfx + "_w01", [128, 1024 + 256 + 1024], F32))
        wsum = gscope.enter_context(nc.sbuf_tensor(pfx + "_ws", [128, NI * 40], BF16))
        mtc = gscope.enter_context(nc.sbuf_tensor(pfx + "_mtc", [128, NI * 512], BF16))
    encf_tmp = gscope.enter_context(nc.sbuf_tensor(pfx + "_encf", [128, NI * 40], F32))

    def f_ld0(e):
        def dd(o, i_):
            cnt[pfx + "ld"] += 16
            e.dma_start(out=o, in_=i_).then_inc(sem[pfx + "ld"], 16)
        if build_G:
            for ee in range(3):
                dd(cwork[:, ee * 1024:(ee + 1) * 1024], c0.ap()[ee])
                dd(mwork[:, ee * 1024:(ee + 1) * 1024], m0.ap()[ee])
                dd(cwork[:, 3072 + ee * 256:3072 + (ee + 1) * 256], c1.ap()[ee])
                dd(mwork[:, 3072 + ee * 256:3072 + (ee + 1) * 256], m1.ap()[ee])
        else:
            dd(G_sb[:, :], g_dram_in[:, :])
        dd(encf_tmp[:, :], encT_d[:, :])
    block.sync(f_ld0)

    if build_G:
        def f_gb(e):
            cnt[pfx + "gb"] += 16
            e.dma_start(out=G_sb[40:41, :], in_=bb_d[:, :]).then_inc(sem[pfx + "gb"], 16)
        block.gpsimd(f_gb)

    def f_vb(e):
        vv = pfx + "vb"
        e.wait_ge(sem[pfx + "ld"], cnt[pfx + "ld"])
        if build_G:
            e.wait_ge(sem[pfx + "gb"], 16)
        ctx.serm(e.memset(ones[:, :], 1.0), vv)
        if build_G:
            for ee in range(3):
                ctx.serw(e, vv)
                ctx.serm(e.tensor_copy(w01[:, 1280:2304],
                                       mwork[:, ee * 1024:(ee + 1) * 1024]), vv)
                ctx.serw(e, vv)
                if ee == 0:
                    ctx.serm(e.tensor_mul(w01[:, 0:1024], cwork[:, 0:1024],
                                          w01[:, 1280:2304]), vv)
                else:
                    ctx.serm(e.tensor_mul(w01[:, 1280:2304],
                                          cwork[:, ee * 1024:(ee + 1) * 1024],
                                          w01[:, 1280:2304]), vv)
                    ctx.serw(e, vv)
                    ctx.serm(e.tensor_add(w01[:, 0:1024], w01[:, 0:1024],
                                          w01[:, 1280:2304]), vv)
            for ee in range(3):
                ctx.serw(e, vv)
                ctx.serm(e.tensor_copy(w01[:, 1280:1536],
                                       mwork[:, 3072 + ee * 256:3072 + (ee + 1) * 256]), vv)
                ctx.serw(e, vv)
                if ee == 0:
                    ctx.serm(e.tensor_mul(w01[:, 1024:1280], cwork[:, 3072:3328],
                                          w01[:, 1280:1536]), vv)
                else:
                    ctx.serm(e.tensor_mul(w01[:, 1280:1536],
                                          cwork[:, 3072 + ee * 256:3072 + (ee + 1) * 256],
                                          w01[:, 1280:1536]), vv)
                    ctx.serw(e, vv)
                    ctx.serm(e.tensor_add(w01[:, 1024:1280], w01[:, 1024:1280],
                                          w01[:, 1280:1536]), vv)
            for j in range(NI):
                ctx.serw(e, vv)
                ctx.serm(e.tensor_copy(wsum[:, j * 40:j * 40 + 32],
                                       w01[:, j * 32:(j + 1) * 32]), vv)
                ctx.serm(e.tensor_copy(wsum[:, j * 40 + 32:j * 40 + 40],
                                       w01[:, 1024 + j * 8:1024 + (j + 1) * 8]), vv)
        ctx.serm(e.tensor_copy(encb[:, :], encf_tmp[:, :]), vv)
    block.vector(f_vb)
    vb_done = cnt[pfx + "vb"]

    def f_ones(e):
        e.wait_ge(sem[pfx + "vb"], vb_done)
        cnt[pfx + "eo"] += 16
        e.dma_start(out=eT[40:41, :], in_=ones[:, :]).then_inc(sem[pfx + "eo"], 16)
    block.sync(f_ones)

    if build_G:
        def f_gm(e):
            for oc in range(8):
                if oc >= 1:
                    e.wait_ge(sem[pfx + "pg"], oc)
                cnt[pfx + "gm"] += 16
                e.dma_start(out=mtc[:, :],
                            in_=MTc.ap()[oc]).then_inc(sem[pfx + "gm"], 16)
        block.gpsimd(f_gm)

        def f_pg(e):
            e.wait_ge(sem[pfx + "vb"], vb_done)
            for oc in range(8):
                e.wait_ge(sem[pfx + "gm"], 16 * (oc + 1))
                if oc >= 2:
                    e.wait_ge(sem[pfx + "sg"], oc - 1)
                for j in range(NI):
                    mm = e.matmul(psG[:, (oc % 2) * 512:(oc % 2 + 1) * 512],
                                  wsum[:, j * 40:(j + 1) * 40],
                                  mtc[:, j * 512:(j + 1) * 512],
                                  start=(j == 0), stop=(j == NI - 1))
                cnt[pfx + "pg"] += 1
                mm.then_inc(sem[pfx + "pg"], 1)
        block.tensor(f_pg)

        def f_sg(e):
            for oc in range(8):
                e.wait_ge(sem[pfx + "pg"], oc + 1)
                cnt[pfx + "sg"] += 1
                e.activation(G_sb[0:40, oc * 512:(oc + 1) * 512],
                             psG[:, (oc % 2) * 512:(oc % 2 + 1) * 512],
                             AF.Copy, scale=0.5).then_inc(sem[pfx + "sg"], 1)
        block.scalar(f_sg)

        if g_store is not None:
            def f_gst(e):
                e.wait_ge(sem[pfx + "sg"], cnt[pfx + "sg"])
                e.wait_ge(sem[pfx + "gb"], 16)
                cnt[pfx + "gs"] += 16
                e.dma_start(out=g_store[:, :], in_=G_sb[:, :]).then_inc(sem[pfx + "gs"], 16)
            block.sync(f_gst)
    g_groups = cnt[pfx + "pg"]
    g_done = cnt[pfx + "sg"]
    if level <= 1:
        gscope.close()
        def f_l1(e):
            e.wait_ge(sem[pfx + "gs"], cnt[pfx + "gs"])
            e.wait_ge(sem[pfx + "eo"], cnt[pfx + "eo"])
        block.sync(f_l1)
        return [(pfx + "gs", cnt[pfx + "gs"])]

    lns, lnc = emit_ln(ctx, block, pfx + "n", x_in, xnT, ident, lnb,
                       build_xnT=True, ptrans=ptrans, wait_sync=wait_sync,
                       wait_vec=wait_vec)

    if level <= 2:
        gscope.close()
        def f_l2(e):
            e.wait_ge(sem[lns], lnc)
            e.wait_ge(sem[pfx + "gs"], cnt[pfx + "gs"])
        block.sync(f_l2)
        return [(pfx + "gs", cnt[pfx + "gs"])]

    def f_pe_eT(e):
        e.wait_ge(sem[lns], lnc)
        e.wait_ge(sem[pfx + "vb"], vb_done)
        if build_G:
            e.wait_ge(sem[pfx + "sg"], g_done)
        for tc in range(2):
            for j in range(NI):
                mm = e.matmul(psG[:, tc * 512:(tc + 1) * 512],
                              encb[:, j * 40:(j + 1) * 40],
                              xnT[:, j * TPC + tc * 512:j * TPC + (tc + 1) * 512],
                              start=(j == 0), stop=(j == NI - 1))
            cnt[pfx + "pg"] += 1
            mm.then_inc(sem[pfx + "pg"], 1)
    block.tensor(f_pe_eT)

    def f_sg_eT(e):
        for tc in range(2):
            e.wait_ge(sem[pfx + "pg"], g_groups + tc + 1)
            cnt[pfx + "sg"] += 1
            e.copy(eT[0:40, tc * 512:(tc + 1) * 512],
                   psG[:, tc * 512:(tc + 1) * 512]).then_inc(sem[pfx + "sg"], 1)
    block.scalar(f_sg_eT)
    eT_done = cnt[pfx + "sg"]

    if level <= 3:
        gscope.close()
        def f_l3(e):
            e.wait_ge(sem[pfx + "sg"], cnt[pfx + "sg"])
            e.wait_ge(sem[pfx + "gs"], cnt[pfx + "gs"])
        block.sync(f_l3)
        return [(pfx + "gs", cnt[pfx + "gs"])]

    gscope.close()

    def extras(e, ot, tc, psl):
        return e.matmul(psl, G_sb[:, ot * 128:(ot + 1) * 128],
                        eT[:, tc * 512:(tc + 1) * 512],
                        start=False, stop=True)

    bias_ready = 16 if build_G else 0
    fg = fgates or {}
    wt = [(lns, lnc), (pfx + "sg", eT_done), (pfx + "eo", 16), (pfx + "gb", bias_ready)]
    wt += fg.get("tensor", [])
    wg = [(pfx + "pg", g_groups)] if wait_gpsimd0 is None else [wait_gpsimd0]
    wg += fg.get("gpsimd", [])
    wsf = [(pfx + "vb", vb_done)] + fg.get("sync", [])
    return emit_fproj(ctx, block, pfx + "f",
                      w_dram=wbT, nko=NI, kt=NI,
                      rhs_slice=lambda i, tc: xnT[:, i * TPC + tc * 512:i * TPC + (tc + 1) * 512],
                      n_tc=2, extras=extras,
                      wait_tensor=tuple(wt),
                      evac_mode="resadd", ident=ident, ptrans=ptrans,
                      x_in_dram=x_in, x_out_dram=x_out, res_scale=0.5,
                      wait_gpsimd=wg,
                      wait_sync_first=wsf, scoped=fscoped,
                      shared_bufs=fshared, wait_scalar_first=fg.get("scalar", []))


def build_launchA(level=9):
    ctx = new_ctx("lA")
    nc, es = ctx.nc, ctx.es

    x0 = nc.dram_tensor("x0", [TPC, D], F32, kind="ExternalInput")
    wbT = nc.dram_tensor("wbT", [NI, 128, D], F32, kind="ExternalInput")
    bb = nc.dram_tensor("bb", [1, D], F32, kind="ExternalInput")
    encT = nc.dram_tensor("encT", [128, NI * 40], F32, kind="ExternalInput")
    MTc = nc.dram_tensor("MTc", [8, 128, NI * 512], BF16, kind="ExternalInput")
    c0 = nc.dram_tensor("c0", [3, 128, 1024], F32, kind="ExternalInput")
    m0 = nc.dram_tensor("m0", [3, 128, 1024], U8, kind="ExternalInput")
    c1 = nc.dram_tensor("c1", [3, 128, 256], F32, kind="ExternalInput")
    m1 = nc.dram_tensor("m1", [3, 128, 256], U8, kind="ExternalInput")
    identD = nc.dram_tensor("identD", [128, 128], BF16, kind="ExternalInput")

    x1 = nc.dram_tensor("x1", [TPC, D], F32, kind="ExternalOutput")
    xn2 = nc.dram_tensor("xn2", [TPC, D], BF16, kind="ExternalOutput")
    g_out = nc.dram_tensor("g_out", [41, D], BF16, kind="ExternalOutput")

    with ctx.es:
        block = es.enter_context(nc.Block())
        ident = ctx.sb("ident", [128, 128], BF16)
        ptrans = ctx.ps("ptrans", [128, 2048], BF16)
        G_sb = ctx.sb("G_sb", [41, D], BF16)
        lnb = LnBufs(ctx)
        ctx.new_sem("ii")

        def f_ident(e):
            ctx.cnt["ii"] += 16
            e.dma_start(out=ident[:, :], in_=identD[:, :]).then_inc(ctx.sem["ii"], 16)
        block.sync(f_ident)

        def f_identw(e):
            e.wait_ge(ctx.sem["ii"], 16)
        block.tensor(f_identw)

        fin_list = emit_l0ffn(ctx, block, "a", x_in=x0, x_out=x1, wbT=wbT, MTc=MTc,
                              c0=c0, m0=m0, c1=c1, m1=m1, encT_d=encT, bb_d=bb,
                              ident=ident, ptrans=ptrans, lnb=lnb,
                              build_G=True, G_sb=G_sb, g_store=g_out, level=level)

        if level >= 5:
            emit_ln(ctx, block, "z", x1, None, ident, lnb, build_xnT=False,
                    xn_bf_store=xn2, wait_sync=fin_list)

        def f_fin(e):
            if level >= 5:
                e.wait_ge(ctx.sem["zo0"], ctx.cnt["zo0"])
                e.wait_ge(ctx.sem["zo1"], ctx.cnt["zo1"])
            for s_, c_ in fin_list:
                e.wait_ge(ctx.sem[s_], c_)
            e.wait_ge(ctx.sem["ags"], ctx.cnt["ags"])
        block.sync(f_fin)

    return ctx.nc




def emit_l1ffn(ctx, block, pfx, *, x_in, x_out, l1wT, wqkhD, BmD, bbD,
               ident, ptrans, lnb, xnT_shared, psX, wait_sync=None,
               wait_vec=None, fshared=None, fgates=None):
    nc = ctx.nc
    sem, cnt = ctx.sem, ctx.cnt
    xnT = xnT_shared
    wqkh = ctx.sb(pfx + "_wqkh", [128, NI * 640], BF16)
    Bm_sb = ctx.sb(pfx + "_Bm", [128, D], BF16)
    bbP = ctx.sb(pfx + "_bbP", [128, NI], F32)
    qkh_f = ctx.sb(pfx + "_qkh", [128, 640], F32)
    s8 = ctx.sb(pfx + "_s8", [128, 16], F32)
    w8 = ctx.sb(pfx + "_w8", [128, 8], F32)
    wh_f = ctx.sb(pfx + "_whf", [128, 128], F32)
    wh_b = ctx.sb(pfx + "_whb", [128, 128], BF16)
    whT = ctx.sb(pfx + "_whT", [128, NT * 128], BF16)
    psQ = psX[0:128, 0:640]
    fg = fgates or {}
    for s_ in ("ld", "qp", "qs", "qv"):
        ctx.new_sem(pfx + s_)

    def f_ld(e):
        for ws_, wc_ in fg.get("l1ld", []):
            e.wait_ge(sem[ws_], wc_)
        for (o, i_) in ((wqkh[:, :], wqkhD[:, :]), (Bm_sb[:, :], BmD[:, :]),
                        (bbP[:, :], bbD[:, :])):
            cnt[pfx + "ld"] += 16
            e.dma_start(out=o, in_=i_).then_inc(sem[pfx + "ld"], 16)
    block.gpsimd(f_ld)

    lns, lnc = emit_ln(ctx, block, pfx + "n", x_in, xnT, ident, lnb,
                       build_xnT=True, ptrans=ptrans, wait_sync=wait_sync,
                       wait_vec=wait_vec, wait_pe=(fgates or {}).get("lnpe"))

    bqp, bqs, bqv = cnt[pfx + "qp"], cnt[pfx + "qs"], cnt[pfx + "qv"]
    grp = [bqp + 2 * t + 1 for t in range(NT)]
    trq = [bqp + 2 * t + 2 for t in range(NT)]
    ev = [bqs + 3 * t + 1 for t in range(NT)]
    ex = [bqs + 3 * t + 2 for t in range(NT)]
    wc = [bqs + 3 * t + 3 for t in range(NT)]
    vneg = [bqv + 7 * t + 4 for t in range(NT)]
    vwh = [bqv + 7 * t + 7 for t in range(NT)]

    def f_pe(e):
        e.wait_ge(sem[lns], lnc)
        e.wait_ge(sem[pfx + "ld"], 16)
        for ws_, wc_ in fg.get("l1pe", []):
            e.wait_ge(sem[ws_], wc_)
        for tt in range(NT):
            if tt >= 1:
                e.wait_ge(sem[pfx + "qs"], ev[tt - 1])
            for i in range(NI):
                e.matmul(psQ[:, 0:512],
                         xnT[:, i * TPC + tt * 128:i * TPC + (tt + 1) * 128],
                         wqkh[:, i * 640:i * 640 + 512],
                         start=(i == 0), stop=(i == NI - 1))
                mm = e.matmul(psQ[:, 512:640],
                              xnT[:, i * TPC + tt * 128:i * TPC + (tt + 1) * 128],
                              wqkh[:, i * 640 + 512:(i + 1) * 640],
                              start=(i == 0), stop=(i == NI - 1))
            cnt[pfx + "qp"] += 1
            mm.then_inc(sem[pfx + "qp"], 1)
            e.wait_ge(sem[pfx + "qv"], vwh[tt])
            if tt >= 2:
                e.wait_ge(sem[pfx + "qs"], wc[tt - 2])
            cnt[pfx + "qp"] += 1
            e.transpose(ptrans[:, (tt % 2) * 1024:(tt % 2) * 1024 + 128],
                        wh_b[:, :], ident[:, :]).then_inc(sem[pfx + "qp"], 1)
    block.tensor(f_pe)

    def f_sc(e):
        for tt in range(NT):
            e.wait_ge(sem[pfx + "qp"], grp[tt])
            if tt >= 1:
                e.wait_ge(sem[pfx + "qv"], vwh[tt - 1])
            cnt[pfx + "qs"] += 1
            e.copy(qkh_f[:, :], psQ[:, :]).then_inc(sem[pfx + "qs"], 1)
            e.wait_ge(sem[pfx + "qv"], vneg[tt])
            cnt[pfx + "qs"] += 1
            e.activation(w8[:, :], s8[:, 0:8], AF.Exp, bias=s8[:, 8:9],
                         accum_out=s8[:, 9:10]).then_inc(sem[pfx + "qs"], 1)
            e.wait_ge(sem[pfx + "qp"], trq[tt])
            cnt[pfx + "qs"] += 1
            e.copy(whT[:, tt * 128:(tt + 1) * 128],
                   ptrans[:, (tt % 2) * 1024:(tt % 2) * 1024 + 128]
                   ).then_inc(sem[pfx + "qs"], 1)
    block.scalar(f_sc)

    ISQ32 = 1.0 / math.sqrt(32.0)

    def f_ve(e):
        for tt in range(NT):
            e.wait_ge(sem[pfx + "qs"], ev[tt])
            ctx.serm(e.tensor_mul(qkh_f[:, 0:256], qkh_f[:, 0:256], qkh_f[:, 256:512]),
                     pfx + "qv")
            ctx.serw(e, pfx + "qv")
            ctx.serm(e.reduce_sum(s8[:, 0:8],
                                  qkh_f[:, 0:256].rearrange("p (e r) -> p e r", e=8),
                                  axis=AX.X), pfx + "qv")
            ctx.serw(e, pfx + "qv")
            ctx.serm(e.tensor_scalar(out=s8[:, 0:8], in0=s8[:, 0:8], scalar1=ISQ32,
                                     scalar2=None, op0=ALU.mult), pfx + "qv")
            ctx.serw(e, pfx + "qv")
            ctx.serm(e.reduce_max(s8[:, 8:9], s8[:, 0:8], axis=AX.X, negate=True),
                     pfx + "qv")
            e.wait_ge(sem[pfx + "qs"], ex[tt])
            ctx.serm(e.reciprocal(s8[:, 10:11], s8[:, 9:10]), pfx + "qv")
            ctx.serw(e, pfx + "qv")
            ctx.serm(e.tensor_mul(wh_f.ap().rearrange("p (e r) -> p e r", e=8),
                                  qkh_f[:, 512:640].rearrange("p (e r) -> p e r", e=8),
                                  w8.ap().unsqueeze(2).broadcast_to([128, 8, 16])),
                     pfx + "qv")
            ctx.serw(e, pfx + "qv")
            ctx.serm(e.tensor_scalar(out=wh_b[:, :], in0=wh_f[:, :],
                                     scalar1=s8[:, 10:11], scalar2=None,
                                     op0=ALU.mult), pfx + "qv")
            assert cnt[pfx + "qv"] == vwh[tt]
    block.vector(f_ve)

    def extras(e, ot, tc, psl):
        return e.matmul(psl, Bm_sb[:, ot * 128:(ot + 1) * 128],
                        whT[:, tc * 512:(tc + 1) * 512], start=False, stop=True)

    wt = [(lns, lnc), (pfx + "qs", wc[NT - 1]), (pfx + "ld", 48)]
    wt += fg.get("tensor", [])
    return emit_fproj(ctx, block, pfx + "f",
                      w_dram=l1wT, nko=NI, kt=NI,
                      rhs_slice=lambda i, tc: xnT[:, i * TPC + tc * 512:i * TPC + (tc + 1) * 512],
                      n_tc=2, extras=extras, wait_tensor=tuple(wt),
                      evac_mode="resadd", ident=ident, ptrans=ptrans,
                      x_in_dram=x_in, x_out_dram=x_out, res_scale=0.5,
                      wait_gpsimd=fg.get("gpsimd", []),
                      wait_sync_first=fg.get("sync", []),
                      wait_scalar_first=list(fg.get("scalar", []))
                      + [(pfx + "ld", 48)],
                      evac_bias=lambda ot: bbP[:, ot:ot + 1],
                      shared_bufs=fshared)


def build_launchC():
    ctx = new_ctx("lC")
    nc, es = ctx.nc, ctx.es
    sem, cnt = ctx.sem, ctx.cnt

    x1 = nc.dram_tensor("x1", [TPC, D], F32, kind="ExternalInput")
    attnT = nc.dram_tensor("attnT", [128, NI * TPC], BF16, kind="ExternalInput")
    woT = nc.dram_tensor("woT", [NI, 128, D], F32, kind="ExternalInput")
    g_in = nc.dram_tensor("g_in", [41, D], BF16, kind="ExternalInput")
    wbT = nc.dram_tensor("wbT", [NI, 128, D], F32, kind="ExternalInput")
    encT = nc.dram_tensor("encT", [128, NI * 40], F32, kind="ExternalInput")
    l1wT = nc.dram_tensor("l1wT", [NI, 128, D], F32, kind="ExternalInput")
    wqkhD = nc.dram_tensor("wqkhD", [128, NI * 640], F32, kind="ExternalInput")
    BmD = nc.dram_tensor("BmD", [128, D], F32, kind="ExternalInput")
    bbD = nc.dram_tensor("bbD", [128, NI], F32, kind="ExternalInput")
    identD = nc.dram_tensor("identD", [128, 128], BF16, kind="ExternalInput")

    x4 = nc.dram_tensor("x4", [TPC, D], F32, kind="ExternalOutput")
    xn5 = nc.dram_tensor("xn5", [TPC, D], BF16, kind="ExternalOutput")
    x2 = nc.dram_tensor("x2", [TPC, D], F32)
    x3 = nc.dram_tensor("x3", [TPC, D], F32)

    with ctx.es:
        block = es.enter_context(nc.Block())
        ident = ctx.sb("ident", [128, 128], BF16)
        ptrans = ctx.ps("ptrans", [128, 2048], BF16)
        G_sb = ctx.sb("G_sb", [41, D], BF16)
        rhs_sb = ctx.sb("rhs_sb", [128, NI * TPC], BF16)
        lnb = LnBufs(ctx, xa2=False)
        psX = ctx.ps("psX", [128, 1024], F32)
        fw = ctx.sb("fw", [128, 2 * D], BF16)
        fps = ctx.ps("fps", [128, 2 * 1024], F32)
        fobf = ctx.sb("fobf", [128, 2 * 1024], BF16)
        fxsl = ctx.sb("fxsl", [128, 3 * 8 * 128], F32)
        fshared = (fw, fps, fobf, fxsl)
        for s_ in ("ii", "ax"):
            ctx.new_sem(s_)

        def f_ii(e):
            cnt["ii"] += 16
            e.dma_start(out=ident[:, :], in_=identD[:, :]).then_inc(sem["ii"], 16)
            cnt["ax"] += 16
            e.dma_start(out=rhs_sb[:, :], in_=attnT[:, :]).then_inc(sem["ax"], 16)
        block.sync(f_ii)

        def f_iiw(e):
            e.wait_ge(sem["ii"], 16)
        block.tensor(f_iiw)

        st1 = emit_fproj(ctx, block, "o",
                         w_dram=woT, nko=NI, kt=NI,
                         rhs_slice=lambda i, tc: rhs_sb[:, i * TPC + tc * 512:
                                                        i * TPC + (tc + 1) * 512],
                         n_tc=2, extras=None,
                         wait_tensor=(("ax", 16),),
                         evac_mode="resadd", ident=ident, ptrans=ptrans,
                         x_in_dram=x1, x_out_dram=x2, res_scale=1.0,
                         shared_bufs=fshared)

        st2 = emit_l0ffn(ctx, block, "c", x_in=x2, x_out=x3, wbT=wbT, MTc=None,
                         c0=None, m0=None, c1=None, m1=None, encT_d=encT,
                         bb_d=None, ident=ident, ptrans=ptrans, lnb=lnb,
                         build_G=False, G_sb=G_sb, g_dram_in=g_in,
                         wait_sync=st1, xnT_shared=rhs_sb, psX=psX,
                         wait_vec=[("op", NI)],
                         fshared=fshared,
                         fgates={"tensor": [("oc", NI)],
                                 "gpsimd": [("op", NI)],
                                 "sync": st1,
                                 "scalar": [("ot", NI * NT)],
                                 "lnpe": [("ov", NI * NT)]})

        st3 = emit_l1ffn(ctx, block, "d", x_in=x3, x_out=x4, l1wT=l1wT,
                         wqkhD=wqkhD, BmD=BmD, bbD=bbD, ident=ident,
                         ptrans=ptrans, lnb=lnb, xnT_shared=rhs_sb, psX=psX,
                         wait_sync=st2, wait_vec=[("cfp", NI)],
                         fshared=fshared,
                         fgates={"tensor": [("cfc", NI)],
                                 "gpsimd": [("cfp", NI)],
                                 "sync": st2,
                                 "scalar": [("cft", NI * NT)],
                                 "lnpe": [("cfv", NI * NT)]})

        emit_ln(ctx, block, "z", x4, None, ident, lnb, build_xnT=False,
                xn_bf_store=xn5, wait_sync=st3)

        def f_fin(e):
            e.wait_ge(sem["zo0"], cnt["zo0"])
            e.wait_ge(sem["zo1"], cnt["zo1"])
            for s_, c_ in st3:
                e.wait_ge(sem[s_], c_)
        block.sync(f_fin)
    return nc

def build_launchB():
    ctx = new_ctx("lB")
    nc, es = ctx.nc, ctx.es
    sem, cnt = ctx.sem, ctx.cnt

    xnT_g = nc.dram_tensor("xnT_g", [4, 128, NI * 1024], BF16, kind="ExternalInput")
    wqkv = nc.dram_tensor("wqkv", [24, 128, D], F32, kind="ExternalInput")
    cosD = nc.dram_tensor("cosD", [128, S], F32, kind="ExternalInput")
    sinD = nc.dram_tensor("sinD", [128, S], F32, kind="ExternalInput")
    identD = nc.dram_tensor("identD", [128, 128], BF16, kind="ExternalInput")
    triD = nc.dram_tensor("triD", [128, 2048], BF16, kind="ExternalInput")
    OT = nc.dram_tensor("OT", [1024, S], BF16, kind="ExternalOutput")
    qkvT = nc.dram_tensor("qkvT", [3072, S], BF16)

    NQ, TQ = 4, 1024

    with ctx.es:
        block = es.enter_context(nc.Block())
        ident = ctx.sb("ident", [128, 128], BF16)
        cosb = ctx.sb("cosb", [128, S], F32)
        sinb = ctx.sb("sinb", [128, S], F32)
        triq4 = ctx.sb("triq4", [128, 4 * 512], BF16)
        ones128 = ctx.sb("ones128", [128, 1], BF16)
        onesF = ctx.sb("onesF", [128, 128], BF16)
        ones1 = ctx.sb("ones1", [1, 128], BF16)
        for s_ in ("ii", "vi"):
            ctx.new_sem(s_)

        def f_consts(e):
            for o, i_ in ((ident, identD), (cosb, cosD), (sinb, sinD),
                          (triq4, triD)):
                cnt["ii"] += 16
                e.dma_start(out=o[:, :], in_=i_[:, :]).then_inc(sem["ii"], 16)
        block.sync(f_consts)

        def f_vinit(e):
            ctx.serm(e.memset(ones128[:, :], 1.0), "vi")
            ctx.serm(e.memset(ones1[:, :], 1.0), "vi")
            ctx.serm(e.memset(onesF[:, :], 1.0), "vi")
        block.vector(f_vinit)
        vi_done = cnt["vi"]

        pscope = ExitStack()
        xnq = pscope.enter_context(nc.sbuf_tensor("xnq", [128, NI * TQ], BF16))
        w_sb = pscope.enter_context(nc.sbuf_tensor("pw", [128, 2 * D], BF16))
        obf = pscope.enter_context(nc.sbuf_tensor("pobf", [128, 2 * TQ], BF16))
        psp = pscope.enter_context(nc.psum_tensor("psp", [128, 2 * TQ], F32))
        for s_ in ("qx", "qw0", "qw1", "qp", "qc", "qo0", "qo1"):
            ctx.new_sem(s_)
        NG = NQ * 24



        def f_qw(e):
            for g in range(NG):
                ot = g % 24
                if g >= 2:
                    e.wait_ge(sem["qc"], g - 1)
                swp = "qw0" if g % 2 == 0 else "qw1"
                cnt[swp] += 16
                e.dma_start(out=w_sb[:, (g % 2) * D:(g % 2 + 1) * D],
                            in_=wqkv.ap()[ot]).then_inc(sem[swp], 16)
        block.gpsimd(f_qw)

        def f_qt(e):
            e.wait_ge(sem["ii"], 64)
            for g in range(NG):
                tq = g // 24
                e.wait_ge(sem["qx"], 16 * (tq + 1))
                e.wait_ge(sem["qw0" if g % 2 == 0 else "qw1"], 16 * (g // 2 + 1))
                if g >= 2:
                    e.wait_ge(sem["qc"], g - 1)
                for tc in range(2):
                    for i in range(NI):
                        mm = e.matmul(psp[:, (g % 2) * TQ + tc * 512:
                                          (g % 2) * TQ + (tc + 1) * 512],
                                      w_sb[:, (g % 2) * D + i * 128:
                                           (g % 2) * D + (i + 1) * 128],
                                      xnq[:, i * TQ + tc * 512:i * TQ + (tc + 1) * 512],
                                      start=(i == 0), stop=(i == NI - 1))
                cnt["qp"] += 1
                mm.then_inc(sem["qp"], 1)
        block.tensor(f_qt)

        def f_qc(e):
            for g in range(NG):
                e.wait_ge(sem["qp"], g + 1)
                if g >= 2:
                    e.wait_ge(sem["qo0" if g % 2 == 0 else "qo1"],
                              16 * ((g - 2) // 2 + 1))
                cnt["qc"] += 1
                e.copy(obf[:, (g % 2) * TQ:(g % 2 + 1) * TQ],
                       psp[:, (g % 2) * TQ:(g % 2 + 1) * TQ]).then_inc(sem["qc"], 1)
        block.scalar(f_qc)

        def _qstore(e, g):
            tq, ot = g // 24, g % 24
            e.wait_ge(sem["qc"], g + 1)
            sop = "qo0" if g % 2 == 0 else "qo1"
            cnt[sop] += 16
            e.dma_start(out=qkvT[ot * 128:(ot + 1) * 128, tq * TQ:(tq + 1) * TQ],
                        in_=obf[:, (g % 2) * TQ:(g % 2 + 1) * TQ]
                        ).then_inc(sem[sop], 16)

        def f_qxs(e):
            for g in range(NG):
                if g % 24 == 0:
                    tq = g // 24
                    if tq >= 1:
                        e.wait_ge(sem["qp"], tq * 24)
                    cnt["qx"] += 16
                    e.dma_start(out=xnq[:, :], in_=xnT_g.ap()[tq]
                                ).then_inc(sem["qx"], 16)
                if g >= 1:
                    _qstore(e, g - 1)
            _qstore(e, NG - 1)
        block.sync(f_qxs)
        proj_done = [("qo0", cnt["qo0"]), ("qo1", cnt["qo1"])]
        pscope.close()

        vT_all = ctx.sb("vT_all", [128, 8 * S], BF16)
        vscope = ExitStack()
        vload = vscope.enter_context(nc.sbuf_tensor("vload", [128, 2 * S], BF16))
        ptr2 = vscope.enter_context(nc.psum_tensor("ptr2", [128, 2048], BF16))
        for s_ in ("vx0", "vx1", "vt", "vc"):
            ctx.new_sem(s_)

        def f_vx(e):
            for s_, c_ in proj_done:
                e.wait_ge(sem[s_], c_)
            for hh in range(HPC):
                if hh >= 2:
                    e.wait_ge(sem["vt"], 32 * (hh - 1))
                sx = "vx0" if hh % 2 == 0 else "vx1"
                cnt[sx] += 16
                e.dma_start(out=vload[:, (hh % 2) * S:(hh % 2 + 1) * S],
                            in_=qkvT[(16 + hh) * 128:(17 + hh) * 128, :]
                            ).then_inc(sem[sx], 16)
        block.sync(f_vx)

        def f_vt(e):
            e.wait_ge(sem["ii"], 64)
            for k in range(32 * HPC):
                hh, kt = k // 32, k % 32
                if kt == 0:
                    e.wait_ge(sem["vx0" if hh % 2 == 0 else "vx1"],
                              16 * (hh // 2 + 1))
                if k >= 2:
                    e.wait_ge(sem["vc"], k - 1)
                cnt["vt"] += 1
                e.transpose(ptr2[:, (k % 2) * 1024:(k % 2) * 1024 + 128],
                            vload[:, (hh % 2) * S + kt * 128:
                                  (hh % 2) * S + (kt + 1) * 128],
                            ident[:, :]).then_inc(sem["vt"], 1)
        block.tensor(f_vt)

        def f_vc(e):
            for k in range(32 * HPC):
                e.wait_ge(sem["vt"], k + 1)
                cnt["vc"] += 1
                e.copy(vT_all[:, k * 128:(k + 1) * 128],
                       ptr2[:, (k % 2) * 1024:(k % 2) * 1024 + 128]
                       ).then_inc(sem["vc"], 1)
        block.scalar(f_vc)
        vc_total = cnt["vc"]
        vt_total = cnt["vt"]
        vscope.close()

        bufs = dict(
            qraw=ctx.sb("qraw", [128, S], BF16),
            kraw=ctx.sb("kraw", [128, S], BF16),
            qr=ctx.sb("qr", [128, S], BF16),
            kr=ctx.sb("kr", [128, S], BF16),
            qsh=ctx.sb("qsh", [128, S], BF16),
            ksh=ctx.sb("ksh", [128, S], BF16),
            P_sb=ctx.sb("P_sb", [128, S], BF16),
            o_sb=ctx.sb("o_sb", [128, 2 * 512], BF16),
            rb_f=ctx.sb("rb_f", [128, 2 * 512], F32),
            rb_t=ctx.sb("rb_t", [128, 512], F32),
            triq4=triq4,
            nrm=ctx.sb("nrm", [1, 24], F32),
            nbias=ctx.sb("nbias", [128, 1], F32),
            nbb=ctx.sb("nbb", [1, 1], BF16),
            psS=ctx.ps("psS", [128, 4 * 512], F32),
            psO=ctx.ps("psO", [128, 2 * 512], F32),
            psNB=ctx.ps("psNB", [128, 2 * 512], F32),
        )
        for s_ in ("hx", "hv", "hs", "hp", "ho0", "ho1"):
            ctx.new_sem(s_)

        for h in range(HPC):
            _emit_B_head(ctx, block, h, qkvT, OT, bufs, cosb, sinb, ident,
                         ones128, onesF, ones1, vT_all,
                         (proj_done + [("vt", vt_total)]) if h == 0 else None,
                         vi_done, vc_total)

        def f_fin(e):
            e.wait_ge(sem["ho0"], cnt["ho0"])
            e.wait_ge(sem["ho1"], cnt["ho1"])
        block.sync(f_fin)
    return nc


def _emit_B_head(ctx, block, h, qkvT, OT, B, cosb, sinb, ident,
                 ones128, onesF, ones1, vT, first_wait, vi_done, vc_total):
    sem, cnt = ctx.sem, ctx.cnt
    bhp, bhs, bhv, bhx = cnt["hp"], cnt["hs"], cnt["hv"], cnt["hx"]
    bho0, bho1 = cnt["ho0"], cnt["ho1"]

    tiles = []
    for qb in range(8):
        for t in range(4 * (qb + 1)):
            tiles.append((qb, t))
    NTILE = len(tiles)

    hp = hs = hv = 0
    pos = {}
    hv += 3; pos["ropeq"] = bhv + hv
    hv += 3; pos["ropek"] = bhv + hv
    hv += 1; pos["qsq"] = bhv + hv
    bm = [0] * 16; bred = [0] * 16
    for b in range(8):
        hp += 1; bm[b] = bhp + hp
        hv += 1; bred[b] = bhv + hv
    hv += 1; pos["qmax"] = bhv + hv
    hv += 1; pos["ksq"] = bhv + hv
    for b in range(8, 16):
        hp += 1; bm[b] = bhp + hp
        hv += 1; bred[b] = bhv + hv
    hv += 1; pos["kmax"] = bhv + hv
    hv += 1; pos["prod"] = bhv + hv
    hs += 1; pos["sqb"] = bhs + hs
    hv += 1; pos["neg"] = bhv + hv
    hv += 1; pos["nbbc"] = bhv + hv
    hp += 1; pos["bcast"] = bhp + hp
    hs += 1; pos["cbias"] = bhs + hs
    smm = [0] * NTILE; summ = [0] * NTILE; avmm = [0] * NTILE
    pe_order = []
    for g in range(min(3, NTILE)):
        pe_order.append(("S", g))
    for g in range(NTILE):
        if g + 3 < NTILE:
            pe_order.append(("S", g + 3))
        pe_order.append(("sum", g))
        pe_order.append(("av", g))
    for kind, g in pe_order:
        hp += 1
        if kind == "S":
            smm[g] = bhp + hp
        elif kind == "sum":
            summ[g] = bhp + hp
        else:
            avmm[g] = bhp + hp
    expp = [0] * NTILE; rcp = [0] * 8; onor = [0] * 8; odiv = [0] * 8
    for g, (qb, t) in enumerate(tiles):
        hs += 1; expp[g] = bhs + hs
        if t == 4 * (qb + 1) - 1:
            hs += 2; rcp[qb] = bhs + hs
            hs += 1; onor[qb] = bhs + hs
    for qb in range(8):
        hv += 1; odiv[qb] = bhv + hv

    def _loads(e, hh):
        for r, t in ((hh, "qraw"), (8 + hh, "kraw")):
            cnt["hx"] += 16
            e.dma_start(out=B[t][:, :], in_=qkvT[r * 128:(r + 1) * 128, :]
                        ).then_inc(sem["hx"], 16)
        e.wait_ge(sem["hx"], cnt["hx"])
        for (dst, srcb) in ((B["qsh"], B["qraw"]), (B["ksh"], B["kraw"])):
            cnt["hx"] += 16
            e.dma_start(out=dst[0:64, :], in_=srcb[64:128, :]).then_inc(sem["hx"], 16)
            cnt["hx"] += 16
            e.dma_start(out=dst[64:128, :], in_=srcb[0:64, :]).then_inc(sem["hx"], 16)

    def f_hx(e):
        if first_wait:
            for s_, c_ in first_wait:
                e.wait_ge(sem[s_], c_)
        _loads(e, h)
    if h == 0:
        block.sync(f_hx)
    loads_done = cnt["hx"]

    def f_hv(e):
        e.wait_ge(sem["hx"], loads_done)
        if h == 0:
            e.wait_ge(sem["vi"], vi_done)
            e.wait_ge(sem["ii"], 64)
        for (dst, srcb, shb) in ((B["qr"], B["qraw"], B["qsh"]),
                                 (B["kr"], B["kraw"], B["ksh"])):
            ctx.serm(e.tensor_mul(dst[:, :], srcb[:, :], cosb[:, :]), "hv")
            ctx.serm(e.tensor_mul(B["P_sb"][:, 0:S], shb[:, :], sinb[:, :]), "hv")
            ctx.serw(e, "hv")
            ctx.serm(e.tensor_add(dst[:, :], dst[:, :], B["P_sb"][:, 0:S]), "hv")
            ctx.serw(e, "hv")
        ctx.serm(e.tensor_mul(B["P_sb"][:, 0:S], B["qr"][:, :], B["qr"][:, :]), "hv")
        for b in range(16):
            e.wait_ge(sem["hp"], bm[b])
            cnt["hv"] += 1
            e.reduce_max(B["nrm"][0:1, b:b + 1],
                         B["psO"][0:1, (b % 2) * 512:(b % 2) * 512 + 512],
                         axis=AX.X).then_inc(sem["hv"], 1)
            if b == 7:
                ctx.serw(e, "hv")
                ctx.serm(e.reduce_max(B["nrm"][0:1, 16:17], B["nrm"][0:1, 0:8],
                                      axis=AX.X), "hv")
                ctx.serw(e, "hv")
                ctx.serm(e.tensor_mul(B["P_sb"][:, 0:S], B["kr"][:, :],
                                      B["kr"][:, :]), "hv")
        ctx.serw(e, "hv")
        ctx.serm(e.reduce_max(B["nrm"][0:1, 17:18], B["nrm"][0:1, 8:16],
                              axis=AX.X), "hv")
        ctx.serw(e, "hv")
        ctx.serm(e.tensor_mul(B["nrm"][0:1, 18:19], B["nrm"][0:1, 16:17],
                              B["nrm"][0:1, 17:18]), "hv")
        e.wait_ge(sem["hs"], pos["sqb"])
        ctx.serm(e.tensor_scalar(out=B["nrm"][0:1, 20:21], in0=B["nrm"][0:1, 19:20],
                                 scalar1=-1.0, scalar2=None, op0=ALU.mult), "hv")
        ctx.serw(e, "hv")
        ctx.serm(e.tensor_copy(B["nbb"][0:1, 0:1], B["nrm"][0:1, 20:21]), "hv")
        assert cnt["hv"] == pos["nbbc"]
        for qb in range(8):
            e.wait_ge(sem["hs"], onor[qb])
            e.wait_ge(sem["hs"], rcp[qb])
            cnt["hv"] += 1
            e.tensor_mul(B["o_sb"][:, (qb % 2) * 512:(qb % 2 + 1) * 512],
                         B["o_sb"][:, (qb % 2) * 512:(qb % 2 + 1) * 512],
                         B["rb_f"][:, (qb % 2) * 512:(qb % 2 + 1) * 512]
                         ).then_inc(sem["hv"], 1)
            assert cnt["hv"] == odiv[qb]
    block.vector(f_hv)

    def f_ht(e):
        if h == 0:
            e.wait_ge(sem["vc"], vc_total)
        else:
            e.wait_ge(sem["hs"], bhs)
        e.wait_ge(sem["hv"], pos["qsq"])
        for b in range(16):
            if b == 8:
                e.wait_ge(sem["hv"], pos["ksq"])
            if b >= 2:
                e.wait_ge(sem["hv"], bred[b - 2])
            cnt["hp"] += 1
            e.matmul(B["psO"][0:1, (b % 2) * 512:(b % 2) * 512 + 512],
                     ones128[:, 0:1],
                     B["P_sb"][:, (b % 8) * 512:(b % 8 + 1) * 512],
                     start=True, stop=True).then_inc(sem["hp"], 1)
            assert cnt["hp"] == bm[b]
        e.wait_ge(sem["hv"], pos["nbbc"])
        cnt["hp"] += 1
        e.matmul(B["psNB"][:, 0:1], ones1[0:1, :], B["nbb"][0:1, 0:1],
                 start=True, stop=True).then_inc(sem["hp"], 1)
        assert cnt["hp"] == pos["bcast"]

        def _emit_S(g):
            qb, t = tiles[g]
            if g >= 4:
                e.wait_ge(sem["hs"], expp[g - 4])
            masked = (t >= 4 * qb)
            mm = e.matmul(B["psS"][:, (g % 4) * 512:(g % 4 + 1) * 512],
                          B["kr"][:, t * 128:(t + 1) * 128],
                          B["qr"][:, qb * 512:(qb + 1) * 512],
                          start=True, stop=not masked)
            if masked:
                s_ = t - 4 * qb
                mm = e.matmul(B["psS"][:, (g % 4) * 512:(g % 4 + 1) * 512],
                              ident[:, :], B["triq4"][:, s_ * 512:(s_ + 1) * 512],
                              start=False, stop=True)
            cnt["hp"] += 1
            mm.then_inc(sem["hp"], 1)
            assert cnt["hp"] == smm[g]

        for g in range(min(3, NTILE)):
            _emit_S(g)
        for g in range(NTILE):
            qb, t = tiles[g]
            nkt = 4 * (qb + 1)
            if g + 3 < NTILE:
                _emit_S(g + 3)
            e.wait_ge(sem["hs"], expp[g])
            if t == 0:
                if qb == 0:
                    e.wait_ge(sem["hs"], pos["cbias"])
                if qb >= 2:
                    e.wait_ge(sem["hs"], rcp[qb - 2])
            cnt["hp"] += 1
            e.matmul(B["psNB"][:, (qb % 2) * 512:(qb % 2 + 1) * 512],
                     onesF[:, :],
                     B["P_sb"][:, (g % 4) * 512:(g % 4 + 1) * 512],
                     start=(t == 0), stop=(t == nkt - 1)).then_inc(sem["hp"], 1)
            assert cnt["hp"] == summ[g]
            if t == 0 and qb >= 2:
                e.wait_ge(sem["hs"], onor[qb - 2])
            cnt["hp"] += 1
            e.matmul(B["psO"][:, (qb % 2) * 512:(qb % 2 + 1) * 512],
                     vT[:, h * S + t * 128:h * S + (t + 1) * 128],
                     B["P_sb"][:, (g % 4) * 512:(g % 4 + 1) * 512],
                     start=(t == 0), stop=(t == nkt - 1)).then_inc(sem["hp"], 1)
            assert cnt["hp"] == avmm[g]
    block.tensor(f_ht)

    def f_hs(e):
        e.wait_ge(sem["hv"], pos["prod"])
        cnt["hs"] += 1
        e.activation(B["nrm"][0:1, 19:20], B["nrm"][0:1, 18:19], AF.Sqrt,
                     scale=1.0 / 128.0).then_inc(sem["hs"], 1)
        assert cnt["hs"] == pos["sqb"]
        e.wait_ge(sem["hp"], pos["bcast"])
        cnt["hs"] += 1
        e.copy(B["nbias"][:, :], B["psNB"][:, 0:1]).then_inc(sem["hs"], 1)
        assert cnt["hs"] == pos["cbias"]
        for g, (qb, t) in enumerate(tiles):
            e.wait_ge(sem["hp"], smm[g])
            if g >= 4:
                e.wait_ge(sem["hp"], avmm[g - 4])
            cnt["hs"] += 1
            e.activation(B["P_sb"][:, (g % 4) * 512:(g % 4 + 1) * 512],
                         B["psS"][:, (g % 4) * 512:(g % 4 + 1) * 512],
                         AF.Exp, bias=B["nbias"][:, 0:1], scale=ISQ
                         ).then_inc(sem["hs"], 1)
            assert cnt["hs"] == expp[g]
            if t == 4 * (qb + 1) - 1:
                e.wait_ge(sem["hp"], summ[g])
                if qb >= 2:
                    e.wait_ge(sem["hv"], odiv[qb - 2])
                cnt["hs"] += 1
                e.activation(B["rb_t"][:, :],
                             B["psNB"][:, (qb % 2) * 512:(qb % 2 + 1) * 512],
                             AF.Ln).then_inc(sem["hs"], 1)
                cnt["hs"] += 1
                e.activation(B["rb_f"][:, (qb % 2) * 512:(qb % 2 + 1) * 512],
                             B["rb_t"][:, :], AF.Exp,
                             scale=-1.0).then_inc(sem["hs"], 1)
                assert cnt["hs"] == rcp[qb]
                e.wait_ge(sem["hp"], avmm[g])
                if qb >= 2:
                    hosem = "ho0" if qb % 2 == 0 else "ho1"
                    hob = bho0 if qb % 2 == 0 else bho1
                    e.wait_ge(sem[hosem], hob + 16 * ((qb - 2) // 2 + 1))
                elif h > 0:
                    e.wait_ge(sem["ho0" if qb % 2 == 0 else "ho1"],
                              bho0 if qb % 2 == 0 else bho1)
                cnt["hs"] += 1
                e.copy(B["o_sb"][:, (qb % 2) * 512:(qb % 2 + 1) * 512],
                       B["psO"][:, (qb % 2) * 512:(qb % 2 + 1) * 512]
                       ).then_inc(sem["hs"], 1)
                assert cnt["hs"] == onor[qb]
    block.scalar(f_hs)

    def f_ho(e):
        if h + 1 < HPC:
            e.wait_ge(sem["hv"], pos["ropek"])
            _loads(e, h + 1)
        for qb in range(8):
            e.wait_ge(sem["hv"], odiv[qb])
            hop = "ho0" if qb % 2 == 0 else "ho1"
            cnt[hop] += 16
            e.dma_start(out=OT[h * 128:(h + 1) * 128, qb * 512:(qb + 1) * 512],
                        in_=B["o_sb"][:, (qb % 2) * 512:(qb % 2 + 1) * 512]
                        ).then_inc(sem[hop], 16)
    block.sync(f_ho)


def build_launchD():
    ctx = new_ctx("lD")
    nc, es = ctx.nc, ctx.es
    sem, cnt = ctx.sem, ctx.cnt

    xnT_g = nc.dram_tensor("xnT_g", [4, 128, NI * 1024], BF16, kind="ExternalInput")
    wqkv12 = nc.dram_tensor("wqkv12", [24, 128, D], F32, kind="ExternalInput")
    qdecD = nc.dram_tensor("qdecD", [8, 128, 256], F32, kind="ExternalInput")
    kdecD = nc.dram_tensor("kdecD", [128, 16], F32, kind="ExternalInput")
    DdecD = nc.dram_tensor("DdecD", [8, 128, 512], BF16, kind="ExternalInput")
    bdBD = nc.dram_tensor("bdBD", [128, 8], F32, kind="ExternalInput")
    identD = nc.dram_tensor("identD", [128, 128], BF16, kind="ExternalInput")
    OT = nc.dram_tensor("OT", [1024, S], BF16, kind="ExternalOutput")
    qkvT = nc.dram_tensor("qkvT", [3072, S], BF16)

    NQ, TQ = 4, 1024
    NB = 16

    with ctx.es:
        block = es.enter_context(nc.Block())
        ident = ctx.sb("ident", [128, 128], BF16)
        kdec = ctx.sb("kdec", [128, 16], F32)
        bdB = ctx.sb("bdB", [128, 8], F32)
        ctx.new_sem("ii")

        def f_consts(e):
            for o, i_ in ((ident, identD), (kdec, kdecD), (bdB, bdBD)):
                cnt["ii"] += 16
                e.dma_start(out=o[:, :], in_=i_[:, :]).then_inc(sem["ii"], 16)
        block.sync(f_consts)

        pscope = ExitStack()
        xnq = pscope.enter_context(nc.sbuf_tensor("xnq", [128, NI * TQ], BF16))
        w_sb = pscope.enter_context(nc.sbuf_tensor("pw", [128, 2 * D], BF16))
        obf = pscope.enter_context(nc.sbuf_tensor("pobf", [128, 2 * TQ], BF16))
        psp = pscope.enter_context(nc.psum_tensor("psp", [128, 2 * TQ], F32))
        for s_ in ("qx", "qw0", "qw1", "qp", "qc", "qo0", "qo1"):
            ctx.new_sem(s_)
        NG = NQ * 24

        def f_qw(e):
            for g in range(NG):
                ot = g % 24
                if g >= 2:
                    e.wait_ge(sem["qc"], g - 1)
                swp = "qw0" if g % 2 == 0 else "qw1"
                cnt[swp] += 16
                e.dma_start(out=w_sb[:, (g % 2) * D:(g % 2 + 1) * D],
                            in_=wqkv12.ap()[ot]).then_inc(sem[swp], 16)
        block.gpsimd(f_qw)

        def f_qt(e):
            e.wait_ge(sem["ii"], 48)
            for g in range(NG):
                tq = g // 24
                e.wait_ge(sem["qx"], 16 * (tq + 1))
                e.wait_ge(sem["qw0" if g % 2 == 0 else "qw1"], 16 * (g // 2 + 1))
                if g >= 2:
                    e.wait_ge(sem["qc"], g - 1)
                for tc in range(2):
                    for i in range(NI):
                        mm = e.matmul(psp[:, (g % 2) * TQ + tc * 512:
                                          (g % 2) * TQ + (tc + 1) * 512],
                                      w_sb[:, (g % 2) * D + i * 128:
                                           (g % 2) * D + (i + 1) * 128],
                                      xnq[:, i * TQ + tc * 512:i * TQ + (tc + 1) * 512],
                                      start=(i == 0), stop=(i == NI - 1))
                cnt["qp"] += 1
                mm.then_inc(sem["qp"], 1)
        block.tensor(f_qt)

        def f_qc(e):
            for g in range(NG):
                e.wait_ge(sem["qp"], g + 1)
                if g >= 2:
                    e.wait_ge(sem["qo0" if g % 2 == 0 else "qo1"],
                              16 * ((g - 2) // 2 + 1))
                cnt["qc"] += 1
                e.activation(obf[:, (g % 2) * TQ:(g % 2 + 1) * TQ],
                             psp[:, (g % 2) * TQ:(g % 2 + 1) * TQ],
                             AF.Silu).then_inc(sem["qc"], 1)
        block.scalar(f_qc)

        def _qstore(e, g):
            tq, ot = g // 24, g % 24
            e.wait_ge(sem["qc"], g + 1)
            sop = "qo0" if g % 2 == 0 else "qo1"
            cnt[sop] += 16
            e.dma_start(out=qkvT[ot * 128:(ot + 1) * 128, tq * TQ:(tq + 1) * TQ],
                        in_=obf[:, (g % 2) * TQ:(g % 2 + 1) * TQ]
                        ).then_inc(sem[sop], 16)

        def f_qxs(e):
            for g in range(NG):
                if g % 24 == 0:
                    tq = g // 24
                    if tq >= 1:
                        e.wait_ge(sem["qp"], tq * 24)
                    cnt["qx"] += 16
                    e.dma_start(out=xnq[:, :], in_=xnT_g.ap()[tq]
                                ).then_inc(sem["qx"], 16)
                if g >= 1:
                    _qstore(e, g - 1)
            _qstore(e, NG - 1)
        block.sync(f_qxs)
        proj_done = [("qo0", cnt["qo0"]), ("qo1", cnt["qo1"])]
        pscope.close()

        bufs = dict(
            qT=ctx.sb("qT", [128, 2 * S], BF16),
            kT=ctx.sb("kT", [128, 2 * S], BF16),
            vraw=ctx.sb("vraw", [128, 2 * S], BF16),
            v_sb=ctx.sb("v_sb", [128, S], BF16),
            ktk=ctx.sb("ktk", [128, S], BF16),
            qdq=ctx.sb("qdq", [128, 2 * S], BF16),
            qdc=ctx.sb("qdc", [128, 2 * 256], F32),
            Dd=ctx.sb("Dd", [128, 2 * 512], BF16),
            P_b=ctx.sb("P_b", [128, 4 * 256], BF16),
            PT_sb=ctx.sb("PT_sb", [128, 2 * 256], BF16),
            o_sb=ctx.sb("o_sb", [128, 2 * 256], BF16),
            S_f=ctx.sb("S_f", [128, 128], F32),
            S_b=ctx.sb("S_b", [128, 128], BF16),
            psS=ctx.ps("psS", [128, 4 * 512], F32),
            ptr2=ctx.ps("ptr2", [128, 2048], BF16),
            psA=ctx.ps("psA", [128, 256], F32),
            psK=ctx.ps("psK", [128, 128], F32),
        )
        for s_ in ("hx", "hv", "hs", "hp", "ho0", "ho1"):
            ctx.new_sem(s_)

        for h in range(HPC):
            _emit_D_head(ctx, block, h, qkvT, OT, bufs, qdecD, DdecD, kdec, bdB,
                         ident, proj_done if h == 0 else None)

        def f_fin(e):
            e.wait_ge(sem["ho0"], cnt["ho0"])
            e.wait_ge(sem["ho1"], cnt["ho1"])
        block.sync(f_fin)
    return nc


def _emit_D_head(ctx, block, h, qkvT, OT, B, qdecD, DdecD, kdec, bdB, ident,
                 first_wait):
    sem, cnt = ctx.sem, ctx.cnt
    NKT = 32
    NB = 16
    bhp, bhs, bhv, bhx = cnt["hp"], cnt["hs"], cnt["hv"], cnt["hx"]
    bho0, bho1 = cnt["ho0"], cnt["ho1"]
    sl = (h % 2) * S
    sl2 = ((h + 1) % 2) * S

    hp = hs = hv = 0
    vtr = [bhp + i + 1 for i in range(NKT)]; hp = NKT
    ktr = [bhp + NKT + i + 1 for i in range(NKT)]; hp = 2 * NKT
    vev = [bhs + i + 1 for i in range(NKT)]; hs = NKT
    kev = [bhs + NKT + i + 1 for i in range(NKT)]; hs = 2 * NKT
    qdqp = [bhv + i + 1 for i in range(NB)]; hv = NB
    hv += 2
    sz = bhv + hv
    smm = [[0, 0] for _ in range(NB)]
    dml = [[0, 0] for _ in range(NB)]
    ptr_ = [[0] * 4 for _ in range(NB)]
    ptc = [[0] * 4 for _ in range(NB)]
    avm = [[0, 0, 0] for _ in range(NB)]
    kmm = [[0, 0] for _ in range(NB)]
    oevp = [0] * NB
    supd = [0] * NB
    sbcp = [0] * NB
    for blk in (0, 1):
        for qsub in range(2):
            hp += 1; smm[blk][qsub] = bhp + hp
    for blk in range(NB):
        for j in range(4):
            hp += 1; ptr_[blk][j] = bhp + hp
        hp += 1; avm[blk][0] = bhp + hp
        hp += 1; avm[blk][1] = bhp + hp
        hp += 1; kmm[blk][0] = bhp + hp
        hp += 1; kmm[blk][1] = bhp + hp
        if blk + 2 < NB:
            for qsub in range(2):
                hp += 1; smm[blk + 2][qsub] = bhp + hp
        hp += 1; avm[blk][2] = bhp + hp
    for blk in range(NB):
        for qsub in range(2):
            hv += 1; dml[blk][qsub] = bhv + hv
        if blk >= 1:
            hv += 1; supd[blk - 1] = bhv + hv
            hv += 1; sbcp[blk - 1] = bhv + hv
    hv += 1; supd[NB - 1] = bhv + hv
    hv += 1; sbcp[NB - 1] = bhv + hv
    for blk in range(NB):
        for j in range(4):
            hs += 1; ptc[blk][j] = bhs + hs
        hs += 1; oevp[blk] = bhs + hs

    def f_hx(e):
        if first_wait:
            for s_, c_ in first_wait:
                e.wait_ge(sem[s_], c_)
        for r, t in ((h, "qT"), (8 + h, "kT"), (16 + h, "vraw")):
            cnt["hx"] += 16
            e.dma_start(out=B[t][:, sl:sl + S], in_=qkvT[r * 128:(r + 1) * 128, :]
                        ).then_inc(sem["hx"], 16)
        cnt["hx"] += 16
        e.dma_start(out=B["qdc"][:, (h % 2) * 256:(h % 2) * 256 + 256],
                    in_=qdecD.ap()[h]).then_inc(sem["hx"], 16)
        cnt["hx"] += 16
        e.dma_start(out=B["Dd"][:, (h % 2) * 512:(h % 2) * 512 + 512],
                    in_=DdecD.ap()[h]).then_inc(sem["hx"], 16)
    if h == 0:
        block.sync(f_hx)
    loads_done = cnt["hx"]

    def f_hv(e):
        e.wait_ge(sem["hx"], loads_done)
        if h == 0:
            e.wait_ge(sem["ii"], 48)
        else:
            e.wait_ge(sem["hp"], bhp)
        for blk in range(NB):
            cnt["hv"] += 1
            e.tensor_mul(B["qdq"][:, sl + blk * 256:sl + (blk + 1) * 256],
                         B["qT"][:, sl + blk * 256:sl + (blk + 1) * 256],
                         B["qdc"][:, (h % 2) * 256:(h % 2) * 256 + 256]
                         ).then_inc(sem["hv"], 1)
        ctx.serm(e.memset(B["S_f"][:, :], 0.0), "hv")
        ctx.serm(e.memset(B["S_b"][:, :], 0.0), "hv")
        assert cnt["hv"] == sz
        def _emit_state(e, blk):
            e.wait_ge(sem["hp"], kmm[blk][1])
            cnt["hv"] += 1
            e.scalar_tensor_tensor(out=B["S_f"][:, :], in0=B["S_f"][:, :],
                                   scalar=bdB[:, h:h + 1], in1=B["psK"][:, :],
                                   op0=ALU.mult, op1=ALU.add).then_inc(sem["hv"], 1)
            assert cnt["hv"] == supd[blk]
            e.wait_ge(sem["hp"], avm[blk][2])
            cnt["hv"] += 1
            e.tensor_copy(B["S_b"][:, :], B["S_f"][:, :]).then_inc(sem["hv"], 1)
            assert cnt["hv"] == sbcp[blk]

        for blk in range(NB):
            for qsub in range(2):
                e.wait_ge(sem["hp"], smm[blk][qsub])
                if blk >= 2:
                    e.wait_ge(sem["hp"], ptr_[blk - 2][3])
                s4 = (2 * blk + qsub) % 4
                cnt["hv"] += 1
                e.tensor_tensor(B["P_b"][:, s4 * 256:(s4 + 1) * 256],
                                B["psS"][:, s4 * 512:s4 * 512 + 256],
                                B["Dd"][:, (h % 2) * 512 + qsub * 256:
                                        (h % 2) * 512 + (qsub + 1) * 256],
                                op=ALU.mult).then_inc(sem["hv"], 1)
                assert cnt["hv"] == dml[blk][qsub]
            if blk >= 1:
                _emit_state(e, blk - 1)
        _emit_state(e, NB - 1)
    block.vector(f_hv)

    def f_ht(e):
        e.wait_ge(sem["hx"], loads_done)
        if h > 0:
            e.wait_ge(sem["hs"], bhs)
            e.wait_ge(sem["hv"], bhv)
        for kt in range(NKT):
            if kt >= 2:
                e.wait_ge(sem["hs"], vev[kt - 2])
            cnt["hp"] += 1
            e.transpose(B["ptr2"][:, (kt % 2) * 1024:(kt % 2) * 1024 + 128],
                        B["vraw"][:, sl + kt * 128:sl + (kt + 1) * 128],
                        ident[:, :]).then_inc(sem["hp"], 1)
            assert cnt["hp"] == vtr[kt]
        for kt in range(NKT):
            e.wait_ge(sem["hs"], vev[NKT - 1] if kt == 0 else kev[kt - 1])
            if kt == 1:
                e.wait_ge(sem["hs"], vev[NKT - 1])
            if kt >= 2:
                e.wait_ge(sem["hs"], kev[kt - 2])
            cnt["hp"] += 1
            e.transpose(B["ptr2"][:, (kt % 2) * 1024:(kt % 2) * 1024 + 128],
                        B["kT"][:, sl + kt * 128:sl + (kt + 1) * 128],
                        ident[:, :]).then_inc(sem["hp"], 1)
            assert cnt["hp"] == ktr[kt]

        def _emit_scores(e, blk):
            for qsub in range(2):
                if blk >= 2:
                    e.wait_ge(sem["hv"], dml[blk - 2][qsub])
                s4 = (2 * blk + qsub) % 4
                cnt["hp"] += 1
                e.matmul(B["psS"][:, s4 * 512:s4 * 512 + 256],
                         B["qT"][:, sl + blk * 256 + qsub * 128:
                                 sl + blk * 256 + (qsub + 1) * 128],
                         B["kT"][:, sl + blk * 256:sl + (blk + 1) * 256],
                         start=True, stop=True).then_inc(sem["hp"], 1)
                assert cnt["hp"] == smm[blk][qsub]

        for blk in (0, 1):
            _emit_scores(e, blk)
        for blk in range(NB):
            for j in range(4):
                ksub, qsub = j // 2, j % 2
                e.wait_ge(sem["hv"], dml[blk][qsub])
                if j >= 2:
                    e.wait_ge(sem["hs"], ptc[blk][j - 2])
                elif blk >= 1:
                    e.wait_ge(sem["hs"], ptc[blk - 1][j + 2])
                else:
                    e.wait_ge(sem["hs"], kev[NKT - 2 + j])
                s4 = (2 * blk + qsub) % 4
                cnt["hp"] += 1
                e.transpose(B["ptr2"][:, (j % 2) * 1024:(j % 2) * 1024 + 128],
                            B["P_b"][:, s4 * 256 + ksub * 128:
                                     s4 * 256 + (ksub + 1) * 128],
                            ident[:, :]).then_inc(sem["hp"], 1)
                assert cnt["hp"] == ptr_[blk][j]
            for ksub in range(2):
                e.wait_ge(sem["hs"], ptc[blk][ksub * 2 + 1])
                if ksub == 0 and blk >= 1:
                    e.wait_ge(sem["hs"], oevp[blk - 1])
                cnt["hp"] += 1
                e.matmul(B["psA"][:, :],
                         B["v_sb"][:, blk * 256 + ksub * 128:
                                   blk * 256 + (ksub + 1) * 128],
                         B["PT_sb"][:, ksub * 256:(ksub + 1) * 256],
                         start=(ksub == 0), stop=False).then_inc(sem["hp"], 1)
                assert cnt["hp"] == avm[blk][ksub]
            for ksub in range(2):
                if blk >= 1 and ksub == 0:
                    e.wait_ge(sem["hv"], supd[blk - 1])
                cnt["hp"] += 1
                e.matmul(B["psK"][:, :],
                         B["ktk"][:, blk * 256 + ksub * 128:
                                  blk * 256 + (ksub + 1) * 128],
                         B["v_sb"][:, blk * 256 + ksub * 128:
                                   blk * 256 + (ksub + 1) * 128],
                         start=(ksub == 0), stop=(ksub == 1)).then_inc(sem["hp"], 1)
                assert cnt["hp"] == kmm[blk][ksub]
            if blk + 2 < NB:
                _emit_scores(e, blk + 2)
            if blk == 0:
                e.wait_ge(sem["hv"], sz)
            else:
                e.wait_ge(sem["hv"], sbcp[blk - 1])
            e.wait_ge(sem["hv"], qdqp[blk])
            cnt["hp"] += 1
            e.matmul(B["psA"][:, :], B["S_b"][:, :],
                     B["qdq"][:, sl + blk * 256:sl + (blk + 1) * 256],
                     start=False, stop=True).then_inc(sem["hp"], 1)
            assert cnt["hp"] == avm[blk][2]
    block.tensor(f_ht)

    def f_hs(e):
        for kt in range(NKT):
            e.wait_ge(sem["hp"], vtr[kt])
            cnt["hs"] += 1
            e.copy(B["v_sb"][:, kt * 128:(kt + 1) * 128],
                   B["ptr2"][:, (kt % 2) * 1024:(kt % 2) * 1024 + 128]
                   ).then_inc(sem["hs"], 1)
            assert cnt["hs"] == vev[kt]
        for kt in range(NKT):
            e.wait_ge(sem["hp"], ktr[kt])
            if h == 0 and kt == 0:
                e.wait_ge(sem["ii"], 48)
            cnt["hs"] += 1
            e.activation(B["ktk"][:, kt * 128:(kt + 1) * 128],
                         B["ptr2"][:, (kt % 2) * 1024:(kt % 2) * 1024 + 128],
                         AF.Copy, scale=kdec[:, h * 2 + (kt % 2):h * 2 + (kt % 2) + 1]
                         ).then_inc(sem["hs"], 1)
            assert cnt["hs"] == kev[kt]
        for blk in range(NB):
            for j in range(4):
                ksub, qsub = j // 2, j % 2
                e.wait_ge(sem["hp"], ptr_[blk][j])
                if blk >= 1:
                    e.wait_ge(sem["hp"], avm[blk - 1][2])
                cnt["hs"] += 1
                e.copy(B["PT_sb"][:, ksub * 256 + qsub * 128:
                                  ksub * 256 + (qsub + 1) * 128],
                       B["ptr2"][:, (j % 2) * 1024:(j % 2) * 1024 + 128]
                       ).then_inc(sem["hs"], 1)
                assert cnt["hs"] == ptc[blk][j]
            e.wait_ge(sem["hp"], avm[blk][2])
            if blk >= 2:
                e.wait_ge(sem["ho0" if blk % 2 == 0 else "ho1"],
                          (bho0 if blk % 2 == 0 else bho1) + 16 * ((blk - 2) // 2 + 1))
            elif h > 0:
                e.wait_ge(sem["ho0" if blk % 2 == 0 else "ho1"],
                          bho0 if blk % 2 == 0 else bho1)
            cnt["hs"] += 1
            e.copy(B["o_sb"][:, (blk % 2) * 256:(blk % 2 + 1) * 256],
                   B["psA"][:, :]).then_inc(sem["hs"], 1)
            assert cnt["hs"] == oevp[blk]
    block.scalar(f_hs)

    def f_ho(e):
        if h + 1 < HPC:
            if h >= 1:
                e.wait_ge(sem["hp"], bhp)
                e.wait_ge(sem["hv"], bhv)
            for r, t in ((h + 1, "qT"), (8 + h + 1, "kT"), (16 + h + 1, "vraw")):
                cnt["hx"] += 16
                e.dma_start(out=B[t][:, sl2:sl2 + S],
                            in_=qkvT[r * 128:(r + 1) * 128, :]
                            ).then_inc(sem["hx"], 16)
            cnt["hx"] += 16
            e.dma_start(out=B["qdc"][:, ((h + 1) % 2) * 256:((h + 1) % 2) * 256 + 256],
                        in_=qdecD.ap()[h + 1]).then_inc(sem["hx"], 16)
            cnt["hx"] += 16
            e.dma_start(out=B["Dd"][:, ((h + 1) % 2) * 512:((h + 1) % 2) * 512 + 512],
                        in_=DdecD.ap()[h + 1]).then_inc(sem["hx"], 16)
        for blk in range(NB):
            e.wait_ge(sem["hs"], oevp[blk])
            hop = "ho0" if blk % 2 == 0 else "ho1"
            cnt[hop] += 16
            e.dma_start(out=OT[h * 128:(h + 1) * 128, blk * 256:(blk + 1) * 256],
                        in_=B["o_sb"][:, (blk % 2) * 256:(blk % 2 + 1) * 256]
                        ).then_inc(sem[hop], 16)
    block.sync(f_ho)


def build_launchE():
    ctx = new_ctx("lE")
    nc, es = ctx.nc, ctx.es
    sem, cnt = ctx.sem, ctx.cnt

    x4 = nc.dram_tensor("x4", [TPC, D], F32, kind="ExternalInput")
    attnT = nc.dram_tensor("attnT", [128, NI * TPC], BF16, kind="ExternalInput")
    xn5T = nc.dram_tensor("xn5T", [128, NI * TPC], BF16, kind="ExternalInput")
    gwT = nc.dram_tensor("gwT", [NI, 128, D], F32, kind="ExternalInput")
    owT = nc.dram_tensor("owT", [NI, 128, D], F32, kind="ExternalInput")
    nwD = nc.dram_tensor("nwD", [128, NI], F32, kind="ExternalInput")
    l1wT = nc.dram_tensor("l1wT", [NI, 128, D], F32, kind="ExternalInput")
    wqkhD = nc.dram_tensor("wqkhD", [128, NI * 640], F32, kind="ExternalInput")
    BmD = nc.dram_tensor("BmD", [128, D], F32, kind="ExternalInput")
    bbD = nc.dram_tensor("bbD", [128, NI], F32, kind="ExternalInput")
    identD = nc.dram_tensor("identD", [128, 128], BF16, kind="ExternalInput")
    x6 = nc.dram_tensor("x6", [TPC, D], F32, kind="ExternalOutput")
    x5 = nc.dram_tensor("x5", [TPC, D], F32)

    with ctx.es:
        block = es.enter_context(nc.Block())
        ident = ctx.sb("ident", [128, 128], BF16)
        ptrans = ctx.ps("ptrans", [128, 2048], BF16)
        rhs_sb = ctx.sb("rhs_sb", [128, NI * TPC], BF16)
        lnb = LnBufs(ctx, nxn=1, xa2=False)
        psX = ctx.ps("psX", [128, 1024], F32)
        fw = ctx.sb("fw", [128, 2 * D], BF16)
        fps = ctx.ps("fps", [128, 2 * 1024], F32)
        fobf = ctx.sb("fobf", [128, 2 * 1024], BF16)
        fxsl = ctx.sb("fxsl", [128, 3 * 8 * 128], F32)
        fshared = (fw, fps, fobf, fxsl)
        nw = ctx.sb("nw", [128, NI], F32)
        rstdb = ctx.sb("rstdb", [128, TPC], F32)
        gbuf = ctx.sb("gbuf", [128, 2 * TPC], BF16)
        sqbuf = ctx.sb("sqbuf", [128, 2 * TPC], BF16)
        rsml = ctx.sb("rsml", [1, 2 * TPC], F32)
        rsb = ctx.sb("rsb", [1, TPC], BF16)
        epsb = ctx.sb("epsb", [1, 8], F32)
        ones128 = ctx.sb("ones128", [128, 1], BF16)
        ones1 = ctx.sb("ones1", [1, 128], BF16)
        for s_ in ("ii", "ax", "rv", "rp", "rs", "gw0", "gw1", "gp", "gc", "gv"):
            ctx.new_sem(s_)

        escope = ExitStack()
        xnT5 = escope.enter_context(nc.sbuf_tensor("xnT5", [128, NI * TPC], BF16))

        def f_ii(e):
            for o, i_ in ((ident[:, :], identD[:, :]), (nw[:, :], nwD[:, :]),
                          (rhs_sb[:, :], attnT[:, :]), (xnT5[:, :], xn5T[:, :])):
                cnt["ii"] += 16
                e.dma_start(out=o, in_=i_).then_inc(sem["ii"], 16)
        block.sync(f_ii)

        def f_vinit(e):
            ctx.serm(e.memset(epsb[:, :], EPS), "rv")
            ctx.serm(e.memset(ones128[:, :], 1.0), "rv")
            ctx.serm(e.memset(ones1[:, :], 1.0), "rv")
        block.vector(f_vinit)
        vinit = cnt["rv"]

        def f_sq(e):
            e.wait_ge(sem["ii"], 64)
            for ot in range(NI):
                if ot >= 2:
                    e.wait_ge(sem["rp"], ot - 1)
                cnt["rv"] += 1
                e.tensor_mul(sqbuf[:, (ot % 2) * TPC:(ot % 2 + 1) * TPC],
                             rhs_sb[:, ot * TPC:(ot + 1) * TPC],
                             rhs_sb[:, ot * TPC:(ot + 1) * TPC]).then_inc(sem["rv"], 1)
        block.vector(f_sq)
        sq_done = cnt["rv"]

        def f_rp(e):
            e.wait_ge(sem["rv"], vinit)
            for ot in range(NI):
                e.wait_ge(sem["rv"], vinit + ot + 1)
                for c in range(2):
                    mm = e.matmul(psX[0:1, c * 512:(c + 1) * 512], ones128[:, 0:1],
                                  sqbuf[:, (ot % 2) * TPC + c * 512:
                                        (ot % 2) * TPC + (c + 1) * 512],
                                  start=(ot == 0), stop=(ot == NI - 1))
                cnt["rp"] += 1
                mm.then_inc(sem["rp"], 1)
            e.wait_ge(sem["rv"], sq_done + 2)
            e.wait_ge(sem["rs"], 1)
            for c in range(2):
                cnt["rp"] += 1
                e.matmul(psX[:, c * 512:(c + 1) * 512], ones1[0:1, :],
                         rsb[0:1, c * 512:(c + 1) * 512],
                         start=True, stop=True).then_inc(sem["rp"], 1)
        block.tensor(f_rp)

        def f_rs(e):
            e.wait_ge(sem["rp"], NI)
            cnt["rs"] += 1
            e.activation(rsml[0:1, 0:TPC], psX[0:1, 0:TPC], AF.Sqrt,
                         scale=1.0 / D, bias=epsb[0:1, 0:1]).then_inc(sem["rs"], 1)
            e.wait_ge(sem["rp"], NI + 2)
            cnt["rs"] += 1
            e.copy(rstdb[:, :], psX[:, :]).then_inc(sem["rs"], 1)
        block.scalar(f_rs)

        def f_rv2(e):
            e.wait_ge(sem["rs"], 1)
            ctx.serm(e.reciprocal(rsml[0:1, TPC:2 * TPC], rsml[0:1, 0:TPC]), "rv")
            ctx.serw(e, "rv")
            ctx.serm(e.tensor_copy(rsb[0:1, :], rsml[0:1, TPC:2 * TPC]), "rv")
        block.vector(f_rv2)

        def f_gw(e):
            for ot in range(NI):
                if ot >= 2:
                    e.wait_ge(sem["gc"], ot - 1)
                swp = "gw0" if ot % 2 == 0 else "gw1"
                cnt[swp] += 16
                e.dma_start(out=fw[:, (ot % 2) * D:(ot % 2 + 1) * D],
                            in_=gwT.ap()[ot]).then_inc(sem[swp], 16)
        block.gpsimd(f_gw)

        def f_gt(e):
            for ot in range(NI):
                e.wait_ge(sem["gw0" if ot % 2 == 0 else "gw1"], 16 * (ot // 2 + 1))
                if ot >= 2:
                    e.wait_ge(sem["gc"], ot - 1)
                for tc in range(2):
                    for i in range(NI):
                        mm = e.matmul(fps[:, (ot % 2) * TPC + tc * 512:
                                          (ot % 2) * TPC + (tc + 1) * 512],
                                      fw[:, (ot % 2) * D + i * 128:
                                         (ot % 2) * D + (i + 1) * 128],
                                      xnT5[:, i * TPC + tc * 512:i * TPC + (tc + 1) * 512],
                                      start=(i == 0), stop=(i == NI - 1))
                cnt["gp"] += 1
                mm.then_inc(sem["gp"], 1)
        block.tensor(f_gt)

        def f_gc(e):
            for ot in range(NI):
                e.wait_ge(sem["gp"], ot + 1)
                if ot >= 2:
                    e.wait_ge(sem["gv"], 3 * (ot - 1))
                cnt["gc"] += 1
                e.activation(gbuf[:, (ot % 2) * TPC:(ot % 2 + 1) * TPC],
                             fps[:, (ot % 2) * TPC:(ot % 2 + 1) * TPC],
                             AF.Sigmoid).then_inc(sem["gc"], 1)
        block.scalar(f_gc)

        def f_gv(e):
            e.wait_ge(sem["rv"], sq_done + 2)
            e.wait_ge(sem["rs"], 2)
            for ot in range(NI):
                e.wait_ge(sem["gc"], ot + 1)
                sl = rhs_sb[:, ot * TPC:(ot + 1) * TPC]
                cnt["gv"] += 1
                e.tensor_scalar(out=sl, in0=sl, scalar1=nw[:, ot:ot + 1],
                                scalar2=None, op0=ALU.mult).then_inc(sem["gv"], 1)
                ctx.serw(e, "gv")
                cnt["gv"] += 1
                e.tensor_mul(sl, sl, rstdb[:, :]).then_inc(sem["gv"], 1)
                ctx.serw(e, "gv")
                cnt["gv"] += 1
                e.tensor_mul(sl, sl, gbuf[:, (ot % 2) * TPC:(ot % 2 + 1) * TPC]
                             ).then_inc(sem["gv"], 1)
        block.vector(f_gv)
        gv_done = cnt["gv"]
        escope.close()

        st1 = emit_fproj(ctx, block, "p",
                         w_dram=owT, nko=NI, kt=NI,
                         rhs_slice=lambda i, tc: rhs_sb[:, i * TPC + tc * 512:
                                                        i * TPC + (tc + 1) * 512],
                         n_tc=2, extras=None,
                         wait_tensor=(("gv", gv_done), ("gc", NI)),
                         evac_mode="resadd", ident=ident, ptrans=ptrans,
                         x_in_dram=x4, x_out_dram=x5, res_scale=1.0,
                         shared_bufs=fshared,
                         wait_gpsimd=[("gp", NI)],
                         wait_scalar_first=[("gv", gv_done)])

        st2 = emit_l1ffn(ctx, block, "d", x_in=x5, x_out=x6, l1wT=l1wT,
                         wqkhD=wqkhD, BmD=BmD, bbD=bbD, ident=ident,
                         ptrans=ptrans, lnb=lnb, xnT_shared=rhs_sb, psX=psX,
                         wait_sync=st1, wait_vec=[("pp", NI)],
                         fshared=fshared,
                         fgates={"tensor": [("pc", NI)],
                                 "gpsimd": [("pp", NI)],
                                 "sync": st1,
                                 "scalar": [("pt", NI * NT)],
                                 "lnpe": [("pv", NI * NT)],
                                 "l1ld": [("gp", NI)],
                                 "l1pe": []})

        def f_fin(e):
            for s_, c_ in st2:
                e.wait_ge(sem[s_], c_)
        block.sync(f_fin)
    return nc


def _prep_E(inputs):
    pr = {}
    pr["gwT"] = _tile_w(np.ascontiguousarray(inputs["L1_attn_gate_w"].T))
    pr["owT"] = _tile_w(np.ascontiguousarray(inputs["L1_attn_out_w"].T))
    pr["nwD"] = _pmaj(np.ascontiguousarray(inputs["L1_attn_norm_w"].reshape(D, 1)))
    return pr


def _xn_slice_pm(xn_b, j):
    sl = np.ascontiguousarray(xn_b[j * TPC:(j + 1) * TPC, :])
    a = sl.view(np.uint16).reshape(TPC, NI, 128).transpose(2, 1, 0)
    return np.ascontiguousarray(a.reshape(128, NI * TPC)).view(BF)


def run_E(inputs, x4, attnT5, xn5):
    prE = _prep_E(inputs)
    prC = _prep_C(inputs)
    common = _prep_common(inputs)
    nc = build_launchE()
    in_maps = []
    for c in range(8):
        b, j = c // 4, c % 4
        m = {"gwT": prE["gwT"], "owT": prE["owT"], "nwD": prE["nwD"],
             "l1wT": prC["l1wT"], "wqkhD": prC["wqkhD"], "BmD": prC["BmD"],
             "bbD": prC["bbD"], "identD": common["identD"]}
        m["x4"] = np.ascontiguousarray(x4[b, j * TPC:(j + 1) * TPC, :])
        m["attnT"] = _attnT_slice(attnT5[b], j)
        m["xn5T"] = _xn_slice_pm(xn5[b], j)
        in_maps.append(m)
    res = run_bass_kernel_spmd(nc, in_maps, core_ids=list(range(8)))
    x6 = np.zeros((2, S, D), np.float32)
    for c in range(8):
        b, j = c // 4, c % 4
        x6[b, j * TPC:(j + 1) * TPC] = res.results[c]["x6"]
    return x6

def _prep_common(inputs):
    p = {}
    p["identD"] = np.ascontiguousarray(np.eye(128, dtype=np.float32).astype(BF))
    return p


def _prep_A(inputs):
    pr = {}
    pr["wbT"] = _tile_w(np.ascontiguousarray(inputs["L0_ffn_W_base"].T))
    pr["bb"] = np.ascontiguousarray(inputs["L0_ffn_b_base"].reshape(1, D))
    pr["encT"] = _pmaj(np.concatenate([inputs["L0_ffn_enc0"].T,
                                       inputs["L0_ffn_enc1"].T], axis=1))
    pr["MTc"] = _build_MT_cols()
    pr["c0"] = np.stack([_pmaj(inputs["L0_ffn_coefs0"][e]) for e in range(3)])
    pr["m0"] = np.stack([_pmaj(inputs["L0_ffn_mask0"][e].astype(np.uint8)) for e in range(3)])
    pr["c1"] = np.stack([_pmaj(inputs["L0_ffn_coefs1"][e]) for e in range(3)])
    pr["m1"] = np.stack([_pmaj(inputs["L0_ffn_mask1"][e].astype(np.uint8)) for e in range(3)])
    return pr


def run_A(inputs):
    x0 = inputs["L0_attn_hidden_states"]
    shared = _prep_A(inputs)
    shared.update(_prep_common(inputs))
    nc = build_launchA()
    in_maps = []
    for c in range(8):
        b, j = c // 4, c % 4
        m = dict(shared)
        m["x0"] = np.ascontiguousarray(x0[b, j * TPC:(j + 1) * TPC, :])
        in_maps.append(m)
    res = run_bass_kernel_spmd(nc, in_maps, core_ids=list(range(8)))
    x1 = np.zeros((2, S, D), np.float32)
    xn2 = np.zeros((2, S, D), BF)
    for c in range(8):
        b, j = c // 4, c % 4
        x1[b, j * TPC:(j + 1) * TPC] = res.results[c]["x1"]
        xn2[b, j * TPC:(j + 1) * TPC] = res.results[c]["xn2"]
    return x1, xn2, res.results[0]["g_out"]


def _prep_B(inputs):
    wq, wk, wv = inputs["L0_attn_wq"], inputs["L0_attn_wk"], inputs["L0_attn_wv"]
    wqkv_hg = []
    for hg in range(4):
        sl = slice(hg * 1024, (hg + 1) * 1024)
        comb = np.concatenate([wq[:, sl], wk[:, sl], wv[:, sl]], axis=1)
        wqkv_hg.append(_tile_w(comb))
    cos, sin = _rope_tables()
    sinS = np.concatenate([-sin[0:64, :], sin[64:128, :]], axis=0)
    return wqkv_hg, cos, np.ascontiguousarray(sinS)


def _xnq_host(xn2_b):
    a = np.asarray(xn2_b).view(np.uint16).reshape(4, 1024, 32, 128)
    a = np.ascontiguousarray(a.transpose(0, 3, 2, 1).reshape(4, 128, 32 * 1024))
    return a.view(BF)


def run_B(inputs, xn2):
    wqkv_hg, cos, sin = _prep_B(inputs)
    common = _prep_common(inputs)
    tri = _tri_mask()
    nc = build_launchB()
    xnq = [_xnq_host(xn2[b]) for b in range(2)]
    in_maps = []
    for c in range(8):
        b, hg = c // 4, c % 4
        in_maps.append({"xnT_g": xnq[b], "wqkv": wqkv_hg[hg], "cosD": cos, "sinD": sin,
                        "identD": common["identD"], "triD": tri})
    res = run_bass_kernel_spmd(nc, in_maps, core_ids=list(range(8)))
    attnT = np.zeros((2, D, S), BF)
    for c in range(8):
        b, hg = c // 4, c % 4
        attnT[b, hg * 1024:(hg + 1) * 1024, :] = res.results[c]["OT"]
    return attnT


def _prep_C(inputs):
    pr = {}
    pr["woT"] = _tile_w(np.ascontiguousarray(inputs["L0_attn_wo"]))
    pr["wbT"] = _tile_w(np.ascontiguousarray(inputs["L0_ffn_W_base"].T))
    pr["encT"] = _pmaj(np.concatenate([inputs["L0_ffn_enc0"].T,
                                       inputs["L0_ffn_enc1"].T], axis=1))
    pr["l1wT"] = _tile_w(np.ascontiguousarray(inputs["L1_ffn_W"].T))
    A_flat = inputs["L1_ffn_A"].reshape(128, D)
    pr["wqkhD"] = _pmaj(np.ascontiguousarray(
        np.concatenate([inputs["L1_ffn_Wq"].T, inputs["L1_ffn_Wk"].T, A_flat.T],
                       axis=1)))
    pr["BmD"] = np.ascontiguousarray(
        inputs["L1_ffn_Bm"].transpose(0, 2, 1).reshape(128, D))
    pr["bbD"] = np.ascontiguousarray(
        inputs["L1_ffn_b"].astype(np.float32).reshape(NI, 128).T)
    return pr


def _attnT_slice(attnT_b, j):
    sl = np.ascontiguousarray(attnT_b[:, j * TPC:(j + 1) * TPC])
    a = sl.view(np.uint16).reshape(NI, 128, TPC).transpose(1, 0, 2)
    return np.ascontiguousarray(a.reshape(128, NI * TPC)).view(BF)


def run_C(inputs, x1, attnT, g):
    pr = _prep_C(inputs)
    pr.update(_prep_common(inputs))
    nc = build_launchC()
    in_maps = []
    for c in range(8):
        b, j = c // 4, c % 4
        m = dict(pr)
        m["x1"] = np.ascontiguousarray(x1[b, j * TPC:(j + 1) * TPC, :])
        m["attnT"] = _attnT_slice(attnT[b], j)
        m["g_in"] = g
        in_maps.append(m)
    res = run_bass_kernel_spmd(nc, in_maps, core_ids=list(range(8)))
    x4 = np.zeros((2, S, D), np.float32)
    xn5 = np.zeros((2, S, D), BF)
    for c in range(8):
        b, j = c // 4, c % 4
        x4[b, j * TPC:(j + 1) * TPC] = res.results[c]["x4"]
        xn5[b, j * TPC:(j + 1) * TPC] = res.results[c]["xn5"]
    return x4, xn5


def _prep_D(inputs):
    qd, kd, Ddec, bd = _l1_decays()
    qkvT_full = np.ascontiguousarray(inputs["L1_attn_qkv_w"].T)
    wqkv_hg, qdecs, kdecs, Ddecs, bds = [], [], [], [], []
    for hg in range(4):
        qc, kc, vc = [], [], []
        qdl = np.zeros((8, 128, 256), np.float32)
        kdl = np.zeros((128, 16), np.float32)
        Ddl = np.zeros((8, 128, 512), np.float32)
        bdl = np.zeros((128, 8), np.float32)
        for lh in range(8):
            gh = hg * 8 + lh
            base = gh * 384
            qc.append(qkvT_full[:, base:base + 128])
            kc.append(qkvT_full[:, base + 128:base + 256])
            vc.append(qkvT_full[:, base + 256:base + 384])
            qdl[lh] = np.broadcast_to(qd[gh][None, :], (128, 256))
            for sub in range(2):
                kdl[:, lh * 2 + sub] = kd[gh][sub * 128:(sub + 1) * 128]
            for qsub in range(2):
                Ddl[lh, :, qsub * 256:(qsub + 1) * 256] = \
                    Ddec[gh][qsub * 128:(qsub + 1) * 128, :].astype(np.float32)
            bdl[:, lh] = bd[gh]
        comb = np.concatenate(qc + kc + vc, axis=1)
        wqkv_hg.append(_tile_w(comb))
        qdecs.append(np.ascontiguousarray(qdl))
        kdecs.append(np.ascontiguousarray(kdl))
        Ddecs.append(np.ascontiguousarray(Ddl.astype(BF)))
        bds.append(np.ascontiguousarray(bdl))
    return wqkv_hg, qdecs, kdecs, Ddecs, bds


def run_D(inputs, xn5):
    wqkv_hg, qdecs, kdecs, Ddecs, bds = _prep_D(inputs)
    common = _prep_common(inputs)
    nc = build_launchD()
    xnq = [_xnq_host(xn5[b]) for b in range(2)]
    in_maps = []
    for c in range(8):
        b, hg = c // 4, c % 4
        in_maps.append({"xnT_g": xnq[b], "wqkv12": wqkv_hg[hg], "qdecD": qdecs[hg],
                        "kdecD": kdecs[hg], "DdecD": Ddecs[hg], "bdBD": bds[hg],
                        "identD": common["identD"]})
    res = run_bass_kernel_spmd(nc, in_maps, core_ids=list(range(8)))
    attnT = np.zeros((2, D, S), BF)
    for c in range(8):
        b, hg = c // 4, c % 4
        attnT[b, hg * 1024:(hg + 1) * 1024, :] = res.results[c]["OT"]
    return attnT


def kernel(**inputs):
    x1, xn2, g = run_A(inputs)
    attnT = run_B(inputs, xn2)
    x4, xn5 = run_C(inputs, x1, attnT, g)
    attnT5 = run_D(inputs, xn5)
    x6 = run_E(inputs, x4, attnT5, np.asarray(xn5))
    return x6


# version 08
# speedup vs baseline: 1.1702x; 1.1702x over previous
import math
from contextlib import ExitStack

import numpy as np
import ml_dtypes

import concourse.bass as bass
import concourse.mybir as mybir
from concourse.bass_utils import run_bass_kernel_spmd

F32 = mybir.dt.float32
BF16 = mybir.dt.bfloat16
U8 = mybir.dt.uint8
AF = mybir.ActivationFunctionType
ALU = mybir.AluOpType
AX = mybir.AxisListType
BF = ml_dtypes.bfloat16

D = 4096
S = 4096
TPC = 1024
NT = TPC // 128
NI = D // 128
HPC = 8
HD = 128
EPS = 1e-5
ISQ = 1.0 / math.sqrt(128.0)


def _build_MT_cols():
    n = D
    i = np.arange(n, dtype=np.float64)
    M = np.zeros((n, n))
    M[:, 0] = 1.0 / math.sqrt(n)
    f = np.arange(1, (n - 1) // 2 + 1, dtype=np.float64)
    ang = 2.0 * np.pi * np.outer(i, f) / n
    M[:, 1:-1:2] = math.sqrt(2.0 / n) * np.cos(ang)
    M[:, 2:-1:2] = math.sqrt(2.0 / n) * np.sin(ang)
    M[:, n - 1] = (1.0 / math.sqrt(n)) * np.where(i % 2 == 0, 1.0, -1.0)
    MT = M.T.astype(np.float32)
    cols = MT.reshape(n, 8, 512).transpose(1, 0, 2)
    out = np.stack([_pmaj(cols[oc]) for oc in range(8)])
    return np.ascontiguousarray(out.astype(BF))


def _rope_tables():
    hd = HD
    inv = (1.0 / (10000.0 ** (np.arange(0, hd, 2, dtype=np.float32) / hd))).astype(np.float32)
    pos = np.arange(S, dtype=np.float32)
    fr = pos[:, None] * inv[None, :]
    emb = np.concatenate([fr, fr], axis=-1).astype(np.float32)
    return (np.ascontiguousarray(np.cos(emb).T.astype(np.float32)),
            np.ascontiguousarray(np.sin(emb).T.astype(np.float32)))


def _tile_w(wT):
    wT = np.asarray(wT)
    K, N = wT.shape
    t = wT.reshape(K // 128, 128, N // 128, 128)
    return np.ascontiguousarray(t.transpose(2, 1, 0, 3).reshape(N // 128, 128, K))


def _pmaj(a):
    a = np.asarray(a)
    JP, R = a.shape
    return np.ascontiguousarray(a.reshape(JP // 128, 128, R).transpose(1, 0, 2)
                                .reshape(128, (JP // 128) * R))


def _tri_mask():
    out = np.zeros((128, 4 * 512), np.float32)
    p = np.arange(128)[:, None]
    j = np.arange(512)[None, :]
    for s_ in range(4):
        out[:, s_ * 512:(s_ + 1) * 512] = np.where(j < s_ * 128 + p, -1e30, 0.0)
    return np.ascontiguousarray(out.astype(BF))


def _l1_decays():
    HEADS_ = 32
    BLK = 256
    base = 1.0 / 2.0 ** (8.0 / HEADS_)
    slope = (base ** (np.arange(HEADS_, dtype=np.float64) + 1.0)) * (1.0 - 1 / (31 + 1e-05) + 1e-05)
    r = np.arange(BLK, dtype=np.float64) + 1.0
    qd = np.exp(-slope[:, None] * r[None, :])
    kd = np.exp(-slope[:, None] * (BLK - r)[None, :])
    diff = r[:, None] - r[None, :]
    Ddec = np.exp(np.where(diff >= 0, -slope[:, None, None] * diff, -np.inf))
    bd = np.exp(-slope * BLK)
    return (qd.astype(np.float32), kd.astype(np.float32),
            np.ascontiguousarray(Ddec.astype(BF)), bd)


class Ctx:
    def __init__(self, nc, es):
        self.nc = nc
        self.es = es
        self.sem = {}
        self.cnt = {}

    def new_sem(self, name):
        assert name not in self.sem, name
        self.sem[name] = self.es.enter_context(self.nc.semaphore(name))
        self.cnt[name] = 0
        return self.sem[name]

    def sb(self, name, shape, dt):
        return self.es.enter_context(self.nc.sbuf_tensor(name, shape, dt))

    def ps(self, name, shape, dt=F32):
        return self.es.enter_context(self.nc.psum_tensor(name, shape, dt))

    def serw(self, e, name):
        e.wait_ge(self.sem[name], self.cnt[name])

    def serm(self, inst, name):
        self.cnt[name] += 1
        inst.then_inc(self.sem[name], 1)


def new_ctx(name):
    nc = bass.Bass("TRN2", target_bir_lowering=False, name=name)
    es = ExitStack()
    return Ctx(nc, es)


class LnBufs:
    def __init__(self, ctx, nt=NT, nxn=2, xa2=True):
        self.xa2 = xa2
        self.xa = ctx.sb("lnb_xa", [128, (2 if xa2 else 1) * D], F32)
        self.st = ctx.sb("lnb_st", [128, 4 * nt], F32)
        self.nxn = nxn
        self.xn = ctx.sb("lnb_xn", [128, nxn * D], BF16)
        self.eps = ctx.sb("lnb_eps", [128, 1], F32)


def emit_ln(ctx, block, tag, x_dram, xnT_sb, ident, lnb, nt=NT, xn_bf_store=None,
            build_xnT=True, ptrans=None, wait_sync=None, wait_vec=None,
            wait_pe=None):
    sx, sv, ss, sp = (tag + c for c in "xvsp")
    sx1 = tag + "x1"
    sc2 = tag + "c2"
    for s_ in (sx, sx1, sv, ss, sp, sc2):
        ctx.new_sem(s_)
    sxp = (sx, sx1)
    so0, so1 = tag + "o0", tag + "o1"
    if xn_bf_store is not None:
        ctx.new_sem(so0)
        ctx.new_sem(so1)
    xa, st, xn = lnb.xa, lnb.st, lnb.xn
    sem, cnt = ctx.sem, ctx.cnt

    def f_sync(e):
        if wait_sync is not None:
            for ws_, wc_ in wait_sync:
                e.wait_ge(sem[ws_], wc_)
        dep = 2 if lnb.xa2 else 1
        for tt in range(nt):
            if tt >= dep:
                e.wait_ge(sem[sv], tt - dep + 1)
                e.wait_ge(sem[ss], tt - dep + 1)
            sl = (tt % 2) if lnb.xa2 else 0
            cnt[sxp[tt % 2]] += 16
            e.dma_start(out=xa[:, sl * D:(sl + 1) * D],
                        in_=x_dram[tt * 128:(tt + 1) * 128, :]
                        ).then_inc(sem[sxp[tt % 2]], 16)
    block.sync(f_sync)

    def f_vec(e):
        for tt in range(nt):
            e.wait_ge(sem[sxp[tt % 2]], 16 * (tt // 2 + 1))
            sl = (tt % 2) if lnb.xa2 else 0
            cnt[sv] += 1
            e.reduce_sum(st[:, tt:tt + 1], xa[:, sl * D:(sl + 1) * D],
                         axis=AX.X).then_inc(sem[sv], 1)
    block.vector(f_vec)

    bss0 = cnt[ss]

    def f_sc(e):
        for tt in range(nt):
            e.wait_ge(sem[sxp[tt % 2]], 16 * (tt // 2 + 1))
            if tt >= 1:
                e.wait_ge(sem[ss], bss0 + tt)
            sl = (tt % 2) if lnb.xa2 else 0
            cnt[ss] += 1
            e.activation(xn[:, 0:D], xa[:, sl * D:(sl + 1) * D], AF.Square,
                         accum_out=st[:, nt + tt:nt + tt + 1]).then_inc(sem[ss], 1)
    block.scalar(f_sc)

    def f_vstat(e):
        e.wait_ge(sem[sv], cnt[sv])
        e.wait_ge(sem[ss], cnt[ss])
        e.memset(lnb.eps[:, :], EPS)
        ctx.serm(e.tensor_scalar(out=st[:, 0:nt], in0=st[:, 0:nt], scalar1=1.0 / D,
                                 scalar2=None, op0=ALU.mult), sv)
        ctx.serw(e, sv)
        ctx.serm(e.tensor_mul(st[:, 2 * nt:3 * nt], st[:, 0:nt], st[:, 0:nt]), sv)
        ctx.serw(e, sv)
        ctx.serm(e.scalar_tensor_tensor(out=st[:, nt:2 * nt], in0=st[:, nt:2 * nt],
                                        scalar=1.0 / D, in1=st[:, 2 * nt:3 * nt],
                                        op0=ALU.mult, op1=ALU.subtract), sv)
    block.vector(f_vstat)

    def f_sstat(e):
        e.wait_ge(sem[sv], cnt[sv])
        cnt[ss] += 1
        e.activation(st[:, 2 * nt:3 * nt], st[:, nt:2 * nt], AF.Sqrt,
                     bias=lnb.eps[:, :]).then_inc(sem[ss], 1)
    block.scalar(f_sstat)

    def f_vrec(e):
        e.wait_ge(sem[ss], cnt[ss])
        cnt[sv] += 1
        e.reciprocal(st[:, 3 * nt:4 * nt], st[:, 2 * nt:3 * nt]).then_inc(sem[sv], 1)
    block.vector(f_vrec)

    bx2 = (cnt[sx], cnt[sx1])
    bv, bp = cnt[sv], cnt[sp]
    bc2 = cnt[sc2]
    per_tt = 1

    def _store(e, tt):
        e.wait_ge(sem[sv], bv + tt * per_tt + 1)
        sop = so0 if tt % 2 == 0 else so1
        cnt[sop] += 16
        e.dma_start(out=xn_bf_store[tt * 128:(tt + 1) * 128, :],
                    in_=xn[:, (tt % lnb.nxn) * D:(tt % lnb.nxn + 1) * D]).then_inc(sem[sop], 16)

    def f_sync2(e):
        e.wait_ge(sem[sv], bv)
        e.wait_ge(sem[ss], cnt[ss])
        dep = 2 if lnb.xa2 else 1
        for tt in range(nt):
            if tt >= dep:
                e.wait_ge(sem[sv], bv + (tt - dep) * per_tt + 1)
            sl = (tt % 2) if lnb.xa2 else 0
            cnt[sxp[tt % 2]] += 16
            e.dma_start(out=xa[:, sl * D:(sl + 1) * D],
                        in_=x_dram[tt * 128:(tt + 1) * 128, :]
                        ).then_inc(sem[sxp[tt % 2]], 16)
            if xn_bf_store is not None and tt >= 1:
                _store(e, tt - 1)
        if xn_bf_store is not None:
            _store(e, nt - 1)
    block.sync(f_sync2)

    def f_vec2(e):
        if wait_vec is not None:
            for ws_, wc_ in wait_vec:
                e.wait_ge(sem[ws_], wc_)
        for tt in range(nt):
            e.wait_ge(sem[sxp[tt % 2]], bx2[tt % 2] + 16 * (tt // 2 + 1))
            if tt >= 1:
                e.wait_ge(sem[sv], bv + tt * per_tt)
            if tt >= 2 and xn_bf_store is not None:
                e.wait_ge(sem[so0 if tt % 2 == 0 else so1], 16 * ((tt - 2) // 2 + 1))
            if tt >= lnb.nxn and build_xnT:
                e.wait_ge(sem[sp], bp + (tt - lnb.nxn + 1) * NI)
            sl = (tt % 2) if lnb.xa2 else 0
            cnt[sv] += 1
            e.tensor_scalar(out=xn[:, (tt % lnb.nxn) * D:(tt % lnb.nxn + 1) * D],
                            in0=xa[:, sl * D:(sl + 1) * D],
                            scalar1=st[:, tt:tt + 1],
                            scalar2=st[:, 3 * nt + tt:3 * nt + tt + 1],
                            op0=ALU.subtract, op1=ALU.mult).then_inc(sem[sv], 1)
    block.vector(f_vec2)

    if build_xnT:
        def f_sc2(e):
            if wait_vec is not None:
                for ws_, wc_ in wait_vec:
                    e.wait_ge(sem[ws_], wc_)
            for tt in range(nt):
                for i in range(NI):
                    k = tt * NI + i
                    e.wait_ge(sem[sp], bp + k + 1)
                    cnt[sc2] += 1
                    e.copy(xnT_sb[:, i * (nt * 128) + tt * 128:
                                  i * (nt * 128) + (tt + 1) * 128],
                           ptrans[:, (k % 2) * 1024:(k % 2) * 1024 + 128]
                           ).then_inc(sem[sc2], 1)
        block.scalar(f_sc2)

    if build_xnT:
        def f_pe2(e):
            if wait_pe is not None:
                for ws_, wc_ in wait_pe:
                    e.wait_ge(sem[ws_], wc_)
            for tt in range(nt):
                e.wait_ge(sem[sv], bv + tt * per_tt + 1)
                for i in range(NI):
                    k = tt * NI + i
                    if k >= 2:
                        e.wait_ge(sem[sc2], bc2 + k - 1)
                    cnt[sp] += 1
                    e.transpose(ptrans[:, (k % 2) * 1024:(k % 2) * 1024 + 128],
                                xn[:, (tt % lnb.nxn) * D + i * 128:
                                   (tt % lnb.nxn) * D + (i + 1) * 128],
                                ident[:, :]).then_inc(sem[sp], 1)
        block.tensor(f_pe2)

    if build_xnT:
        return sc2, cnt[sc2]
    return sv, cnt[sv]


def emit_fproj(ctx, block, tag, *, w_dram, nko, kt, rhs_slice, n_tc,
               tc_width=512, extras=None, wait_tensor=(), evac_mode,
               ident=None, ptrans=None,
               x_in_dram=None, x_out_dram=None, res_scale=1.0,
               evac_bias=None, out_sb=None, act=None, out_dram_bf=None,
               wait_gpsimd=None, wait_sync_first=None, wait_scalar_first=None,
               scoped=False, shared_bufs=None):
    sw0, sw1, sp, sc = tag + "w0", tag + "w1", tag + "p", tag + "c"
    for s_ in (sw0, sw1, sp, sc):
        ctx.new_sem(s_)
    sem, cnt = ctx.sem, ctx.cnt
    K = kt * 128
    TW = n_tc * tc_width
    fscope = ExitStack()
    nc_ = ctx.nc
    if shared_bufs is not None:
        w_sb, pss = shared_bufs[0], shared_bufs[1]
    else:
        w_sb = fscope.enter_context(nc_.sbuf_tensor(tag + "_w", [128, 2 * K], BF16))
        pss = fscope.enter_context(nc_.psum_tensor(tag + "_ps", [128, 2 * TW], F32))

    def _wl(w):
        if w is None:
            return []
        if isinstance(w, tuple):
            return [w]
        return list(w)

    def f_g(e):
        for ws_, wc_ in _wl(wait_gpsimd):
            e.wait_ge(sem[ws_], wc_)
        for ot in range(nko):
            if ot >= 2:
                e.wait_ge(sem[sc], ot - 1)
            swp = sw0 if ot % 2 == 0 else sw1
            cnt[swp] += 16
            e.dma_start(
                out=w_sb[:, (ot % 2) * K:(ot % 2 + 1) * K],
                in_=w_dram.ap()[ot],
            ).then_inc(sem[swp], 16)
    block.gpsimd(f_g)

    if evac_mode in ("resadd", "dram_bf"):
        if shared_bufs is not None:
            obf = shared_bufs[2]
        else:
            obf = fscope.enter_context(nc_.sbuf_tensor(tag + "_obf", [128, 2 * TW], BF16))
    if evac_mode == "resadd":
        st_, sv_ = tag + "t", tag + "v"
        for s_ in (st_, sv_):
            ctx.new_sem(s_)
        ntt = TW // 128
        sxs = [tag + "x%d" % q for q in range(3)]
        sos = [tag + "o%d" % q for q in range(3)]
        for s_ in sxs + sos:
            ctx.new_sem(s_)
        if shared_bufs is not None:
            xsl = shared_bufs[3]
        else:
            xsl = fscope.enter_context(
                nc_.sbuf_tensor(tag + "_xsl", [128, 3 * ntt * 128], F32))
    if evac_mode == "dram_bf":
        sos2 = [tag + "o0", tag + "o1"]
        for s_ in sos2:
            ctx.new_sem(s_)

    def _emit_transposes(e, ot):
        e.wait_ge(sem[sc], ot + 1)
        for tt in range(ntt):
            k = ot * ntt + tt
            if k >= 2:
                e.wait_ge(sem[sv_], k - 1)
            cnt[st_] += 1
            e.transpose(ptrans[:, (k % 2) * 1024:(k % 2) * 1024 + 128],
                        obf[:, (ot % 2) * TW + tt * 128:(ot % 2) * TW + (tt + 1) * 128],
                        ident[:, :]).then_inc(sem[st_], 1)

    def f_t(e):
        for s_, c_ in wait_tensor:
            e.wait_ge(sem[s_], c_)
        for ot in range(nko):
            e.wait_ge(sem[sw0 if ot % 2 == 0 else sw1], 16 * (ot // 2 + 1))
            if ot >= 2:
                e.wait_ge(sem[sc], ot - 1)
            for tc in range(n_tc):
                pslice = pss[:, (ot % 2) * TW + tc * tc_width:
                             (ot % 2) * TW + (tc + 1) * tc_width]
                for i in range(kt):
                    mm = e.matmul(pslice,
                                  w_sb[:, (ot % 2) * K + i * 128:(ot % 2) * K + (i + 1) * 128],
                                  rhs_slice(i, tc),
                                  start=(i == 0),
                                  stop=(i == kt - 1 and extras is None))
                if extras is not None:
                    mm = extras(e, ot, tc, pslice)
            cnt[sp] += 1
            mm.then_inc(sem[sp], 1)
            if evac_mode == "resadd" and ot >= 1:
                _emit_transposes(e, ot - 1)
        if evac_mode == "resadd":
            _emit_transposes(e, nko - 1)
    block.tensor(f_t)

    if evac_mode == "resadd":
        def f_sc_(e):
            for ws_, wc_ in _wl(wait_scalar_first):
                e.wait_ge(sem[ws_], wc_)
            for ot in range(nko):
                e.wait_ge(sem[sp], ot + 1)
                if ot >= 2:
                    e.wait_ge(sem[st_], (ot - 1) * ntt)
                cnt[sc] += 1
                if evac_bias is not None:
                    e.activation(obf[:, (ot % 2) * TW:(ot % 2 + 1) * TW],
                                 pss[:, (ot % 2) * TW:(ot % 2 + 1) * TW],
                                 AF.Identity, bias=evac_bias(ot)).then_inc(sem[sc], 1)
                else:
                    e.copy(obf[:, (ot % 2) * TW:(ot % 2 + 1) * TW],
                           pss[:, (ot % 2) * TW:(ot % 2 + 1) * TW]
                           ).then_inc(sem[sc], 1)
        block.scalar(f_sc_)

        def f_v(e):
            for k in range(nko * ntt):
                g, tt = k // ntt, k % ntt
                e.wait_ge(sem[st_], k + 1)
                e.wait_ge(sem[sxs[g % 3]], 16 * ntt * (g // 3 + 1))
                xs = xsl[:, ((g % 3) * ntt + tt) * 128:((g % 3) * ntt + tt + 1) * 128]
                cnt[sv_] += 1
                e.scalar_tensor_tensor(out=xs,
                                       in0=ptrans[:, (k % 2) * 1024:(k % 2) * 1024 + 128],
                                       scalar=res_scale,
                                       in1=xs,
                                       op0=ALU.mult, op1=ALU.add).then_inc(sem[sv_], 1)
        block.vector(f_v)

        def f_sxo(e):
            for ws_, wc_ in _wl(wait_sync_first):
                e.wait_ge(sem[ws_], wc_)
            for g in range(nko + 2):
                if g < nko:
                    if g >= 3:
                        e.wait_ge(sem[sos[g % 3]],
                                  16 * ntt * ((g - 3) // 3 + 1))
                    for tt in range(ntt):
                        cnt[sxs[g % 3]] += 16
                        e.dma_start(out=xsl[:, ((g % 3) * ntt + tt) * 128:
                                            ((g % 3) * ntt + tt + 1) * 128],
                                    in_=x_in_dram[tt * 128:(tt + 1) * 128,
                                                  g * 128:(g + 1) * 128]
                                    ).then_inc(sem[sxs[g % 3]], 16)
                if g >= 2:
                    go = g - 2
                    for tt in range(ntt):
                        ko = go * ntt + tt
                        e.wait_ge(sem[sv_], ko + 1)
                        cnt[sos[go % 3]] += 16
                        e.dma_start(out=x_out_dram[tt * 128:(tt + 1) * 128,
                                                   go * 128:(go + 1) * 128],
                                    in_=xsl[:, ((go % 3) * ntt + tt) * 128:
                                            ((go % 3) * ntt + tt + 1) * 128]
                                    ).then_inc(sem[sos[go % 3]], 16)
        block.sync(f_sxo)
        if scoped:
            fscope.close()
        else:
            ctx.es.enter_context(fscope.pop_all())
        return [(s_, cnt[s_]) for s_ in sos]

    elif evac_mode == "sbuf":
        def f_sc_(e):
            for ot in range(nko):
                e.wait_ge(sem[sp], ot + 1)
                cnt[sc] += 1
                e.activation(out_sb[:, ot * TW:(ot + 1) * TW],
                             pss[:, (ot % 2) * TW:(ot % 2 + 1) * TW],
                             act if act is not None else AF.Copy).then_inc(sem[sc], 1)
        block.scalar(f_sc_)
        if scoped:
            fscope.close()
        else:
            ctx.es.enter_context(fscope.pop_all())
        return sc, cnt[sc]

    elif evac_mode == "dram_bf":
        def f_sc_(e):
            for ot in range(nko):
                e.wait_ge(sem[sp], ot + 1)
                if ot >= 2:
                    e.wait_ge(sem[sos2[ot % 2]], 16 * ((ot - 2) // 2 + 1))
                cnt[sc] += 1
                e.activation(obf[:, (ot % 2) * TW:(ot % 2 + 1) * TW],
                             pss[:, (ot % 2) * TW:(ot % 2 + 1) * TW],
                             act if act is not None else AF.Copy).then_inc(sem[sc], 1)
        block.scalar(f_sc_)

        def f_so(e):
            for ot in range(nko):
                e.wait_ge(sem[sc], ot + 1)
                cnt[sos2[ot % 2]] += 16
                e.dma_start(out=out_dram_bf[ot * 128:(ot + 1) * 128, :],
                            in_=obf[:, (ot % 2) * TW:(ot % 2 + 1) * TW]
                            ).then_inc(sem[sos2[ot % 2]], 16)
        block.sync(f_so)
        if scoped:
            fscope.close()
        else:
            ctx.es.enter_context(fscope.pop_all())
        return [(s_, cnt[s_]) for s_ in sos2]
    raise ValueError(evac_mode)


def emit_l0ffn(ctx, block, pfx, *, x_in, x_out, wbT, MTc, c0, m0, c1, m1, encT_d,
               bb_d, ident, ptrans, lnb, build_G, G_sb, g_dram_in=None,
               g_store=None, wait_sync=None, level=9, xnT_shared=None,
               psX=None, wait_vec=None, wait_gpsimd0=None, fscoped=False,
               fshared=None, fgates=None):
    nc = ctx.nc
    sem, cnt = ctx.sem, ctx.cnt
    xnT = xnT_shared if xnT_shared is not None else ctx.sb(pfx + "_xnT", [128, NI * TPC], BF16)
    eT = ctx.sb(pfx + "_eT", [41, TPC], BF16)
    encb = ctx.sb(pfx + "_encb", [128, NI * 40], BF16)
    ones = ctx.sb(pfx + "_ones", [1, TPC], BF16)
    psG = (psX[0:40, 0:1024] if psX is not None else None)
    if psG is None:
        psG_t = ctx.ps(pfx + "_psG", [40, 1024], F32)
        psG = psG_t[0:40, 0:1024]
    for s_ in ("ld", "vb", "gm", "pg", "sg", "eo", "gb", "gs"):
        ctx.new_sem(pfx + s_)

    gscope = ExitStack()
    if build_G:
        cwork = gscope.enter_context(nc.sbuf_tensor(pfx + "_cw", [128, 3 * 1024 + 3 * 256], F32))
        mwork = gscope.enter_context(nc.sbuf_tensor(pfx + "_mw", [128, 3 * 1024 + 3 * 256], U8))
        w01 = gscope.enter_context(nc.sbuf_tensor(pfx + "_w01", [128, 1024 + 256 + 1024], F32))
        wsum = gscope.enter_context(nc.sbuf_tensor(pfx + "_ws", [128, NI * 40], BF16))
        mtc = gscope.enter_context(nc.sbuf_tensor(pfx + "_mtc", [128, NI * 512], BF16))
    encf_tmp = gscope.enter_context(nc.sbuf_tensor(pfx + "_encf", [128, NI * 40], F32))

    def f_ld0(e):
        def dd(o, i_):
            cnt[pfx + "ld"] += 16
            e.dma_start(out=o, in_=i_).then_inc(sem[pfx + "ld"], 16)
        if build_G:
            for ee in range(3):
                dd(cwork[:, ee * 1024:(ee + 1) * 1024], c0.ap()[ee])
                dd(mwork[:, ee * 1024:(ee + 1) * 1024], m0.ap()[ee])
                dd(cwork[:, 3072 + ee * 256:3072 + (ee + 1) * 256], c1.ap()[ee])
                dd(mwork[:, 3072 + ee * 256:3072 + (ee + 1) * 256], m1.ap()[ee])
        else:
            dd(G_sb[:, :], g_dram_in[:, :])
        dd(encf_tmp[:, :], encT_d[:, :])
    block.sync(f_ld0)

    if build_G:
        def f_gb(e):
            cnt[pfx + "gb"] += 16
            e.dma_start(out=G_sb[40:41, :], in_=bb_d[:, :]).then_inc(sem[pfx + "gb"], 16)
        block.gpsimd(f_gb)

    def f_vb(e):
        vv = pfx + "vb"
        e.wait_ge(sem[pfx + "ld"], cnt[pfx + "ld"])
        if build_G:
            e.wait_ge(sem[pfx + "gb"], 16)
        ctx.serm(e.memset(ones[:, :], 1.0), vv)
        if build_G:
            for ee in range(3):
                ctx.serw(e, vv)
                ctx.serm(e.tensor_copy(w01[:, 1280:2304],
                                       mwork[:, ee * 1024:(ee + 1) * 1024]), vv)
                ctx.serw(e, vv)
                if ee == 0:
                    ctx.serm(e.tensor_mul(w01[:, 0:1024], cwork[:, 0:1024],
                                          w01[:, 1280:2304]), vv)
                else:
                    ctx.serm(e.tensor_mul(w01[:, 1280:2304],
                                          cwork[:, ee * 1024:(ee + 1) * 1024],
                                          w01[:, 1280:2304]), vv)
                    ctx.serw(e, vv)
                    ctx.serm(e.tensor_add(w01[:, 0:1024], w01[:, 0:1024],
                                          w01[:, 1280:2304]), vv)
            for ee in range(3):
                ctx.serw(e, vv)
                ctx.serm(e.tensor_copy(w01[:, 1280:1536],
                                       mwork[:, 3072 + ee * 256:3072 + (ee + 1) * 256]), vv)
                ctx.serw(e, vv)
                if ee == 0:
                    ctx.serm(e.tensor_mul(w01[:, 1024:1280], cwork[:, 3072:3328],
                                          w01[:, 1280:1536]), vv)
                else:
                    ctx.serm(e.tensor_mul(w01[:, 1280:1536],
                                          cwork[:, 3072 + ee * 256:3072 + (ee + 1) * 256],
                                          w01[:, 1280:1536]), vv)
                    ctx.serw(e, vv)
                    ctx.serm(e.tensor_add(w01[:, 1024:1280], w01[:, 1024:1280],
                                          w01[:, 1280:1536]), vv)
            for j in range(NI):
                ctx.serw(e, vv)
                ctx.serm(e.tensor_copy(wsum[:, j * 40:j * 40 + 32],
                                       w01[:, j * 32:(j + 1) * 32]), vv)
                ctx.serm(e.tensor_copy(wsum[:, j * 40 + 32:j * 40 + 40],
                                       w01[:, 1024 + j * 8:1024 + (j + 1) * 8]), vv)
        ctx.serm(e.tensor_copy(encb[:, :], encf_tmp[:, :]), vv)
    block.vector(f_vb)
    vb_done = cnt[pfx + "vb"]

    def f_ones(e):
        e.wait_ge(sem[pfx + "vb"], vb_done)
        cnt[pfx + "eo"] += 16
        e.dma_start(out=eT[40:41, :], in_=ones[:, :]).then_inc(sem[pfx + "eo"], 16)
    block.sync(f_ones)

    if build_G:
        def f_gm(e):
            for oc in range(8):
                if oc >= 1:
                    e.wait_ge(sem[pfx + "pg"], oc)
                cnt[pfx + "gm"] += 16
                e.dma_start(out=mtc[:, :],
                            in_=MTc.ap()[oc]).then_inc(sem[pfx + "gm"], 16)
        block.gpsimd(f_gm)

        def f_pg(e):
            e.wait_ge(sem[pfx + "vb"], vb_done)
            for oc in range(8):
                e.wait_ge(sem[pfx + "gm"], 16 * (oc + 1))
                if oc >= 2:
                    e.wait_ge(sem[pfx + "sg"], oc - 1)
                for j in range(NI):
                    mm = e.matmul(psG[:, (oc % 2) * 512:(oc % 2 + 1) * 512],
                                  wsum[:, j * 40:(j + 1) * 40],
                                  mtc[:, j * 512:(j + 1) * 512],
                                  start=(j == 0), stop=(j == NI - 1))
                cnt[pfx + "pg"] += 1
                mm.then_inc(sem[pfx + "pg"], 1)
        block.tensor(f_pg)

        def f_sg(e):
            for oc in range(8):
                e.wait_ge(sem[pfx + "pg"], oc + 1)
                cnt[pfx + "sg"] += 1
                e.activation(G_sb[0:40, oc * 512:(oc + 1) * 512],
                             psG[:, (oc % 2) * 512:(oc % 2 + 1) * 512],
                             AF.Copy, scale=0.5).then_inc(sem[pfx + "sg"], 1)
        block.scalar(f_sg)

        if g_store is not None:
            def f_gst(e):
                e.wait_ge(sem[pfx + "sg"], cnt[pfx + "sg"])
                e.wait_ge(sem[pfx + "gb"], 16)
                cnt[pfx + "gs"] += 16
                e.dma_start(out=g_store[:, :], in_=G_sb[:, :]).then_inc(sem[pfx + "gs"], 16)
            block.sync(f_gst)
    g_groups = cnt[pfx + "pg"]
    g_done = cnt[pfx + "sg"]
    if level <= 1:
        gscope.close()
        def f_l1(e):
            e.wait_ge(sem[pfx + "gs"], cnt[pfx + "gs"])
            e.wait_ge(sem[pfx + "eo"], cnt[pfx + "eo"])
        block.sync(f_l1)
        return [(pfx + "gs", cnt[pfx + "gs"])]

    lns, lnc = emit_ln(ctx, block, pfx + "n", x_in, xnT, ident, lnb,
                       build_xnT=True, ptrans=ptrans, wait_sync=wait_sync,
                       wait_vec=wait_vec)

    if level <= 2:
        gscope.close()
        def f_l2(e):
            e.wait_ge(sem[lns], lnc)
            e.wait_ge(sem[pfx + "gs"], cnt[pfx + "gs"])
        block.sync(f_l2)
        return [(pfx + "gs", cnt[pfx + "gs"])]

    def f_pe_eT(e):
        e.wait_ge(sem[lns], lnc)
        e.wait_ge(sem[pfx + "vb"], vb_done)
        if build_G:
            e.wait_ge(sem[pfx + "sg"], g_done)
        for tc in range(2):
            for j in range(NI):
                mm = e.matmul(psG[:, tc * 512:(tc + 1) * 512],
                              encb[:, j * 40:(j + 1) * 40],
                              xnT[:, j * TPC + tc * 512:j * TPC + (tc + 1) * 512],
                              start=(j == 0), stop=(j == NI - 1))
            cnt[pfx + "pg"] += 1
            mm.then_inc(sem[pfx + "pg"], 1)
    block.tensor(f_pe_eT)

    def f_sg_eT(e):
        for tc in range(2):
            e.wait_ge(sem[pfx + "pg"], g_groups + tc + 1)
            cnt[pfx + "sg"] += 1
            e.copy(eT[0:40, tc * 512:(tc + 1) * 512],
                   psG[:, tc * 512:(tc + 1) * 512]).then_inc(sem[pfx + "sg"], 1)
    block.scalar(f_sg_eT)
    eT_done = cnt[pfx + "sg"]

    if level <= 3:
        gscope.close()
        def f_l3(e):
            e.wait_ge(sem[pfx + "sg"], cnt[pfx + "sg"])
            e.wait_ge(sem[pfx + "gs"], cnt[pfx + "gs"])
        block.sync(f_l3)
        return [(pfx + "gs", cnt[pfx + "gs"])]

    gscope.close()

    def extras(e, ot, tc, psl):
        return e.matmul(psl, G_sb[:, ot * 128:(ot + 1) * 128],
                        eT[:, tc * 512:(tc + 1) * 512],
                        start=False, stop=True)

    bias_ready = 16 if build_G else 0
    fg = fgates or {}
    wt = [(lns, lnc), (pfx + "sg", eT_done), (pfx + "eo", 16), (pfx + "gb", bias_ready)]
    wt += fg.get("tensor", [])
    wg = [(pfx + "pg", g_groups)] if wait_gpsimd0 is None else [wait_gpsimd0]
    wg += fg.get("gpsimd", [])
    wsf = [(pfx + "vb", vb_done)] + fg.get("sync", [])
    return emit_fproj(ctx, block, pfx + "f",
                      w_dram=wbT, nko=NI, kt=NI,
                      rhs_slice=lambda i, tc: xnT[:, i * TPC + tc * 512:i * TPC + (tc + 1) * 512],
                      n_tc=2, extras=extras,
                      wait_tensor=tuple(wt),
                      evac_mode="resadd", ident=ident, ptrans=ptrans,
                      x_in_dram=x_in, x_out_dram=x_out, res_scale=0.5,
                      wait_gpsimd=wg,
                      wait_sync_first=wsf, scoped=fscoped,
                      shared_bufs=fshared, wait_scalar_first=fg.get("scalar", []))


def build_launchA(level=9):
    ctx = new_ctx("lA")
    nc, es = ctx.nc, ctx.es

    x0 = nc.dram_tensor("x0", [TPC, D], F32, kind="ExternalInput")
    wbT = nc.dram_tensor("wbT", [NI, 128, D], F32, kind="ExternalInput")
    bb = nc.dram_tensor("bb", [1, D], F32, kind="ExternalInput")
    encT = nc.dram_tensor("encT", [128, NI * 40], F32, kind="ExternalInput")
    MTc = nc.dram_tensor("MTc", [8, 128, NI * 512], BF16, kind="ExternalInput")
    c0 = nc.dram_tensor("c0", [3, 128, 1024], F32, kind="ExternalInput")
    m0 = nc.dram_tensor("m0", [3, 128, 1024], U8, kind="ExternalInput")
    c1 = nc.dram_tensor("c1", [3, 128, 256], F32, kind="ExternalInput")
    m1 = nc.dram_tensor("m1", [3, 128, 256], U8, kind="ExternalInput")
    identD = nc.dram_tensor("identD", [128, 128], BF16, kind="ExternalInput")

    x1 = nc.dram_tensor("x1", [TPC, D], F32, kind="ExternalOutput")
    xn2 = nc.dram_tensor("xn2", [TPC, D], BF16, kind="ExternalOutput")
    g_out = nc.dram_tensor("g_out", [41, D], BF16, kind="ExternalOutput")

    with ctx.es:
        block = es.enter_context(nc.Block())
        ident = ctx.sb("ident", [128, 128], BF16)
        ptrans = ctx.ps("ptrans", [128, 2048], BF16)
        G_sb = ctx.sb("G_sb", [41, D], BF16)
        lnb = LnBufs(ctx)
        ctx.new_sem("ii")

        def f_ident(e):
            ctx.cnt["ii"] += 16
            e.dma_start(out=ident[:, :], in_=identD[:, :]).then_inc(ctx.sem["ii"], 16)
        block.sync(f_ident)

        def f_identw(e):
            e.wait_ge(ctx.sem["ii"], 16)
        block.tensor(f_identw)

        fin_list = emit_l0ffn(ctx, block, "a", x_in=x0, x_out=x1, wbT=wbT, MTc=MTc,
                              c0=c0, m0=m0, c1=c1, m1=m1, encT_d=encT, bb_d=bb,
                              ident=ident, ptrans=ptrans, lnb=lnb,
                              build_G=True, G_sb=G_sb, g_store=g_out, level=level)

        if level >= 5:
            emit_ln(ctx, block, "z", x1, None, ident, lnb, build_xnT=False,
                    xn_bf_store=xn2, wait_sync=fin_list)

        def f_fin(e):
            if level >= 5:
                e.wait_ge(ctx.sem["zo0"], ctx.cnt["zo0"])
                e.wait_ge(ctx.sem["zo1"], ctx.cnt["zo1"])
            for s_, c_ in fin_list:
                e.wait_ge(ctx.sem[s_], c_)
            e.wait_ge(ctx.sem["ags"], ctx.cnt["ags"])
        block.sync(f_fin)

    return ctx.nc




def emit_l1ffn(ctx, block, pfx, *, x_in, x_out, l1wT, wqkhD, BmD, bbD,
               ident, ptrans, lnb, xnT_shared, psX, wait_sync=None,
               wait_vec=None, fshared=None, fgates=None):
    nc = ctx.nc
    sem, cnt = ctx.sem, ctx.cnt
    xnT = xnT_shared
    wqkh = ctx.sb(pfx + "_wqkh", [128, NI * 640], BF16)
    Bm_sb = ctx.sb(pfx + "_Bm", [128, D], BF16)
    bbP = ctx.sb(pfx + "_bbP", [128, NI], F32)
    qkh_f = ctx.sb(pfx + "_qkh", [128, 640], F32)
    s8 = ctx.sb(pfx + "_s8", [128, 16], F32)
    w8 = ctx.sb(pfx + "_w8", [128, 8], F32)
    wh_f = ctx.sb(pfx + "_whf", [128, 128], F32)
    wh_b = ctx.sb(pfx + "_whb", [128, 128], BF16)
    whT = ctx.sb(pfx + "_whT", [128, NT * 128], BF16)
    psQ = psX[0:128, 0:640]
    fg = fgates or {}
    for s_ in ("ld", "qp", "qs", "qv"):
        ctx.new_sem(pfx + s_)

    def f_ld(e):
        for ws_, wc_ in fg.get("l1ld", []):
            e.wait_ge(sem[ws_], wc_)
        for (o, i_) in ((wqkh[:, :], wqkhD[:, :]), (Bm_sb[:, :], BmD[:, :]),
                        (bbP[:, :], bbD[:, :])):
            cnt[pfx + "ld"] += 16
            e.dma_start(out=o, in_=i_).then_inc(sem[pfx + "ld"], 16)
    block.gpsimd(f_ld)

    lns, lnc = emit_ln(ctx, block, pfx + "n", x_in, xnT, ident, lnb,
                       build_xnT=True, ptrans=ptrans, wait_sync=wait_sync,
                       wait_vec=wait_vec, wait_pe=(fgates or {}).get("lnpe"))

    bqp, bqs, bqv = cnt[pfx + "qp"], cnt[pfx + "qs"], cnt[pfx + "qv"]
    grp = [bqp + 2 * t + 1 for t in range(NT)]
    trq = [bqp + 2 * t + 2 for t in range(NT)]
    ev = [bqs + 3 * t + 1 for t in range(NT)]
    ex = [bqs + 3 * t + 2 for t in range(NT)]
    wc = [bqs + 3 * t + 3 for t in range(NT)]
    vneg = [bqv + 7 * t + 4 for t in range(NT)]
    vwh = [bqv + 7 * t + 7 for t in range(NT)]

    def f_pe(e):
        e.wait_ge(sem[lns], lnc)
        e.wait_ge(sem[pfx + "ld"], 16)
        for ws_, wc_ in fg.get("l1pe", []):
            e.wait_ge(sem[ws_], wc_)
        for tt in range(NT):
            if tt >= 1:
                e.wait_ge(sem[pfx + "qs"], ev[tt - 1])
            for i in range(NI):
                e.matmul(psQ[:, 0:512],
                         xnT[:, i * TPC + tt * 128:i * TPC + (tt + 1) * 128],
                         wqkh[:, i * 640:i * 640 + 512],
                         start=(i == 0), stop=(i == NI - 1))
                mm = e.matmul(psQ[:, 512:640],
                              xnT[:, i * TPC + tt * 128:i * TPC + (tt + 1) * 128],
                              wqkh[:, i * 640 + 512:(i + 1) * 640],
                              start=(i == 0), stop=(i == NI - 1))
            cnt[pfx + "qp"] += 1
            mm.then_inc(sem[pfx + "qp"], 1)
            e.wait_ge(sem[pfx + "qv"], vwh[tt])
            if tt >= 2:
                e.wait_ge(sem[pfx + "qs"], wc[tt - 2])
            cnt[pfx + "qp"] += 1
            e.transpose(ptrans[:, (tt % 2) * 1024:(tt % 2) * 1024 + 128],
                        wh_b[:, :], ident[:, :]).then_inc(sem[pfx + "qp"], 1)
    block.tensor(f_pe)

    def f_sc(e):
        for tt in range(NT):
            e.wait_ge(sem[pfx + "qp"], grp[tt])
            if tt >= 1:
                e.wait_ge(sem[pfx + "qv"], vwh[tt - 1])
            cnt[pfx + "qs"] += 1
            e.copy(qkh_f[:, :], psQ[:, :]).then_inc(sem[pfx + "qs"], 1)
            e.wait_ge(sem[pfx + "qv"], vneg[tt])
            cnt[pfx + "qs"] += 1
            e.activation(w8[:, :], s8[:, 0:8], AF.Exp, bias=s8[:, 8:9],
                         accum_out=s8[:, 9:10]).then_inc(sem[pfx + "qs"], 1)
            e.wait_ge(sem[pfx + "qp"], trq[tt])
            cnt[pfx + "qs"] += 1
            e.copy(whT[:, tt * 128:(tt + 1) * 128],
                   ptrans[:, (tt % 2) * 1024:(tt % 2) * 1024 + 128]
                   ).then_inc(sem[pfx + "qs"], 1)
    block.scalar(f_sc)

    ISQ32 = 1.0 / math.sqrt(32.0)

    def f_ve(e):
        for tt in range(NT):
            e.wait_ge(sem[pfx + "qs"], ev[tt])
            ctx.serm(e.tensor_mul(qkh_f[:, 0:256], qkh_f[:, 0:256], qkh_f[:, 256:512]),
                     pfx + "qv")
            ctx.serw(e, pfx + "qv")
            ctx.serm(e.reduce_sum(s8[:, 0:8],
                                  qkh_f[:, 0:256].rearrange("p (e r) -> p e r", e=8),
                                  axis=AX.X), pfx + "qv")
            ctx.serw(e, pfx + "qv")
            ctx.serm(e.tensor_scalar(out=s8[:, 0:8], in0=s8[:, 0:8], scalar1=ISQ32,
                                     scalar2=None, op0=ALU.mult), pfx + "qv")
            ctx.serw(e, pfx + "qv")
            ctx.serm(e.reduce_max(s8[:, 8:9], s8[:, 0:8], axis=AX.X, negate=True),
                     pfx + "qv")
            e.wait_ge(sem[pfx + "qs"], ex[tt])
            ctx.serm(e.reciprocal(s8[:, 10:11], s8[:, 9:10]), pfx + "qv")
            ctx.serw(e, pfx + "qv")
            ctx.serm(e.tensor_mul(wh_f.ap().rearrange("p (e r) -> p e r", e=8),
                                  qkh_f[:, 512:640].rearrange("p (e r) -> p e r", e=8),
                                  w8.ap().unsqueeze(2).broadcast_to([128, 8, 16])),
                     pfx + "qv")
            ctx.serw(e, pfx + "qv")
            ctx.serm(e.tensor_scalar(out=wh_b[:, :], in0=wh_f[:, :],
                                     scalar1=s8[:, 10:11], scalar2=None,
                                     op0=ALU.mult), pfx + "qv")
            assert cnt[pfx + "qv"] == vwh[tt]
    block.vector(f_ve)

    def extras(e, ot, tc, psl):
        return e.matmul(psl, Bm_sb[:, ot * 128:(ot + 1) * 128],
                        whT[:, tc * 512:(tc + 1) * 512], start=False, stop=True)

    wt = [(lns, lnc), (pfx + "qs", wc[NT - 1]), (pfx + "ld", 48)]
    wt += fg.get("tensor", [])
    return emit_fproj(ctx, block, pfx + "f",
                      w_dram=l1wT, nko=NI, kt=NI,
                      rhs_slice=lambda i, tc: xnT[:, i * TPC + tc * 512:i * TPC + (tc + 1) * 512],
                      n_tc=2, extras=extras, wait_tensor=tuple(wt),
                      evac_mode="resadd", ident=ident, ptrans=ptrans,
                      x_in_dram=x_in, x_out_dram=x_out, res_scale=0.5,
                      wait_gpsimd=fg.get("gpsimd", []),
                      wait_sync_first=fg.get("sync", []),
                      wait_scalar_first=list(fg.get("scalar", []))
                      + [(pfx + "ld", 48)],
                      evac_bias=lambda ot: bbP[:, ot:ot + 1],
                      shared_bufs=fshared)


def build_launchC():
    ctx = new_ctx("lC")
    nc, es = ctx.nc, ctx.es
    sem, cnt = ctx.sem, ctx.cnt

    x1 = nc.dram_tensor("x1", [TPC, D], F32, kind="ExternalInput")
    attnT = nc.dram_tensor("attnT", [128, NI * TPC], BF16, kind="ExternalInput")
    woT = nc.dram_tensor("woT", [NI, 128, D], F32, kind="ExternalInput")
    g_in = nc.dram_tensor("g_in", [41, D], BF16, kind="ExternalInput")
    wbT = nc.dram_tensor("wbT", [NI, 128, D], F32, kind="ExternalInput")
    encT = nc.dram_tensor("encT", [128, NI * 40], F32, kind="ExternalInput")
    l1wT = nc.dram_tensor("l1wT", [NI, 128, D], F32, kind="ExternalInput")
    wqkhD = nc.dram_tensor("wqkhD", [128, NI * 640], F32, kind="ExternalInput")
    BmD = nc.dram_tensor("BmD", [128, D], F32, kind="ExternalInput")
    bbD = nc.dram_tensor("bbD", [128, NI], F32, kind="ExternalInput")
    identD = nc.dram_tensor("identD", [128, 128], BF16, kind="ExternalInput")

    x4 = nc.dram_tensor("x4", [TPC, D], F32, kind="ExternalOutput")
    xn5 = nc.dram_tensor("xn5", [TPC, D], BF16, kind="ExternalOutput")
    x2 = nc.dram_tensor("x2", [TPC, D], F32)
    x3 = nc.dram_tensor("x3", [TPC, D], F32)

    with ctx.es:
        block = es.enter_context(nc.Block())
        ident = ctx.sb("ident", [128, 128], BF16)
        ptrans = ctx.ps("ptrans", [128, 2048], BF16)
        G_sb = ctx.sb("G_sb", [41, D], BF16)
        rhs_sb = ctx.sb("rhs_sb", [128, NI * TPC], BF16)
        lnb = LnBufs(ctx, xa2=False)
        psX = ctx.ps("psX", [128, 1024], F32)
        fw = ctx.sb("fw", [128, 2 * D], BF16)
        fps = ctx.ps("fps", [128, 2 * 1024], F32)
        fobf = ctx.sb("fobf", [128, 2 * 1024], BF16)
        fxsl = ctx.sb("fxsl", [128, 3 * 8 * 128], F32)
        fshared = (fw, fps, fobf, fxsl)
        for s_ in ("ii", "ax"):
            ctx.new_sem(s_)

        def f_ii(e):
            cnt["ii"] += 16
            e.dma_start(out=ident[:, :], in_=identD[:, :]).then_inc(sem["ii"], 16)
            cnt["ax"] += 16
            e.dma_start(out=rhs_sb[:, :], in_=attnT[:, :]).then_inc(sem["ax"], 16)
        block.sync(f_ii)

        def f_iiw(e):
            e.wait_ge(sem["ii"], 16)
        block.tensor(f_iiw)

        st1 = emit_fproj(ctx, block, "o",
                         w_dram=woT, nko=NI, kt=NI,
                         rhs_slice=lambda i, tc: rhs_sb[:, i * TPC + tc * 512:
                                                        i * TPC + (tc + 1) * 512],
                         n_tc=2, extras=None,
                         wait_tensor=(("ax", 16),),
                         evac_mode="resadd", ident=ident, ptrans=ptrans,
                         x_in_dram=x1, x_out_dram=x2, res_scale=1.0,
                         shared_bufs=fshared)

        st2 = emit_l0ffn(ctx, block, "c", x_in=x2, x_out=x3, wbT=wbT, MTc=None,
                         c0=None, m0=None, c1=None, m1=None, encT_d=encT,
                         bb_d=None, ident=ident, ptrans=ptrans, lnb=lnb,
                         build_G=False, G_sb=G_sb, g_dram_in=g_in,
                         wait_sync=st1, xnT_shared=rhs_sb, psX=psX,
                         wait_vec=[("op", NI)],
                         fshared=fshared,
                         fgates={"tensor": [("oc", NI)],
                                 "gpsimd": [("op", NI)],
                                 "sync": st1,
                                 "scalar": [("ot", NI * NT)],
                                 "lnpe": [("ov", NI * NT)]})

        st3 = emit_l1ffn(ctx, block, "d", x_in=x3, x_out=x4, l1wT=l1wT,
                         wqkhD=wqkhD, BmD=BmD, bbD=bbD, ident=ident,
                         ptrans=ptrans, lnb=lnb, xnT_shared=rhs_sb, psX=psX,
                         wait_sync=st2, wait_vec=[("cfp", NI)],
                         fshared=fshared,
                         fgates={"tensor": [("cfc", NI)],
                                 "gpsimd": [("cfp", NI)],
                                 "sync": st2,
                                 "scalar": [("cft", NI * NT)],
                                 "lnpe": [("cfv", NI * NT)]})

        emit_ln(ctx, block, "z", x4, None, ident, lnb, build_xnT=False,
                xn_bf_store=xn5, wait_sync=st3)

        def f_fin(e):
            e.wait_ge(sem["zo0"], cnt["zo0"])
            e.wait_ge(sem["zo1"], cnt["zo1"])
            for s_, c_ in st3:
                e.wait_ge(sem[s_], c_)
        block.sync(f_fin)
    return nc

def build_launchB():
    ctx = new_ctx("lB")
    nc, es = ctx.nc, ctx.es
    sem, cnt = ctx.sem, ctx.cnt

    xnT_g = nc.dram_tensor("xnT_g", [4, 128, NI * 1024], BF16, kind="ExternalInput")
    wqkv = nc.dram_tensor("wqkv", [24, 128, D], F32, kind="ExternalInput")
    cosD = nc.dram_tensor("cosD", [128, S], F32, kind="ExternalInput")
    sinD = nc.dram_tensor("sinD", [128, S], F32, kind="ExternalInput")
    identD = nc.dram_tensor("identD", [128, 128], BF16, kind="ExternalInput")
    triD = nc.dram_tensor("triD", [128, 2048], BF16, kind="ExternalInput")
    OT = nc.dram_tensor("OT", [1024, S], BF16, kind="ExternalOutput")
    qkvT = nc.dram_tensor("qkvT", [3072, S], BF16)

    NQ, TQ = 4, 1024

    with ctx.es:
        block = es.enter_context(nc.Block())
        ident = ctx.sb("ident", [128, 128], BF16)
        cosb = ctx.sb("cosb", [128, S], F32)
        sinb = ctx.sb("sinb", [128, S], F32)
        triq4 = ctx.sb("triq4", [128, 4 * 512], BF16)
        ones128 = ctx.sb("ones128", [128, 1], BF16)
        onesF = ctx.sb("onesF", [128, 128], BF16)
        ones1 = ctx.sb("ones1", [1, 128], BF16)
        for s_ in ("ii", "vi"):
            ctx.new_sem(s_)

        def f_consts(e):
            for o, i_ in ((ident, identD), (cosb, cosD), (sinb, sinD),
                          (triq4, triD)):
                cnt["ii"] += 16
                e.dma_start(out=o[:, :], in_=i_[:, :]).then_inc(sem["ii"], 16)
        block.sync(f_consts)

        def f_vinit(e):
            ctx.serm(e.memset(ones128[:, :], 1.0), "vi")
            ctx.serm(e.memset(ones1[:, :], 1.0), "vi")
            ctx.serm(e.memset(onesF[:, :], 1.0), "vi")
        block.vector(f_vinit)
        vi_done = cnt["vi"]

        pscope = ExitStack()
        xnq = pscope.enter_context(nc.sbuf_tensor("xnq", [128, NI * TQ], BF16))
        w_sb = pscope.enter_context(nc.sbuf_tensor("pw", [128, 2 * D], BF16))
        obf = pscope.enter_context(nc.sbuf_tensor("pobf", [128, 2 * TQ], BF16))
        psp = pscope.enter_context(nc.psum_tensor("psp", [128, 2 * TQ], F32))
        for s_ in ("qx", "qw0", "qw1", "qp", "qc", "qo0", "qo1"):
            ctx.new_sem(s_)
        NG = NQ * 24



        def f_qw(e):
            for g in range(NG):
                ot = g % 24
                if g >= 2:
                    e.wait_ge(sem["qc"], g - 1)
                swp = "qw0" if g % 2 == 0 else "qw1"
                cnt[swp] += 16
                e.dma_start(out=w_sb[:, (g % 2) * D:(g % 2 + 1) * D],
                            in_=wqkv.ap()[ot]).then_inc(sem[swp], 16)
        block.gpsimd(f_qw)

        def f_qt(e):
            e.wait_ge(sem["ii"], 64)
            for g in range(NG):
                tq = g // 24
                e.wait_ge(sem["qx"], 16 * (tq + 1))
                e.wait_ge(sem["qw0" if g % 2 == 0 else "qw1"], 16 * (g // 2 + 1))
                if g >= 2:
                    e.wait_ge(sem["qc"], g - 1)
                for tc in range(2):
                    for i in range(NI):
                        mm = e.matmul(psp[:, (g % 2) * TQ + tc * 512:
                                          (g % 2) * TQ + (tc + 1) * 512],
                                      w_sb[:, (g % 2) * D + i * 128:
                                           (g % 2) * D + (i + 1) * 128],
                                      xnq[:, i * TQ + tc * 512:i * TQ + (tc + 1) * 512],
                                      start=(i == 0), stop=(i == NI - 1))
                cnt["qp"] += 1
                mm.then_inc(sem["qp"], 1)
        block.tensor(f_qt)

        def f_qc(e):
            for g in range(NG):
                e.wait_ge(sem["qp"], g + 1)
                if g >= 2:
                    e.wait_ge(sem["qo0" if g % 2 == 0 else "qo1"],
                              16 * ((g - 2) // 2 + 1))
                cnt["qc"] += 1
                e.copy(obf[:, (g % 2) * TQ:(g % 2 + 1) * TQ],
                       psp[:, (g % 2) * TQ:(g % 2 + 1) * TQ]).then_inc(sem["qc"], 1)
        block.scalar(f_qc)

        def _qstore(e, g):
            tq, ot = g // 24, g % 24
            e.wait_ge(sem["qc"], g + 1)
            sop = "qo0" if g % 2 == 0 else "qo1"
            cnt[sop] += 16
            e.dma_start(out=qkvT[ot * 128:(ot + 1) * 128, tq * TQ:(tq + 1) * TQ],
                        in_=obf[:, (g % 2) * TQ:(g % 2 + 1) * TQ]
                        ).then_inc(sem[sop], 16)

        def f_qxs(e):
            for g in range(NG):
                if g % 24 == 0:
                    tq = g // 24
                    if tq >= 1:
                        e.wait_ge(sem["qp"], tq * 24)
                    cnt["qx"] += 16
                    e.dma_start(out=xnq[:, :], in_=xnT_g.ap()[tq]
                                ).then_inc(sem["qx"], 16)
                if g >= 1:
                    _qstore(e, g - 1)
            _qstore(e, NG - 1)
        block.sync(f_qxs)
        proj_done = [("qo0", cnt["qo0"]), ("qo1", cnt["qo1"])]
        pscope.close()

        vT_all = ctx.sb("vT_all", [128, 8 * S], BF16)
        vscope = ExitStack()
        vload = vscope.enter_context(nc.sbuf_tensor("vload", [128, 2 * S], BF16))
        ptr2 = vscope.enter_context(nc.psum_tensor("ptr2", [128, 2048], BF16))
        for s_ in ("vx0", "vx1", "vt", "vc"):
            ctx.new_sem(s_)

        def f_vx(e):
            for s_, c_ in proj_done:
                e.wait_ge(sem[s_], c_)
            for hh in range(HPC):
                if hh >= 2:
                    e.wait_ge(sem["vt"], 32 * (hh - 1))
                sx = "vx0" if hh % 2 == 0 else "vx1"
                cnt[sx] += 16
                e.dma_start(out=vload[:, (hh % 2) * S:(hh % 2 + 1) * S],
                            in_=qkvT[(16 + hh) * 128:(17 + hh) * 128, :]
                            ).then_inc(sem[sx], 16)
        block.sync(f_vx)

        def f_vt(e):
            e.wait_ge(sem["ii"], 64)
            for k in range(32 * HPC):
                hh, kt = k // 32, k % 32
                if kt == 0:
                    e.wait_ge(sem["vx0" if hh % 2 == 0 else "vx1"],
                              16 * (hh // 2 + 1))
                if k >= 2:
                    e.wait_ge(sem["vc"], k - 1)
                cnt["vt"] += 1
                e.transpose(ptr2[:, (k % 2) * 1024:(k % 2) * 1024 + 128],
                            vload[:, (hh % 2) * S + kt * 128:
                                  (hh % 2) * S + (kt + 1) * 128],
                            ident[:, :]).then_inc(sem["vt"], 1)
        block.tensor(f_vt)

        def f_vc(e):
            for k in range(32 * HPC):
                e.wait_ge(sem["vt"], k + 1)
                cnt["vc"] += 1
                e.copy(vT_all[:, k * 128:(k + 1) * 128],
                       ptr2[:, (k % 2) * 1024:(k % 2) * 1024 + 128]
                       ).then_inc(sem["vc"], 1)
        block.scalar(f_vc)
        vc_total = cnt["vc"]
        vt_total = cnt["vt"]
        vscope.close()

        bufs = dict(
            qraw=ctx.sb("qraw", [128, S], BF16),
            kraw=ctx.sb("kraw", [128, S], BF16),
            qr=ctx.sb("qr", [128, S], BF16),
            kr=ctx.sb("kr", [128, S], BF16),
            qsh=ctx.sb("qsh", [128, S], BF16),
            ksh=ctx.sb("ksh", [128, S], BF16),
            P_sb=ctx.sb("P_sb", [128, S], BF16),
            o_sb=ctx.sb("o_sb", [128, 2 * 512], BF16),
            rb_f=ctx.sb("rb_f", [128, 2 * 512], F32),
            rb_t=ctx.sb("rb_t", [128, 512], F32),
            triq4=triq4,
            nrm=ctx.sb("nrm", [1, 24], F32),
            nbias=ctx.sb("nbias", [128, 1], F32),
            nbb=ctx.sb("nbb", [1, 1], BF16),
            psS=ctx.ps("psS", [128, 4 * 512], F32),
            psO=ctx.ps("psO", [128, 2 * 512], F32),
            psNB=ctx.ps("psNB", [128, 2 * 512], F32),
        )
        for s_ in ("hx", "hv", "hs", "hp", "ho0", "ho1"):
            ctx.new_sem(s_)

        for h in range(HPC):
            _emit_B_head(ctx, block, h, qkvT, OT, bufs, cosb, sinb, ident,
                         ones128, onesF, ones1, vT_all,
                         (proj_done + [("vt", vt_total)]) if h == 0 else None,
                         vi_done, vc_total)

        def f_fin(e):
            e.wait_ge(sem["ho0"], cnt["ho0"])
            e.wait_ge(sem["ho1"], cnt["ho1"])
        block.sync(f_fin)
    return nc


def _emit_B_head(ctx, block, h, qkvT, OT, B, cosb, sinb, ident,
                 ones128, onesF, ones1, vT, first_wait, vi_done, vc_total):
    sem, cnt = ctx.sem, ctx.cnt
    bhp, bhs, bhv, bhx = cnt["hp"], cnt["hs"], cnt["hv"], cnt["hx"]
    bho0, bho1 = cnt["ho0"], cnt["ho1"]

    tiles = []
    for qb in range(8):
        for t in range(4 * (qb + 1)):
            tiles.append((qb, t))
    NTILE = len(tiles)

    hp = hs = hv = 0
    pos = {}
    hv += 3; pos["ropeq"] = bhv + hv
    hv += 3; pos["ropek"] = bhv + hv
    hv += 1; pos["qsq"] = bhv + hv
    bm = [0] * 16; bred = [0] * 16
    for b in range(8):
        hp += 1; bm[b] = bhp + hp
        hv += 1; bred[b] = bhv + hv
    hv += 1; pos["qmax"] = bhv + hv
    hv += 1; pos["ksq"] = bhv + hv
    for b in range(8, 16):
        hp += 1; bm[b] = bhp + hp
        hv += 1; bred[b] = bhv + hv
    hv += 1; pos["kmax"] = bhv + hv
    hv += 1; pos["prod"] = bhv + hv
    hs += 1; pos["sqb"] = bhs + hs
    hv += 1; pos["neg"] = bhv + hv
    hv += 1; pos["nbbc"] = bhv + hv
    hp += 1; pos["bcast"] = bhp + hp
    hs += 1; pos["cbias"] = bhs + hs
    smm = [0] * NTILE; summ = [0] * NTILE; avmm = [0] * NTILE
    pe_order = []
    for g in range(min(3, NTILE)):
        pe_order.append(("S", g))
    for g in range(NTILE):
        if g + 3 < NTILE:
            pe_order.append(("S", g + 3))
        pe_order.append(("sum", g))
        pe_order.append(("av", g))
    for kind, g in pe_order:
        hp += 1
        if kind == "S":
            smm[g] = bhp + hp
        elif kind == "sum":
            summ[g] = bhp + hp
        else:
            avmm[g] = bhp + hp
    expp = [0] * NTILE; rcp = [0] * 8; onor = [0] * 8; odiv = [0] * 8
    for g, (qb, t) in enumerate(tiles):
        hs += 1; expp[g] = bhs + hs
        if t == 4 * (qb + 1) - 1:
            hs += 2; rcp[qb] = bhs + hs
            hs += 1; onor[qb] = bhs + hs
    for qb in range(8):
        hv += 1; odiv[qb] = bhv + hv

    def _loads(e, hh):
        for r, t in ((hh, "qraw"), (8 + hh, "kraw")):
            cnt["hx"] += 16
            e.dma_start(out=B[t][:, :], in_=qkvT[r * 128:(r + 1) * 128, :]
                        ).then_inc(sem["hx"], 16)
        e.wait_ge(sem["hx"], cnt["hx"])
        for (dst, srcb) in ((B["qsh"], B["qraw"]), (B["ksh"], B["kraw"])):
            cnt["hx"] += 16
            e.dma_start(out=dst[0:64, :], in_=srcb[64:128, :]).then_inc(sem["hx"], 16)
            cnt["hx"] += 16
            e.dma_start(out=dst[64:128, :], in_=srcb[0:64, :]).then_inc(sem["hx"], 16)

    def f_hx(e):
        if first_wait:
            for s_, c_ in first_wait:
                e.wait_ge(sem[s_], c_)
        _loads(e, h)
    if h == 0:
        block.sync(f_hx)
    loads_done = cnt["hx"]

    def f_hv(e):
        e.wait_ge(sem["hx"], loads_done)
        if h == 0:
            e.wait_ge(sem["vi"], vi_done)
            e.wait_ge(sem["ii"], 64)
        for (dst, srcb, shb) in ((B["qr"], B["qraw"], B["qsh"]),
                                 (B["kr"], B["kraw"], B["ksh"])):
            ctx.serm(e.tensor_mul(dst[:, :], srcb[:, :], cosb[:, :]), "hv")
            ctx.serm(e.tensor_mul(B["P_sb"][:, 0:S], shb[:, :], sinb[:, :]), "hv")
            ctx.serw(e, "hv")
            ctx.serm(e.tensor_add(dst[:, :], dst[:, :], B["P_sb"][:, 0:S]), "hv")
            ctx.serw(e, "hv")
        ctx.serm(e.tensor_mul(B["P_sb"][:, 0:S], B["qr"][:, :], B["qr"][:, :]), "hv")
        for b in range(16):
            e.wait_ge(sem["hp"], bm[b])
            cnt["hv"] += 1
            e.reduce_max(B["nrm"][0:1, b:b + 1],
                         B["psO"][0:1, (b % 2) * 512:(b % 2) * 512 + 512],
                         axis=AX.X).then_inc(sem["hv"], 1)
            if b == 7:
                ctx.serw(e, "hv")
                ctx.serm(e.reduce_max(B["nrm"][0:1, 16:17], B["nrm"][0:1, 0:8],
                                      axis=AX.X), "hv")
                ctx.serw(e, "hv")
                ctx.serm(e.tensor_mul(B["P_sb"][:, 0:S], B["kr"][:, :],
                                      B["kr"][:, :]), "hv")
        ctx.serw(e, "hv")
        ctx.serm(e.reduce_max(B["nrm"][0:1, 17:18], B["nrm"][0:1, 8:16],
                              axis=AX.X), "hv")
        ctx.serw(e, "hv")
        ctx.serm(e.tensor_mul(B["nrm"][0:1, 18:19], B["nrm"][0:1, 16:17],
                              B["nrm"][0:1, 17:18]), "hv")
        e.wait_ge(sem["hs"], pos["sqb"])
        ctx.serm(e.tensor_scalar(out=B["nrm"][0:1, 20:21], in0=B["nrm"][0:1, 19:20],
                                 scalar1=-1.0, scalar2=None, op0=ALU.mult), "hv")
        ctx.serw(e, "hv")
        ctx.serm(e.tensor_copy(B["nbb"][0:1, 0:1], B["nrm"][0:1, 20:21]), "hv")
        assert cnt["hv"] == pos["nbbc"]
        for qb in range(8):
            e.wait_ge(sem["hs"], onor[qb])
            e.wait_ge(sem["hs"], rcp[qb])
            cnt["hv"] += 1
            e.tensor_mul(B["o_sb"][:, (qb % 2) * 512:(qb % 2 + 1) * 512],
                         B["o_sb"][:, (qb % 2) * 512:(qb % 2 + 1) * 512],
                         B["rb_f"][:, (qb % 2) * 512:(qb % 2 + 1) * 512]
                         ).then_inc(sem["hv"], 1)
            assert cnt["hv"] == odiv[qb]
    block.vector(f_hv)

    def f_ht(e):
        if h == 0:
            e.wait_ge(sem["vc"], vc_total)
        else:
            e.wait_ge(sem["hs"], bhs)
        e.wait_ge(sem["hv"], pos["qsq"])
        for b in range(16):
            if b == 8:
                e.wait_ge(sem["hv"], pos["ksq"])
            if b >= 2:
                e.wait_ge(sem["hv"], bred[b - 2])
            cnt["hp"] += 1
            e.matmul(B["psO"][0:1, (b % 2) * 512:(b % 2) * 512 + 512],
                     ones128[:, 0:1],
                     B["P_sb"][:, (b % 8) * 512:(b % 8 + 1) * 512],
                     start=True, stop=True).then_inc(sem["hp"], 1)
            assert cnt["hp"] == bm[b]
        e.wait_ge(sem["hv"], pos["nbbc"])
        cnt["hp"] += 1
        e.matmul(B["psNB"][:, 0:1], ones1[0:1, :], B["nbb"][0:1, 0:1],
                 start=True, stop=True).then_inc(sem["hp"], 1)
        assert cnt["hp"] == pos["bcast"]

        def _emit_S(g):
            qb, t = tiles[g]
            if g >= 4:
                e.wait_ge(sem["hs"], expp[g - 4])
            masked = (t >= 4 * qb)
            mm = e.matmul(B["psS"][:, (g % 4) * 512:(g % 4 + 1) * 512],
                          B["kr"][:, t * 128:(t + 1) * 128],
                          B["qr"][:, qb * 512:(qb + 1) * 512],
                          start=True, stop=not masked)
            if masked:
                s_ = t - 4 * qb
                mm = e.matmul(B["psS"][:, (g % 4) * 512:(g % 4 + 1) * 512],
                              ident[:, :], B["triq4"][:, s_ * 512:(s_ + 1) * 512],
                              start=False, stop=True)
            cnt["hp"] += 1
            mm.then_inc(sem["hp"], 1)
            assert cnt["hp"] == smm[g]

        for g in range(min(3, NTILE)):
            _emit_S(g)
        for g in range(NTILE):
            qb, t = tiles[g]
            nkt = 4 * (qb + 1)
            if g + 3 < NTILE:
                _emit_S(g + 3)
            e.wait_ge(sem["hs"], expp[g])
            if t == 0:
                if qb == 0:
                    e.wait_ge(sem["hs"], pos["cbias"])
                if qb >= 2:
                    e.wait_ge(sem["hs"], rcp[qb - 2])
            cnt["hp"] += 1
            e.matmul(B["psNB"][:, (qb % 2) * 512:(qb % 2 + 1) * 512],
                     onesF[:, :],
                     B["P_sb"][:, (g % 4) * 512:(g % 4 + 1) * 512],
                     start=(t == 0), stop=(t == nkt - 1)).then_inc(sem["hp"], 1)
            assert cnt["hp"] == summ[g]
            if t == 0 and qb >= 2:
                e.wait_ge(sem["hs"], onor[qb - 2])
            cnt["hp"] += 1
            e.matmul(B["psO"][:, (qb % 2) * 512:(qb % 2 + 1) * 512],
                     vT[:, h * S + t * 128:h * S + (t + 1) * 128],
                     B["P_sb"][:, (g % 4) * 512:(g % 4 + 1) * 512],
                     start=(t == 0), stop=(t == nkt - 1)).then_inc(sem["hp"], 1)
            assert cnt["hp"] == avmm[g]
    block.tensor(f_ht)

    def f_hs(e):
        e.wait_ge(sem["hv"], pos["prod"])
        cnt["hs"] += 1
        e.activation(B["nrm"][0:1, 19:20], B["nrm"][0:1, 18:19], AF.Sqrt,
                     scale=1.0 / 128.0).then_inc(sem["hs"], 1)
        assert cnt["hs"] == pos["sqb"]
        e.wait_ge(sem["hp"], pos["bcast"])
        cnt["hs"] += 1
        e.copy(B["nbias"][:, :], B["psNB"][:, 0:1]).then_inc(sem["hs"], 1)
        assert cnt["hs"] == pos["cbias"]
        for g, (qb, t) in enumerate(tiles):
            e.wait_ge(sem["hp"], smm[g])
            if g == 0:
                e.wait_ge(sem["hs"], pos["cbias"])
            if g >= 4:
                e.wait_ge(sem["hp"], avmm[g - 4])
            cnt["hs"] += 1
            e.activation(B["P_sb"][:, (g % 4) * 512:(g % 4 + 1) * 512],
                         B["psS"][:, (g % 4) * 512:(g % 4 + 1) * 512],
                         AF.Exp, bias=B["nbias"][:, 0:1], scale=ISQ
                         ).then_inc(sem["hs"], 1)
            assert cnt["hs"] == expp[g]
            if t == 4 * (qb + 1) - 1:
                e.wait_ge(sem["hp"], summ[g])
                if qb >= 2:
                    e.wait_ge(sem["hv"], odiv[qb - 2])
                cnt["hs"] += 1
                e.activation(B["rb_t"][:, :],
                             B["psNB"][:, (qb % 2) * 512:(qb % 2 + 1) * 512],
                             AF.Ln).then_inc(sem["hs"], 1)
                cnt["hs"] += 1
                e.wait_ge(sem["hs"], rcp[qb] - 1)
                e.activation(B["rb_f"][:, (qb % 2) * 512:(qb % 2 + 1) * 512],
                             B["rb_t"][:, :], AF.Exp,
                             scale=-1.0).then_inc(sem["hs"], 1)
                assert cnt["hs"] == rcp[qb]
                e.wait_ge(sem["hp"], avmm[g])
                if qb >= 2:
                    hosem = "ho0" if qb % 2 == 0 else "ho1"
                    hob = bho0 if qb % 2 == 0 else bho1
                    e.wait_ge(sem[hosem], hob + 16 * ((qb - 2) // 2 + 1))
                elif h > 0:
                    e.wait_ge(sem["ho0" if qb % 2 == 0 else "ho1"],
                              bho0 if qb % 2 == 0 else bho1)
                cnt["hs"] += 1
                e.copy(B["o_sb"][:, (qb % 2) * 512:(qb % 2 + 1) * 512],
                       B["psO"][:, (qb % 2) * 512:(qb % 2 + 1) * 512]
                       ).then_inc(sem["hs"], 1)
                assert cnt["hs"] == onor[qb]
    block.scalar(f_hs)

    def f_ho(e):
        if h + 1 < HPC:
            e.wait_ge(sem["hv"], pos["ropek"])
            _loads(e, h + 1)
        for qb in range(8):
            e.wait_ge(sem["hv"], odiv[qb])
            hop = "ho0" if qb % 2 == 0 else "ho1"
            cnt[hop] += 16
            e.dma_start(out=OT[h * 128:(h + 1) * 128, qb * 512:(qb + 1) * 512],
                        in_=B["o_sb"][:, (qb % 2) * 512:(qb % 2 + 1) * 512]
                        ).then_inc(sem[hop], 16)
    block.sync(f_ho)


def build_launchD():
    ctx = new_ctx("lD")
    nc, es = ctx.nc, ctx.es
    sem, cnt = ctx.sem, ctx.cnt

    xnT_g = nc.dram_tensor("xnT_g", [4, 128, NI * 1024], BF16, kind="ExternalInput")
    wqkv12 = nc.dram_tensor("wqkv12", [24, 128, D], F32, kind="ExternalInput")
    qdecD = nc.dram_tensor("qdecD", [8, 128, 256], F32, kind="ExternalInput")
    kdecD = nc.dram_tensor("kdecD", [128, 16], F32, kind="ExternalInput")
    DdecD = nc.dram_tensor("DdecD", [8, 128, 512], BF16, kind="ExternalInput")
    bdBD = nc.dram_tensor("bdBD", [128, 8], F32, kind="ExternalInput")
    identD = nc.dram_tensor("identD", [128, 128], BF16, kind="ExternalInput")
    OT = nc.dram_tensor("OT", [1024, S], BF16, kind="ExternalOutput")
    qkvT = nc.dram_tensor("qkvT", [3072, S], BF16)

    NQ, TQ = 4, 1024
    NB = 16

    with ctx.es:
        block = es.enter_context(nc.Block())
        ident = ctx.sb("ident", [128, 128], BF16)
        kdec = ctx.sb("kdec", [128, 16], F32)
        bdB = ctx.sb("bdB", [128, 8], F32)
        ctx.new_sem("ii")

        def f_consts(e):
            for o, i_ in ((ident, identD), (kdec, kdecD), (bdB, bdBD)):
                cnt["ii"] += 16
                e.dma_start(out=o[:, :], in_=i_[:, :]).then_inc(sem["ii"], 16)
        block.sync(f_consts)

        pscope = ExitStack()
        xnq = pscope.enter_context(nc.sbuf_tensor("xnq", [128, NI * TQ], BF16))
        w_sb = pscope.enter_context(nc.sbuf_tensor("pw", [128, 2 * D], BF16))
        obf = pscope.enter_context(nc.sbuf_tensor("pobf", [128, 2 * TQ], BF16))
        psp = pscope.enter_context(nc.psum_tensor("psp", [128, 2 * TQ], F32))
        for s_ in ("qx", "qw0", "qw1", "qp", "qc", "qo0", "qo1"):
            ctx.new_sem(s_)
        NG = NQ * 24

        def f_qw(e):
            for g in range(NG):
                ot = g % 24
                if g >= 2:
                    e.wait_ge(sem["qc"], g - 1)
                swp = "qw0" if g % 2 == 0 else "qw1"
                cnt[swp] += 16
                e.dma_start(out=w_sb[:, (g % 2) * D:(g % 2 + 1) * D],
                            in_=wqkv12.ap()[ot]).then_inc(sem[swp], 16)
        block.gpsimd(f_qw)

        def f_qt(e):
            e.wait_ge(sem["ii"], 48)
            for g in range(NG):
                tq = g // 24
                e.wait_ge(sem["qx"], 16 * (tq + 1))
                e.wait_ge(sem["qw0" if g % 2 == 0 else "qw1"], 16 * (g // 2 + 1))
                if g >= 2:
                    e.wait_ge(sem["qc"], g - 1)
                for tc in range(2):
                    for i in range(NI):
                        mm = e.matmul(psp[:, (g % 2) * TQ + tc * 512:
                                          (g % 2) * TQ + (tc + 1) * 512],
                                      w_sb[:, (g % 2) * D + i * 128:
                                           (g % 2) * D + (i + 1) * 128],
                                      xnq[:, i * TQ + tc * 512:i * TQ + (tc + 1) * 512],
                                      start=(i == 0), stop=(i == NI - 1))
                cnt["qp"] += 1
                mm.then_inc(sem["qp"], 1)
        block.tensor(f_qt)

        def f_qc(e):
            for g in range(NG):
                e.wait_ge(sem["qp"], g + 1)
                if g >= 2:
                    e.wait_ge(sem["qo0" if g % 2 == 0 else "qo1"],
                              16 * ((g - 2) // 2 + 1))
                cnt["qc"] += 1
                e.activation(obf[:, (g % 2) * TQ:(g % 2 + 1) * TQ],
                             psp[:, (g % 2) * TQ:(g % 2 + 1) * TQ],
                             AF.Silu).then_inc(sem["qc"], 1)
        block.scalar(f_qc)

        def _qstore(e, g):
            tq, ot = g // 24, g % 24
            e.wait_ge(sem["qc"], g + 1)
            sop = "qo0" if g % 2 == 0 else "qo1"
            cnt[sop] += 16
            e.dma_start(out=qkvT[ot * 128:(ot + 1) * 128, tq * TQ:(tq + 1) * TQ],
                        in_=obf[:, (g % 2) * TQ:(g % 2 + 1) * TQ]
                        ).then_inc(sem[sop], 16)

        def f_qxs(e):
            for g in range(NG):
                if g % 24 == 0:
                    tq = g // 24
                    if tq >= 1:
                        e.wait_ge(sem["qp"], tq * 24)
                    cnt["qx"] += 16
                    e.dma_start(out=xnq[:, :], in_=xnT_g.ap()[tq]
                                ).then_inc(sem["qx"], 16)
                if g >= 1:
                    _qstore(e, g - 1)
            _qstore(e, NG - 1)
        block.sync(f_qxs)
        proj_done = [("qo0", cnt["qo0"]), ("qo1", cnt["qo1"])]
        pscope.close()

        bufs = dict(
            qT=ctx.sb("qT", [128, 2 * S], BF16),
            kT=ctx.sb("kT", [128, 2 * S], BF16),
            vraw=ctx.sb("vraw", [128, 2 * S], BF16),
            v_sb=ctx.sb("v_sb", [128, S], BF16),
            ktk=ctx.sb("ktk", [128, S], BF16),
            qdq=ctx.sb("qdq", [128, 2 * S], BF16),
            qdc=ctx.sb("qdc", [128, 2 * 256], F32),
            Dd=ctx.sb("Dd", [128, 2 * 512], BF16),
            P_b=ctx.sb("P_b", [128, 4 * 256], BF16),
            PT_sb=ctx.sb("PT_sb", [128, 2 * 256], BF16),
            o_sb=ctx.sb("o_sb", [128, 2 * 256], BF16),
            S_f=ctx.sb("S_f", [128, 128], F32),
            S_b=ctx.sb("S_b", [128, 128], BF16),
            psS=ctx.ps("psS", [128, 4 * 512], F32),
            ptr2=ctx.ps("ptr2", [128, 2048], BF16),
            psA=ctx.ps("psA", [128, 256], F32),
            psK=ctx.ps("psK", [128, 128], F32),
        )
        for s_ in ("hx", "hv", "hs", "hp", "ho0", "ho1"):
            ctx.new_sem(s_)

        for h in range(HPC):
            _emit_D_head(ctx, block, h, qkvT, OT, bufs, qdecD, DdecD, kdec, bdB,
                         ident, proj_done if h == 0 else None)

        def f_fin(e):
            e.wait_ge(sem["ho0"], cnt["ho0"])
            e.wait_ge(sem["ho1"], cnt["ho1"])
        block.sync(f_fin)
    return nc


def _emit_D_head(ctx, block, h, qkvT, OT, B, qdecD, DdecD, kdec, bdB, ident,
                 first_wait):
    sem, cnt = ctx.sem, ctx.cnt
    NKT = 32
    NB = 16
    bhp, bhs, bhv, bhx = cnt["hp"], cnt["hs"], cnt["hv"], cnt["hx"]
    bho0, bho1 = cnt["ho0"], cnt["ho1"]
    sl = (h % 2) * S
    sl2 = ((h + 1) % 2) * S

    hp = hs = hv = 0
    vtr = [bhp + i + 1 for i in range(NKT)]; hp = NKT
    ktr = [bhp + NKT + i + 1 for i in range(NKT)]; hp = 2 * NKT
    vev = [bhs + i + 1 for i in range(NKT)]; hs = NKT
    kev = [bhs + NKT + i + 1 for i in range(NKT)]; hs = 2 * NKT
    qdqp = [bhv + i + 1 for i in range(NB)]; hv = NB
    hv += 2
    sz = bhv + hv
    smm = [[0, 0] for _ in range(NB)]
    dml = [[0, 0] for _ in range(NB)]
    ptr_ = [[0] * 4 for _ in range(NB)]
    ptc = [[0] * 4 for _ in range(NB)]
    avm = [[0, 0, 0] for _ in range(NB)]
    kmm = [[0, 0] for _ in range(NB)]
    oevp = [0] * NB
    supd = [0] * NB
    sbcp = [0] * NB
    for blk in (0, 1):
        for qsub in range(2):
            hp += 1; smm[blk][qsub] = bhp + hp
    for blk in range(NB):
        for j in range(4):
            hp += 1; ptr_[blk][j] = bhp + hp
        hp += 1; avm[blk][0] = bhp + hp
        hp += 1; avm[blk][1] = bhp + hp
        hp += 1; kmm[blk][0] = bhp + hp
        hp += 1; kmm[blk][1] = bhp + hp
        if blk + 2 < NB:
            for qsub in range(2):
                hp += 1; smm[blk + 2][qsub] = bhp + hp
        hp += 1; avm[blk][2] = bhp + hp
    for blk in range(NB):
        for qsub in range(2):
            hv += 1; dml[blk][qsub] = bhv + hv
        if blk >= 1:
            hv += 1; supd[blk - 1] = bhv + hv
            hv += 1; sbcp[blk - 1] = bhv + hv
    hv += 1; supd[NB - 1] = bhv + hv
    hv += 1; sbcp[NB - 1] = bhv + hv
    for blk in range(NB):
        for j in range(4):
            hs += 1; ptc[blk][j] = bhs + hs
        hs += 1; oevp[blk] = bhs + hs

    def f_hx(e):
        if first_wait:
            for s_, c_ in first_wait:
                e.wait_ge(sem[s_], c_)
        for r, t in ((h, "qT"), (8 + h, "kT"), (16 + h, "vraw")):
            cnt["hx"] += 16
            e.dma_start(out=B[t][:, sl:sl + S], in_=qkvT[r * 128:(r + 1) * 128, :]
                        ).then_inc(sem["hx"], 16)
        cnt["hx"] += 16
        e.dma_start(out=B["qdc"][:, (h % 2) * 256:(h % 2) * 256 + 256],
                    in_=qdecD.ap()[h]).then_inc(sem["hx"], 16)
        cnt["hx"] += 16
        e.dma_start(out=B["Dd"][:, (h % 2) * 512:(h % 2) * 512 + 512],
                    in_=DdecD.ap()[h]).then_inc(sem["hx"], 16)
    if h == 0:
        block.sync(f_hx)
    loads_done = cnt["hx"]

    def f_hv(e):
        e.wait_ge(sem["hx"], loads_done)
        if h == 0:
            e.wait_ge(sem["ii"], 48)
        else:
            e.wait_ge(sem["hp"], bhp)
        for blk in range(NB):
            cnt["hv"] += 1
            e.tensor_mul(B["qdq"][:, sl + blk * 256:sl + (blk + 1) * 256],
                         B["qT"][:, sl + blk * 256:sl + (blk + 1) * 256],
                         B["qdc"][:, (h % 2) * 256:(h % 2) * 256 + 256]
                         ).then_inc(sem["hv"], 1)
        ctx.serm(e.memset(B["S_f"][:, :], 0.0), "hv")
        ctx.serm(e.memset(B["S_b"][:, :], 0.0), "hv")
        assert cnt["hv"] == sz
        def _emit_state(e, blk):
            e.wait_ge(sem["hp"], kmm[blk][1])
            cnt["hv"] += 1
            e.scalar_tensor_tensor(out=B["S_f"][:, :], in0=B["S_f"][:, :],
                                   scalar=bdB[:, h:h + 1], in1=B["psK"][:, :],
                                   op0=ALU.mult, op1=ALU.add).then_inc(sem["hv"], 1)
            assert cnt["hv"] == supd[blk]
            e.wait_ge(sem["hp"], avm[blk][2])
            cnt["hv"] += 1
            e.tensor_copy(B["S_b"][:, :], B["S_f"][:, :]).then_inc(sem["hv"], 1)
            assert cnt["hv"] == sbcp[blk]

        for blk in range(NB):
            for qsub in range(2):
                e.wait_ge(sem["hp"], smm[blk][qsub])
                if blk >= 2:
                    e.wait_ge(sem["hp"], ptr_[blk - 2][3])
                s4 = (2 * blk + qsub) % 4
                cnt["hv"] += 1
                e.tensor_tensor(B["P_b"][:, s4 * 256:(s4 + 1) * 256],
                                B["psS"][:, s4 * 512:s4 * 512 + 256],
                                B["Dd"][:, (h % 2) * 512 + qsub * 256:
                                        (h % 2) * 512 + (qsub + 1) * 256],
                                op=ALU.mult).then_inc(sem["hv"], 1)
                assert cnt["hv"] == dml[blk][qsub]
            if blk >= 1:
                _emit_state(e, blk - 1)
        _emit_state(e, NB - 1)
    block.vector(f_hv)

    def f_ht(e):
        e.wait_ge(sem["hx"], loads_done)
        if h > 0:
            e.wait_ge(sem["hs"], bhs)
            e.wait_ge(sem["hv"], bhv)
        for kt in range(NKT):
            if kt >= 2:
                e.wait_ge(sem["hs"], vev[kt - 2])
            cnt["hp"] += 1
            e.transpose(B["ptr2"][:, (kt % 2) * 1024:(kt % 2) * 1024 + 128],
                        B["vraw"][:, sl + kt * 128:sl + (kt + 1) * 128],
                        ident[:, :]).then_inc(sem["hp"], 1)
            assert cnt["hp"] == vtr[kt]
        for kt in range(NKT):
            e.wait_ge(sem["hs"], vev[NKT - 1] if kt == 0 else kev[kt - 1])
            if kt == 1:
                e.wait_ge(sem["hs"], vev[NKT - 1])
            if kt >= 2:
                e.wait_ge(sem["hs"], kev[kt - 2])
            cnt["hp"] += 1
            e.transpose(B["ptr2"][:, (kt % 2) * 1024:(kt % 2) * 1024 + 128],
                        B["kT"][:, sl + kt * 128:sl + (kt + 1) * 128],
                        ident[:, :]).then_inc(sem["hp"], 1)
            assert cnt["hp"] == ktr[kt]

        def _emit_scores(e, blk):
            for qsub in range(2):
                if blk >= 2:
                    e.wait_ge(sem["hv"], dml[blk - 2][qsub])
                s4 = (2 * blk + qsub) % 4
                cnt["hp"] += 1
                e.matmul(B["psS"][:, s4 * 512:s4 * 512 + 256],
                         B["qT"][:, sl + blk * 256 + qsub * 128:
                                 sl + blk * 256 + (qsub + 1) * 128],
                         B["kT"][:, sl + blk * 256:sl + (blk + 1) * 256],
                         start=True, stop=True).then_inc(sem["hp"], 1)
                assert cnt["hp"] == smm[blk][qsub]

        for blk in (0, 1):
            _emit_scores(e, blk)
        for blk in range(NB):
            for j in range(4):
                ksub, qsub = j // 2, j % 2
                e.wait_ge(sem["hv"], dml[blk][qsub])
                if j >= 2:
                    e.wait_ge(sem["hs"], ptc[blk][j - 2])
                elif blk >= 1:
                    e.wait_ge(sem["hs"], ptc[blk - 1][j + 2])
                else:
                    e.wait_ge(sem["hs"], kev[NKT - 2 + j])
                s4 = (2 * blk + qsub) % 4
                cnt["hp"] += 1
                e.transpose(B["ptr2"][:, (j % 2) * 1024:(j % 2) * 1024 + 128],
                            B["P_b"][:, s4 * 256 + ksub * 128:
                                     s4 * 256 + (ksub + 1) * 128],
                            ident[:, :]).then_inc(sem["hp"], 1)
                assert cnt["hp"] == ptr_[blk][j]
            for ksub in range(2):
                e.wait_ge(sem["hs"], ptc[blk][ksub * 2 + 1])
                if ksub == 0 and blk >= 1:
                    e.wait_ge(sem["hs"], oevp[blk - 1])
                cnt["hp"] += 1
                e.matmul(B["psA"][:, :],
                         B["v_sb"][:, blk * 256 + ksub * 128:
                                   blk * 256 + (ksub + 1) * 128],
                         B["PT_sb"][:, ksub * 256:(ksub + 1) * 256],
                         start=(ksub == 0), stop=False).then_inc(sem["hp"], 1)
                assert cnt["hp"] == avm[blk][ksub]
            for ksub in range(2):
                if blk >= 1 and ksub == 0:
                    e.wait_ge(sem["hv"], supd[blk - 1])
                cnt["hp"] += 1
                e.matmul(B["psK"][:, :],
                         B["ktk"][:, blk * 256 + ksub * 128:
                                  blk * 256 + (ksub + 1) * 128],
                         B["v_sb"][:, blk * 256 + ksub * 128:
                                   blk * 256 + (ksub + 1) * 128],
                         start=(ksub == 0), stop=(ksub == 1)).then_inc(sem["hp"], 1)
                assert cnt["hp"] == kmm[blk][ksub]
            if blk + 2 < NB:
                _emit_scores(e, blk + 2)
            if blk == 0:
                e.wait_ge(sem["hv"], sz)
            else:
                e.wait_ge(sem["hv"], sbcp[blk - 1])
            e.wait_ge(sem["hv"], qdqp[blk])
            cnt["hp"] += 1
            e.matmul(B["psA"][:, :], B["S_b"][:, :],
                     B["qdq"][:, sl + blk * 256:sl + (blk + 1) * 256],
                     start=False, stop=True).then_inc(sem["hp"], 1)
            assert cnt["hp"] == avm[blk][2]
    block.tensor(f_ht)

    def f_hs(e):
        for kt in range(NKT):
            e.wait_ge(sem["hp"], vtr[kt])
            cnt["hs"] += 1
            e.copy(B["v_sb"][:, kt * 128:(kt + 1) * 128],
                   B["ptr2"][:, (kt % 2) * 1024:(kt % 2) * 1024 + 128]
                   ).then_inc(sem["hs"], 1)
            assert cnt["hs"] == vev[kt]
        for kt in range(NKT):
            e.wait_ge(sem["hp"], ktr[kt])
            if h == 0 and kt == 0:
                e.wait_ge(sem["ii"], 48)
            cnt["hs"] += 1
            e.activation(B["ktk"][:, kt * 128:(kt + 1) * 128],
                         B["ptr2"][:, (kt % 2) * 1024:(kt % 2) * 1024 + 128],
                         AF.Copy, scale=kdec[:, h * 2 + (kt % 2):h * 2 + (kt % 2) + 1]
                         ).then_inc(sem["hs"], 1)
            assert cnt["hs"] == kev[kt]
        for blk in range(NB):
            for j in range(4):
                ksub, qsub = j // 2, j % 2
                e.wait_ge(sem["hp"], ptr_[blk][j])
                if blk >= 1:
                    e.wait_ge(sem["hp"], avm[blk - 1][2])
                cnt["hs"] += 1
                e.copy(B["PT_sb"][:, ksub * 256 + qsub * 128:
                                  ksub * 256 + (qsub + 1) * 128],
                       B["ptr2"][:, (j % 2) * 1024:(j % 2) * 1024 + 128]
                       ).then_inc(sem["hs"], 1)
                assert cnt["hs"] == ptc[blk][j]
            e.wait_ge(sem["hp"], avm[blk][2])
            if blk >= 2:
                e.wait_ge(sem["ho0" if blk % 2 == 0 else "ho1"],
                          (bho0 if blk % 2 == 0 else bho1) + 16 * ((blk - 2) // 2 + 1))
            elif h > 0:
                e.wait_ge(sem["ho0" if blk % 2 == 0 else "ho1"],
                          bho0 if blk % 2 == 0 else bho1)
            cnt["hs"] += 1
            e.copy(B["o_sb"][:, (blk % 2) * 256:(blk % 2 + 1) * 256],
                   B["psA"][:, :]).then_inc(sem["hs"], 1)
            assert cnt["hs"] == oevp[blk]
    block.scalar(f_hs)

    def f_ho(e):
        if h + 1 < HPC:
            if h >= 1:
                e.wait_ge(sem["hp"], bhp)
                e.wait_ge(sem["hv"], bhv)
            for r, t in ((h + 1, "qT"), (8 + h + 1, "kT"), (16 + h + 1, "vraw")):
                cnt["hx"] += 16
                e.dma_start(out=B[t][:, sl2:sl2 + S],
                            in_=qkvT[r * 128:(r + 1) * 128, :]
                            ).then_inc(sem["hx"], 16)
            cnt["hx"] += 16
            e.dma_start(out=B["qdc"][:, ((h + 1) % 2) * 256:((h + 1) % 2) * 256 + 256],
                        in_=qdecD.ap()[h + 1]).then_inc(sem["hx"], 16)
            cnt["hx"] += 16
            e.dma_start(out=B["Dd"][:, ((h + 1) % 2) * 512:((h + 1) % 2) * 512 + 512],
                        in_=DdecD.ap()[h + 1]).then_inc(sem["hx"], 16)
        for blk in range(NB):
            e.wait_ge(sem["hs"], oevp[blk])
            hop = "ho0" if blk % 2 == 0 else "ho1"
            cnt[hop] += 16
            e.dma_start(out=OT[h * 128:(h + 1) * 128, blk * 256:(blk + 1) * 256],
                        in_=B["o_sb"][:, (blk % 2) * 256:(blk % 2 + 1) * 256]
                        ).then_inc(sem[hop], 16)
    block.sync(f_ho)


def build_launchE():
    ctx = new_ctx("lE")
    nc, es = ctx.nc, ctx.es
    sem, cnt = ctx.sem, ctx.cnt

    x4 = nc.dram_tensor("x4", [TPC, D], F32, kind="ExternalInput")
    attnT = nc.dram_tensor("attnT", [128, NI * TPC], BF16, kind="ExternalInput")
    xn5T = nc.dram_tensor("xn5T", [128, NI * TPC], BF16, kind="ExternalInput")
    gwT = nc.dram_tensor("gwT", [NI, 128, D], F32, kind="ExternalInput")
    owT = nc.dram_tensor("owT", [NI, 128, D], F32, kind="ExternalInput")
    nwD = nc.dram_tensor("nwD", [128, NI], F32, kind="ExternalInput")
    l1wT = nc.dram_tensor("l1wT", [NI, 128, D], F32, kind="ExternalInput")
    wqkhD = nc.dram_tensor("wqkhD", [128, NI * 640], F32, kind="ExternalInput")
    BmD = nc.dram_tensor("BmD", [128, D], F32, kind="ExternalInput")
    bbD = nc.dram_tensor("bbD", [128, NI], F32, kind="ExternalInput")
    identD = nc.dram_tensor("identD", [128, 128], BF16, kind="ExternalInput")
    x6 = nc.dram_tensor("x6", [TPC, D], F32, kind="ExternalOutput")
    x5 = nc.dram_tensor("x5", [TPC, D], F32)

    with ctx.es:
        block = es.enter_context(nc.Block())
        ident = ctx.sb("ident", [128, 128], BF16)
        ptrans = ctx.ps("ptrans", [128, 2048], BF16)
        rhs_sb = ctx.sb("rhs_sb", [128, NI * TPC], BF16)
        lnb = LnBufs(ctx, nxn=1, xa2=False)
        psX = ctx.ps("psX", [128, 1024], F32)
        fw = ctx.sb("fw", [128, 2 * D], BF16)
        fps = ctx.ps("fps", [128, 2 * 1024], F32)
        fobf = ctx.sb("fobf", [128, 2 * 1024], BF16)
        fxsl = ctx.sb("fxsl", [128, 3 * 8 * 128], F32)
        fshared = (fw, fps, fobf, fxsl)
        nw = ctx.sb("nw", [128, NI], F32)
        rstdb = ctx.sb("rstdb", [128, TPC], F32)
        gbuf = ctx.sb("gbuf", [128, 2 * TPC], BF16)
        sqbuf = ctx.sb("sqbuf", [128, 2 * TPC], BF16)
        rsml = ctx.sb("rsml", [1, 2 * TPC], F32)
        rsb = ctx.sb("rsb", [1, TPC], BF16)
        epsb = ctx.sb("epsb", [1, 8], F32)
        ones128 = ctx.sb("ones128", [128, 1], BF16)
        ones1 = ctx.sb("ones1", [1, 128], BF16)
        for s_ in ("ii", "ax", "rv", "rp", "rs", "gw0", "gw1", "gp", "gc", "gv"):
            ctx.new_sem(s_)

        escope = ExitStack()
        xnT5 = escope.enter_context(nc.sbuf_tensor("xnT5", [128, NI * TPC], BF16))

        def f_ii(e):
            for o, i_ in ((ident[:, :], identD[:, :]), (nw[:, :], nwD[:, :]),
                          (rhs_sb[:, :], attnT[:, :]), (xnT5[:, :], xn5T[:, :])):
                cnt["ii"] += 16
                e.dma_start(out=o, in_=i_).then_inc(sem["ii"], 16)
        block.sync(f_ii)

        def f_vinit(e):
            ctx.serm(e.memset(epsb[:, :], EPS), "rv")
            ctx.serm(e.memset(ones128[:, :], 1.0), "rv")
            ctx.serm(e.memset(ones1[:, :], 1.0), "rv")
        block.vector(f_vinit)
        vinit = cnt["rv"]

        def f_sq(e):
            e.wait_ge(sem["ii"], 64)
            for ot in range(NI):
                if ot >= 2:
                    e.wait_ge(sem["rp"], ot - 1)
                cnt["rv"] += 1
                e.tensor_mul(sqbuf[:, (ot % 2) * TPC:(ot % 2 + 1) * TPC],
                             rhs_sb[:, ot * TPC:(ot + 1) * TPC],
                             rhs_sb[:, ot * TPC:(ot + 1) * TPC]).then_inc(sem["rv"], 1)
        block.vector(f_sq)
        sq_done = cnt["rv"]

        def f_rp(e):
            e.wait_ge(sem["rv"], vinit)
            for ot in range(NI):
                e.wait_ge(sem["rv"], vinit + ot + 1)
                for c in range(2):
                    mm = e.matmul(psX[0:1, c * 512:(c + 1) * 512], ones128[:, 0:1],
                                  sqbuf[:, (ot % 2) * TPC + c * 512:
                                        (ot % 2) * TPC + (c + 1) * 512],
                                  start=(ot == 0), stop=(ot == NI - 1))
                cnt["rp"] += 1
                mm.then_inc(sem["rp"], 1)
            e.wait_ge(sem["rv"], sq_done + 2)
            e.wait_ge(sem["rs"], 1)
            for c in range(2):
                cnt["rp"] += 1
                e.matmul(psX[:, c * 512:(c + 1) * 512], ones1[0:1, :],
                         rsb[0:1, c * 512:(c + 1) * 512],
                         start=True, stop=True).then_inc(sem["rp"], 1)
        block.tensor(f_rp)

        def f_rs(e):
            e.wait_ge(sem["rp"], NI)
            cnt["rs"] += 1
            e.activation(rsml[0:1, 0:TPC], psX[0:1, 0:TPC], AF.Sqrt,
                         scale=1.0 / D, bias=epsb[0:1, 0:1]).then_inc(sem["rs"], 1)
            e.wait_ge(sem["rp"], NI + 2)
            cnt["rs"] += 1
            e.copy(rstdb[:, :], psX[:, :]).then_inc(sem["rs"], 1)
        block.scalar(f_rs)

        def f_rv2(e):
            e.wait_ge(sem["rs"], 1)
            ctx.serm(e.reciprocal(rsml[0:1, TPC:2 * TPC], rsml[0:1, 0:TPC]), "rv")
            ctx.serw(e, "rv")
            ctx.serm(e.tensor_copy(rsb[0:1, :], rsml[0:1, TPC:2 * TPC]), "rv")
        block.vector(f_rv2)

        def f_gw(e):
            for ot in range(NI):
                if ot >= 2:
                    e.wait_ge(sem["gc"], ot - 1)
                swp = "gw0" if ot % 2 == 0 else "gw1"
                cnt[swp] += 16
                e.dma_start(out=fw[:, (ot % 2) * D:(ot % 2 + 1) * D],
                            in_=gwT.ap()[ot]).then_inc(sem[swp], 16)
        block.gpsimd(f_gw)

        def f_gt(e):
            for ot in range(NI):
                e.wait_ge(sem["gw0" if ot % 2 == 0 else "gw1"], 16 * (ot // 2 + 1))
                if ot >= 2:
                    e.wait_ge(sem["gc"], ot - 1)
                for tc in range(2):
                    for i in range(NI):
                        mm = e.matmul(fps[:, (ot % 2) * TPC + tc * 512:
                                          (ot % 2) * TPC + (tc + 1) * 512],
                                      fw[:, (ot % 2) * D + i * 128:
                                         (ot % 2) * D + (i + 1) * 128],
                                      xnT5[:, i * TPC + tc * 512:i * TPC + (tc + 1) * 512],
                                      start=(i == 0), stop=(i == NI - 1))
                cnt["gp"] += 1
                mm.then_inc(sem["gp"], 1)
        block.tensor(f_gt)

        def f_gc(e):
            for ot in range(NI):
                e.wait_ge(sem["gp"], ot + 1)
                if ot >= 2:
                    e.wait_ge(sem["gv"], 3 * (ot - 1))
                cnt["gc"] += 1
                e.activation(gbuf[:, (ot % 2) * TPC:(ot % 2 + 1) * TPC],
                             fps[:, (ot % 2) * TPC:(ot % 2 + 1) * TPC],
                             AF.Sigmoid).then_inc(sem["gc"], 1)
        block.scalar(f_gc)

        def f_gv(e):
            e.wait_ge(sem["rv"], sq_done + 2)
            e.wait_ge(sem["rs"], 2)
            for ot in range(NI):
                e.wait_ge(sem["gc"], ot + 1)
                sl = rhs_sb[:, ot * TPC:(ot + 1) * TPC]
                cnt["gv"] += 1
                e.tensor_scalar(out=sl, in0=sl, scalar1=nw[:, ot:ot + 1],
                                scalar2=None, op0=ALU.mult).then_inc(sem["gv"], 1)
                ctx.serw(e, "gv")
                cnt["gv"] += 1
                e.tensor_mul(sl, sl, rstdb[:, :]).then_inc(sem["gv"], 1)
                ctx.serw(e, "gv")
                cnt["gv"] += 1
                e.tensor_mul(sl, sl, gbuf[:, (ot % 2) * TPC:(ot % 2 + 1) * TPC]
                             ).then_inc(sem["gv"], 1)
        block.vector(f_gv)
        gv_done = cnt["gv"]
        escope.close()

        st1 = emit_fproj(ctx, block, "p",
                         w_dram=owT, nko=NI, kt=NI,
                         rhs_slice=lambda i, tc: rhs_sb[:, i * TPC + tc * 512:
                                                        i * TPC + (tc + 1) * 512],
                         n_tc=2, extras=None,
                         wait_tensor=(("gv", gv_done), ("gc", NI)),
                         evac_mode="resadd", ident=ident, ptrans=ptrans,
                         x_in_dram=x4, x_out_dram=x5, res_scale=1.0,
                         shared_bufs=fshared,
                         wait_gpsimd=[("gp", NI)],
                         wait_scalar_first=[("gv", gv_done)])

        st2 = emit_l1ffn(ctx, block, "d", x_in=x5, x_out=x6, l1wT=l1wT,
                         wqkhD=wqkhD, BmD=BmD, bbD=bbD, ident=ident,
                         ptrans=ptrans, lnb=lnb, xnT_shared=rhs_sb, psX=psX,
                         wait_sync=st1, wait_vec=[("pp", NI)],
                         fshared=fshared,
                         fgates={"tensor": [("pc", NI)],
                                 "gpsimd": [("pp", NI)],
                                 "sync": st1,
                                 "scalar": [("pt", NI * NT)],
                                 "lnpe": [("pv", NI * NT)],
                                 "l1ld": [("gp", NI)],
                                 "l1pe": []})

        def f_fin(e):
            for s_, c_ in st2:
                e.wait_ge(sem[s_], c_)
        block.sync(f_fin)
    return nc


def _prep_E(inputs):
    pr = {}
    pr["gwT"] = _tile_w(np.ascontiguousarray(inputs["L1_attn_gate_w"].T))
    pr["owT"] = _tile_w(np.ascontiguousarray(inputs["L1_attn_out_w"].T))
    pr["nwD"] = _pmaj(np.ascontiguousarray(inputs["L1_attn_norm_w"].reshape(D, 1)))
    return pr


def _xn_slice_pm(xn_b, j):
    sl = np.ascontiguousarray(xn_b[j * TPC:(j + 1) * TPC, :])
    a = sl.view(np.uint16).reshape(TPC, NI, 128).transpose(2, 1, 0)
    return np.ascontiguousarray(a.reshape(128, NI * TPC)).view(BF)


def run_E(inputs, x4, attnT5, xn5):
    prE = _prep_E(inputs)
    prC = _prep_C(inputs)
    common = _prep_common(inputs)
    nc = build_launchE()
    in_maps = []
    for c in range(8):
        b, j = c // 4, c % 4
        m = {"gwT": prE["gwT"], "owT": prE["owT"], "nwD": prE["nwD"],
             "l1wT": prC["l1wT"], "wqkhD": prC["wqkhD"], "BmD": prC["BmD"],
             "bbD": prC["bbD"], "identD": common["identD"]}
        m["x4"] = np.ascontiguousarray(x4[b, j * TPC:(j + 1) * TPC, :])
        m["attnT"] = _attnT_slice(attnT5[b], j)
        m["xn5T"] = _xn_slice_pm(xn5[b], j)
        in_maps.append(m)
    res = run_bass_kernel_spmd(nc, in_maps, core_ids=list(range(8)))
    x6 = np.zeros((2, S, D), np.float32)
    for c in range(8):
        b, j = c // 4, c % 4
        x6[b, j * TPC:(j + 1) * TPC] = res.results[c]["x6"]
    return x6

def _prep_common(inputs):
    p = {}
    p["identD"] = np.ascontiguousarray(np.eye(128, dtype=np.float32).astype(BF))
    return p


def _prep_A(inputs):
    pr = {}
    pr["wbT"] = _tile_w(np.ascontiguousarray(inputs["L0_ffn_W_base"].T))
    pr["bb"] = np.ascontiguousarray(inputs["L0_ffn_b_base"].reshape(1, D))
    pr["encT"] = _pmaj(np.concatenate([inputs["L0_ffn_enc0"].T,
                                       inputs["L0_ffn_enc1"].T], axis=1))
    pr["MTc"] = _build_MT_cols()
    pr["c0"] = np.stack([_pmaj(inputs["L0_ffn_coefs0"][e]) for e in range(3)])
    pr["m0"] = np.stack([_pmaj(inputs["L0_ffn_mask0"][e].astype(np.uint8)) for e in range(3)])
    pr["c1"] = np.stack([_pmaj(inputs["L0_ffn_coefs1"][e]) for e in range(3)])
    pr["m1"] = np.stack([_pmaj(inputs["L0_ffn_mask1"][e].astype(np.uint8)) for e in range(3)])
    return pr


def run_A(inputs):
    x0 = inputs["L0_attn_hidden_states"]
    shared = _prep_A(inputs)
    shared.update(_prep_common(inputs))
    nc = build_launchA()
    in_maps = []
    for c in range(8):
        b, j = c // 4, c % 4
        m = dict(shared)
        m["x0"] = np.ascontiguousarray(x0[b, j * TPC:(j + 1) * TPC, :])
        in_maps.append(m)
    res = run_bass_kernel_spmd(nc, in_maps, core_ids=list(range(8)))
    x1 = np.zeros((2, S, D), np.float32)
    xn2 = np.zeros((2, S, D), BF)
    for c in range(8):
        b, j = c // 4, c % 4
        x1[b, j * TPC:(j + 1) * TPC] = res.results[c]["x1"]
        xn2[b, j * TPC:(j + 1) * TPC] = res.results[c]["xn2"]
    return x1, xn2, res.results[0]["g_out"]


def _prep_B(inputs):
    wq, wk, wv = inputs["L0_attn_wq"], inputs["L0_attn_wk"], inputs["L0_attn_wv"]
    wqkv_hg = []
    for hg in range(4):
        sl = slice(hg * 1024, (hg + 1) * 1024)
        comb = np.concatenate([wq[:, sl], wk[:, sl], wv[:, sl]], axis=1)
        wqkv_hg.append(_tile_w(comb))
    cos, sin = _rope_tables()
    sinS = np.concatenate([-sin[0:64, :], sin[64:128, :]], axis=0)
    return wqkv_hg, cos, np.ascontiguousarray(sinS)


def _xnq_host(xn2_b):
    a = np.asarray(xn2_b).view(np.uint16).reshape(4, 1024, 32, 128)
    a = np.ascontiguousarray(a.transpose(0, 3, 2, 1).reshape(4, 128, 32 * 1024))
    return a.view(BF)


def run_B(inputs, xn2):
    wqkv_hg, cos, sin = _prep_B(inputs)
    common = _prep_common(inputs)
    tri = _tri_mask()
    nc = build_launchB()
    xnq = [_xnq_host(xn2[b]) for b in range(2)]
    in_maps = []
    for c in range(8):
        b, hg = c // 4, c % 4
        in_maps.append({"xnT_g": xnq[b], "wqkv": wqkv_hg[hg], "cosD": cos, "sinD": sin,
                        "identD": common["identD"], "triD": tri})
    res = run_bass_kernel_spmd(nc, in_maps, core_ids=list(range(8)))
    attnT = np.zeros((2, D, S), BF)
    for c in range(8):
        b, hg = c // 4, c % 4
        attnT[b, hg * 1024:(hg + 1) * 1024, :] = res.results[c]["OT"]
    return attnT


def _prep_C(inputs):
    pr = {}
    pr["woT"] = _tile_w(np.ascontiguousarray(inputs["L0_attn_wo"]))
    pr["wbT"] = _tile_w(np.ascontiguousarray(inputs["L0_ffn_W_base"].T))
    pr["encT"] = _pmaj(np.concatenate([inputs["L0_ffn_enc0"].T,
                                       inputs["L0_ffn_enc1"].T], axis=1))
    pr["l1wT"] = _tile_w(np.ascontiguousarray(inputs["L1_ffn_W"].T))
    A_flat = inputs["L1_ffn_A"].reshape(128, D)
    pr["wqkhD"] = _pmaj(np.ascontiguousarray(
        np.concatenate([inputs["L1_ffn_Wq"].T, inputs["L1_ffn_Wk"].T, A_flat.T],
                       axis=1)))
    pr["BmD"] = np.ascontiguousarray(
        inputs["L1_ffn_Bm"].transpose(0, 2, 1).reshape(128, D))
    pr["bbD"] = np.ascontiguousarray(
        inputs["L1_ffn_b"].astype(np.float32).reshape(NI, 128).T)
    return pr


def _attnT_slice(attnT_b, j):
    sl = np.ascontiguousarray(attnT_b[:, j * TPC:(j + 1) * TPC])
    a = sl.view(np.uint16).reshape(NI, 128, TPC).transpose(1, 0, 2)
    return np.ascontiguousarray(a.reshape(128, NI * TPC)).view(BF)


def run_C(inputs, x1, attnT, g):
    pr = _prep_C(inputs)
    pr.update(_prep_common(inputs))
    nc = build_launchC()
    in_maps = []
    for c in range(8):
        b, j = c // 4, c % 4
        m = dict(pr)
        m["x1"] = np.ascontiguousarray(x1[b, j * TPC:(j + 1) * TPC, :])
        m["attnT"] = _attnT_slice(attnT[b], j)
        m["g_in"] = g
        in_maps.append(m)
    res = run_bass_kernel_spmd(nc, in_maps, core_ids=list(range(8)))
    x4 = np.zeros((2, S, D), np.float32)
    xn5 = np.zeros((2, S, D), BF)
    for c in range(8):
        b, j = c // 4, c % 4
        x4[b, j * TPC:(j + 1) * TPC] = res.results[c]["x4"]
        xn5[b, j * TPC:(j + 1) * TPC] = res.results[c]["xn5"]
    return x4, xn5


def _prep_D(inputs):
    qd, kd, Ddec, bd = _l1_decays()
    qkvT_full = np.ascontiguousarray(inputs["L1_attn_qkv_w"].T)
    wqkv_hg, qdecs, kdecs, Ddecs, bds = [], [], [], [], []
    for hg in range(4):
        qc, kc, vc = [], [], []
        qdl = np.zeros((8, 128, 256), np.float32)
        kdl = np.zeros((128, 16), np.float32)
        Ddl = np.zeros((8, 128, 512), np.float32)
        bdl = np.zeros((128, 8), np.float32)
        for lh in range(8):
            gh = hg * 8 + lh
            base = gh * 384
            qc.append(qkvT_full[:, base:base + 128])
            kc.append(qkvT_full[:, base + 128:base + 256])
            vc.append(qkvT_full[:, base + 256:base + 384])
            qdl[lh] = np.broadcast_to(qd[gh][None, :], (128, 256))
            for sub in range(2):
                kdl[:, lh * 2 + sub] = kd[gh][sub * 128:(sub + 1) * 128]
            for qsub in range(2):
                Ddl[lh, :, qsub * 256:(qsub + 1) * 256] = \
                    Ddec[gh][qsub * 128:(qsub + 1) * 128, :].astype(np.float32)
            bdl[:, lh] = bd[gh]
        comb = np.concatenate(qc + kc + vc, axis=1)
        wqkv_hg.append(_tile_w(comb))
        qdecs.append(np.ascontiguousarray(qdl))
        kdecs.append(np.ascontiguousarray(kdl))
        Ddecs.append(np.ascontiguousarray(Ddl.astype(BF)))
        bds.append(np.ascontiguousarray(bdl))
    return wqkv_hg, qdecs, kdecs, Ddecs, bds


def run_D(inputs, xn5):
    wqkv_hg, qdecs, kdecs, Ddecs, bds = _prep_D(inputs)
    common = _prep_common(inputs)
    nc = build_launchD()
    xnq = [_xnq_host(xn5[b]) for b in range(2)]
    in_maps = []
    for c in range(8):
        b, hg = c // 4, c % 4
        in_maps.append({"xnT_g": xnq[b], "wqkv12": wqkv_hg[hg], "qdecD": qdecs[hg],
                        "kdecD": kdecs[hg], "DdecD": Ddecs[hg], "bdBD": bds[hg],
                        "identD": common["identD"]})
    res = run_bass_kernel_spmd(nc, in_maps, core_ids=list(range(8)))
    attnT = np.zeros((2, D, S), BF)
    for c in range(8):
        b, hg = c // 4, c % 4
        attnT[b, hg * 1024:(hg + 1) * 1024, :] = res.results[c]["OT"]
    return attnT


def kernel(**inputs):
    x1, xn2, g = run_A(inputs)
    attnT = run_B(inputs, xn2)
    x4, xn5 = run_C(inputs, x1, attnT, g)
    attnT5 = run_D(inputs, xn5)
    x6 = run_E(inputs, x4, attnT5, np.asarray(xn5))
    return x6


# version 09 (final)
# speedup vs baseline: 1.1784x; 1.0070x over previous
import math
from contextlib import ExitStack

import numpy as np
import ml_dtypes

import concourse.bass as bass
import concourse.mybir as mybir
from concourse.bass_utils import run_bass_kernel_spmd

F32 = mybir.dt.float32
BF16 = mybir.dt.bfloat16
U8 = mybir.dt.uint8
AF = mybir.ActivationFunctionType
ALU = mybir.AluOpType
AX = mybir.AxisListType
BF = ml_dtypes.bfloat16

D = 4096
S = 4096
TPC = 1024
NT = TPC // 128
NI = D // 128
HPC = 8
HD = 128
EPS = 1e-5
ISQ = 1.0 / math.sqrt(128.0)


def _build_MT_cols():
    n = D
    i = np.arange(n, dtype=np.float64)
    M = np.zeros((n, n))
    M[:, 0] = 1.0 / math.sqrt(n)
    f = np.arange(1, (n - 1) // 2 + 1, dtype=np.float64)
    ang = 2.0 * np.pi * np.outer(i, f) / n
    M[:, 1:-1:2] = math.sqrt(2.0 / n) * np.cos(ang)
    M[:, 2:-1:2] = math.sqrt(2.0 / n) * np.sin(ang)
    M[:, n - 1] = (1.0 / math.sqrt(n)) * np.where(i % 2 == 0, 1.0, -1.0)
    MT = M.T.astype(np.float32)
    cols = MT.reshape(n, 8, 512).transpose(1, 0, 2)
    out = np.stack([_pmaj(cols[oc]) for oc in range(8)])
    return np.ascontiguousarray(out.astype(BF))


def _rope_tables():
    hd = HD
    inv = (1.0 / (10000.0 ** (np.arange(0, hd, 2, dtype=np.float32) / hd))).astype(np.float32)
    pos = np.arange(S, dtype=np.float32)
    fr = pos[:, None] * inv[None, :]
    emb = np.concatenate([fr, fr], axis=-1).astype(np.float32)
    return (np.ascontiguousarray(np.cos(emb).T.astype(np.float32)),
            np.ascontiguousarray(np.sin(emb).T.astype(np.float32)))


def _tile_w(wT):
    wT = np.asarray(wT)
    K, N = wT.shape
    t = wT.reshape(K // 128, 128, N // 128, 128)
    return np.ascontiguousarray(t.transpose(2, 1, 0, 3).reshape(N // 128, 128, K))


def _pmaj(a):
    a = np.asarray(a)
    JP, R = a.shape
    return np.ascontiguousarray(a.reshape(JP // 128, 128, R).transpose(1, 0, 2)
                                .reshape(128, (JP // 128) * R))


def _tri_mask():
    out = np.zeros((128, 4 * 512), np.float32)
    p = np.arange(128)[:, None]
    j = np.arange(512)[None, :]
    for s_ in range(4):
        out[:, s_ * 512:(s_ + 1) * 512] = np.where(j < s_ * 128 + p, -1e30, 0.0)
    return np.ascontiguousarray(out.astype(BF))


def _l1_decays():
    HEADS_ = 32
    BLK = 256
    base = 1.0 / 2.0 ** (8.0 / HEADS_)
    slope = (base ** (np.arange(HEADS_, dtype=np.float64) + 1.0)) * (1.0 - 1 / (31 + 1e-05) + 1e-05)
    r = np.arange(BLK, dtype=np.float64) + 1.0
    qd = np.exp(-slope[:, None] * r[None, :])
    kd = np.exp(-slope[:, None] * (BLK - r)[None, :])
    diff = r[:, None] - r[None, :]
    Ddec = np.exp(np.where(diff >= 0, -slope[:, None, None] * diff, -np.inf))
    bd = np.exp(-slope * BLK)
    return (qd.astype(np.float32), kd.astype(np.float32),
            np.ascontiguousarray(Ddec.astype(BF)), bd)


class Ctx:
    def __init__(self, nc, es):
        self.nc = nc
        self.es = es
        self.sem = {}
        self.cnt = {}

    def new_sem(self, name):
        assert name not in self.sem, name
        self.sem[name] = self.es.enter_context(self.nc.semaphore(name))
        self.cnt[name] = 0
        return self.sem[name]

    def sb(self, name, shape, dt):
        return self.es.enter_context(self.nc.sbuf_tensor(name, shape, dt))

    def ps(self, name, shape, dt=F32):
        return self.es.enter_context(self.nc.psum_tensor(name, shape, dt))

    def serw(self, e, name):
        e.wait_ge(self.sem[name], self.cnt[name])

    def serm(self, inst, name):
        self.cnt[name] += 1
        inst.then_inc(self.sem[name], 1)


def new_ctx(name):
    nc = bass.Bass("TRN2", target_bir_lowering=False, name=name)
    es = ExitStack()
    return Ctx(nc, es)


class LnBufs:
    def __init__(self, ctx, nt=NT, nxn=2, xa2=True):
        self.xa2 = xa2
        self.xa = ctx.sb("lnb_xa", [128, (2 if xa2 else 1) * D], F32)
        self.st = ctx.sb("lnb_st", [128, 4 * nt], F32)
        self.nxn = nxn
        self.xn = ctx.sb("lnb_xn", [128, nxn * D], BF16)
        self.eps = ctx.sb("lnb_eps", [128, 1], F32)


def emit_ln(ctx, block, tag, x_dram, xnT_sb, ident, lnb, nt=NT, xn_bf_store=None,
            build_xnT=True, ptrans=None, wait_sync=None, wait_vec=None,
            wait_pe=None):
    sx, sv, ss, sp = (tag + c for c in "xvsp")
    sx1 = tag + "x1"
    sc2 = tag + "c2"
    for s_ in (sx, sx1, sv, ss, sp, sc2):
        ctx.new_sem(s_)
    sxp = (sx, sx1)
    so0, so1 = tag + "o0", tag + "o1"
    if xn_bf_store is not None:
        ctx.new_sem(so0)
        ctx.new_sem(so1)
    xa, st, xn = lnb.xa, lnb.st, lnb.xn
    sem, cnt = ctx.sem, ctx.cnt

    def f_sync(e):
        if wait_sync is not None:
            for ws_, wc_ in wait_sync:
                e.wait_ge(sem[ws_], wc_)
        dep = 2 if lnb.xa2 else 1
        for tt in range(nt):
            if tt >= dep:
                e.wait_ge(sem[sv], tt - dep + 1)
                e.wait_ge(sem[ss], tt - dep + 1)
            sl = (tt % 2) if lnb.xa2 else 0
            cnt[sxp[tt % 2]] += 16
            e.dma_start(out=xa[:, sl * D:(sl + 1) * D],
                        in_=x_dram[tt * 128:(tt + 1) * 128, :]
                        ).then_inc(sem[sxp[tt % 2]], 16)
    block.sync(f_sync)

    def f_vec(e):
        for tt in range(nt):
            e.wait_ge(sem[sxp[tt % 2]], 16 * (tt // 2 + 1))
            sl = (tt % 2) if lnb.xa2 else 0
            cnt[sv] += 1
            e.reduce_sum(st[:, tt:tt + 1], xa[:, sl * D:(sl + 1) * D],
                         axis=AX.X).then_inc(sem[sv], 1)
    block.vector(f_vec)

    bss0 = cnt[ss]

    def f_sc(e):
        for tt in range(nt):
            e.wait_ge(sem[sxp[tt % 2]], 16 * (tt // 2 + 1))
            if tt >= 1:
                e.wait_ge(sem[ss], bss0 + tt)
            sl = (tt % 2) if lnb.xa2 else 0
            cnt[ss] += 1
            e.activation(xn[:, 0:D], xa[:, sl * D:(sl + 1) * D], AF.Square,
                         accum_out=st[:, nt + tt:nt + tt + 1]).then_inc(sem[ss], 1)
    block.scalar(f_sc)

    def f_vstat(e):
        e.wait_ge(sem[sv], cnt[sv])
        e.wait_ge(sem[ss], cnt[ss])
        e.memset(lnb.eps[:, :], EPS)
        ctx.serm(e.tensor_scalar(out=st[:, 0:nt], in0=st[:, 0:nt], scalar1=1.0 / D,
                                 scalar2=None, op0=ALU.mult), sv)
        ctx.serw(e, sv)
        ctx.serm(e.tensor_mul(st[:, 2 * nt:3 * nt], st[:, 0:nt], st[:, 0:nt]), sv)
        ctx.serw(e, sv)
        ctx.serm(e.scalar_tensor_tensor(out=st[:, nt:2 * nt], in0=st[:, nt:2 * nt],
                                        scalar=1.0 / D, in1=st[:, 2 * nt:3 * nt],
                                        op0=ALU.mult, op1=ALU.subtract), sv)
    block.vector(f_vstat)

    def f_sstat(e):
        e.wait_ge(sem[sv], cnt[sv])
        cnt[ss] += 1
        e.activation(st[:, 2 * nt:3 * nt], st[:, nt:2 * nt], AF.Sqrt,
                     bias=lnb.eps[:, :]).then_inc(sem[ss], 1)
    block.scalar(f_sstat)

    def f_vrec(e):
        e.wait_ge(sem[ss], cnt[ss])
        cnt[sv] += 1
        e.reciprocal(st[:, 3 * nt:4 * nt], st[:, 2 * nt:3 * nt]).then_inc(sem[sv], 1)
    block.vector(f_vrec)

    bx2 = (cnt[sx], cnt[sx1])
    bv, bp = cnt[sv], cnt[sp]
    bc2 = cnt[sc2]
    per_tt = 1

    def _store(e, tt):
        e.wait_ge(sem[sv], bv + tt * per_tt + 1)
        sop = so0 if tt % 2 == 0 else so1
        cnt[sop] += 16
        e.dma_start(out=xn_bf_store[tt * 128:(tt + 1) * 128, :],
                    in_=xn[:, (tt % lnb.nxn) * D:(tt % lnb.nxn + 1) * D]).then_inc(sem[sop], 16)

    def f_sync2(e):
        e.wait_ge(sem[sv], bv)
        e.wait_ge(sem[ss], cnt[ss])
        dep = 2 if lnb.xa2 else 1
        for tt in range(nt):
            if tt >= dep:
                e.wait_ge(sem[sv], bv + (tt - dep) * per_tt + 1)
            sl = (tt % 2) if lnb.xa2 else 0
            cnt[sxp[tt % 2]] += 16
            e.dma_start(out=xa[:, sl * D:(sl + 1) * D],
                        in_=x_dram[tt * 128:(tt + 1) * 128, :]
                        ).then_inc(sem[sxp[tt % 2]], 16)
            if xn_bf_store is not None and tt >= 1:
                _store(e, tt - 1)
        if xn_bf_store is not None:
            _store(e, nt - 1)
    block.sync(f_sync2)

    def f_vec2(e):
        if wait_vec is not None:
            for ws_, wc_ in wait_vec:
                e.wait_ge(sem[ws_], wc_)
        for tt in range(nt):
            e.wait_ge(sem[sxp[tt % 2]], bx2[tt % 2] + 16 * (tt // 2 + 1))
            if tt >= 1:
                e.wait_ge(sem[sv], bv + tt * per_tt)
            if tt >= 2 and xn_bf_store is not None:
                e.wait_ge(sem[so0 if tt % 2 == 0 else so1], 16 * ((tt - 2) // 2 + 1))
            if tt >= lnb.nxn and build_xnT:
                e.wait_ge(sem[sp], bp + (tt - lnb.nxn + 1) * NI)
            sl = (tt % 2) if lnb.xa2 else 0
            cnt[sv] += 1
            e.tensor_scalar(out=xn[:, (tt % lnb.nxn) * D:(tt % lnb.nxn + 1) * D],
                            in0=xa[:, sl * D:(sl + 1) * D],
                            scalar1=st[:, tt:tt + 1],
                            scalar2=st[:, 3 * nt + tt:3 * nt + tt + 1],
                            op0=ALU.subtract, op1=ALU.mult).then_inc(sem[sv], 1)
    block.vector(f_vec2)

    if build_xnT:
        def f_sc2(e):
            if wait_vec is not None:
                for ws_, wc_ in wait_vec:
                    e.wait_ge(sem[ws_], wc_)
            for tt in range(nt):
                for i in range(NI):
                    k = tt * NI + i
                    e.wait_ge(sem[sp], bp + k + 1)
                    cnt[sc2] += 1
                    e.copy(xnT_sb[:, i * (nt * 128) + tt * 128:
                                  i * (nt * 128) + (tt + 1) * 128],
                           ptrans[:, (k % 2) * 1024:(k % 2) * 1024 + 128]
                           ).then_inc(sem[sc2], 1)
        block.scalar(f_sc2)

    if build_xnT:
        def f_pe2(e):
            if wait_pe is not None:
                for ws_, wc_ in wait_pe:
                    e.wait_ge(sem[ws_], wc_)
            for tt in range(nt):
                e.wait_ge(sem[sv], bv + tt * per_tt + 1)
                for i in range(NI):
                    k = tt * NI + i
                    if k >= 2:
                        e.wait_ge(sem[sc2], bc2 + k - 1)
                    cnt[sp] += 1
                    e.transpose(ptrans[:, (k % 2) * 1024:(k % 2) * 1024 + 128],
                                xn[:, (tt % lnb.nxn) * D + i * 128:
                                   (tt % lnb.nxn) * D + (i + 1) * 128],
                                ident[:, :]).then_inc(sem[sp], 1)
        block.tensor(f_pe2)

    if build_xnT:
        return sc2, cnt[sc2]
    return sv, cnt[sv]


def emit_fproj(ctx, block, tag, *, w_dram, nko, kt, rhs_slice, n_tc,
               tc_width=512, extras=None, wait_tensor=(), evac_mode,
               ident=None, ptrans=None,
               x_in_dram=None, x_out_dram=None, res_scale=1.0,
               evac_bias=None, out_sb=None, act=None, out_dram_bf=None,
               wait_gpsimd=None, wait_sync_first=None, wait_scalar_first=None,
               scoped=False, shared_bufs=None):
    sw0, sw1, sp, sc = tag + "w0", tag + "w1", tag + "p", tag + "c"
    for s_ in (sw0, sw1, sp, sc):
        ctx.new_sem(s_)
    sem, cnt = ctx.sem, ctx.cnt
    K = kt * 128
    TW = n_tc * tc_width
    fscope = ExitStack()
    nc_ = ctx.nc
    if shared_bufs is not None:
        w_sb, pss = shared_bufs[0], shared_bufs[1]
    else:
        w_sb = fscope.enter_context(nc_.sbuf_tensor(tag + "_w", [128, 2 * K], BF16))
        pss = fscope.enter_context(nc_.psum_tensor(tag + "_ps", [128, 2 * TW], F32))

    def _wl(w):
        if w is None:
            return []
        if isinstance(w, tuple):
            return [w]
        return list(w)

    def f_g(e):
        for ws_, wc_ in _wl(wait_gpsimd):
            e.wait_ge(sem[ws_], wc_)
        for ot in range(nko):
            if ot >= 2:
                e.wait_ge(sem[sc], ot - 1)
            swp = sw0 if ot % 2 == 0 else sw1
            cnt[swp] += 16
            e.dma_start(
                out=w_sb[:, (ot % 2) * K:(ot % 2 + 1) * K],
                in_=w_dram.ap()[ot],
            ).then_inc(sem[swp], 16)
    block.gpsimd(f_g)

    if evac_mode in ("resadd", "dram_bf"):
        if shared_bufs is not None:
            obf = shared_bufs[2]
        else:
            obf = fscope.enter_context(nc_.sbuf_tensor(tag + "_obf", [128, 2 * TW], BF16))
    if evac_mode == "resadd":
        st_, sv_ = tag + "t", tag + "v"
        for s_ in (st_, sv_):
            ctx.new_sem(s_)
        ntt = TW // 128
        sxs = [tag + "x%d" % q for q in range(3)]
        sos = [tag + "o%d" % q for q in range(3)]
        for s_ in sxs + sos:
            ctx.new_sem(s_)
        if shared_bufs is not None:
            xsl = shared_bufs[3]
        else:
            xsl = fscope.enter_context(
                nc_.sbuf_tensor(tag + "_xsl", [128, 3 * ntt * 128], F32))
    if evac_mode == "dram_bf":
        sos2 = [tag + "o0", tag + "o1"]
        for s_ in sos2:
            ctx.new_sem(s_)

    def _emit_transposes(e, ot):
        e.wait_ge(sem[sc], ot + 1)
        for tt in range(ntt):
            k = ot * ntt + tt
            if k >= 2:
                e.wait_ge(sem[sv_], k - 1)
            cnt[st_] += 1
            e.transpose(ptrans[:, (k % 2) * 1024:(k % 2) * 1024 + 128],
                        obf[:, (ot % 2) * TW + tt * 128:(ot % 2) * TW + (tt + 1) * 128],
                        ident[:, :]).then_inc(sem[st_], 1)

    def f_t(e):
        for s_, c_ in wait_tensor:
            e.wait_ge(sem[s_], c_)
        for ot in range(nko):
            e.wait_ge(sem[sw0 if ot % 2 == 0 else sw1], 16 * (ot // 2 + 1))
            if ot >= 2:
                e.wait_ge(sem[sc], ot - 1)
            for tc in range(n_tc):
                pslice = pss[:, (ot % 2) * TW + tc * tc_width:
                             (ot % 2) * TW + (tc + 1) * tc_width]
                for i in range(kt):
                    mm = e.matmul(pslice,
                                  w_sb[:, (ot % 2) * K + i * 128:(ot % 2) * K + (i + 1) * 128],
                                  rhs_slice(i, tc),
                                  start=(i == 0),
                                  stop=(i == kt - 1 and extras is None))
                if extras is not None:
                    mm = extras(e, ot, tc, pslice)
            cnt[sp] += 1
            mm.then_inc(sem[sp], 1)
            if evac_mode == "resadd" and ot >= 1:
                _emit_transposes(e, ot - 1)
        if evac_mode == "resadd":
            _emit_transposes(e, nko - 1)
    block.tensor(f_t)

    if evac_mode == "resadd":
        def f_sc_(e):
            for ws_, wc_ in _wl(wait_scalar_first):
                e.wait_ge(sem[ws_], wc_)
            for ot in range(nko):
                e.wait_ge(sem[sp], ot + 1)
                if ot >= 2:
                    e.wait_ge(sem[st_], (ot - 1) * ntt)
                cnt[sc] += 1
                if evac_bias is not None:
                    e.activation(obf[:, (ot % 2) * TW:(ot % 2 + 1) * TW],
                                 pss[:, (ot % 2) * TW:(ot % 2 + 1) * TW],
                                 AF.Identity, bias=evac_bias(ot)).then_inc(sem[sc], 1)
                else:
                    e.copy(obf[:, (ot % 2) * TW:(ot % 2 + 1) * TW],
                           pss[:, (ot % 2) * TW:(ot % 2 + 1) * TW]
                           ).then_inc(sem[sc], 1)
        block.scalar(f_sc_)

        def f_v(e):
            for k in range(nko * ntt):
                g, tt = k // ntt, k % ntt
                e.wait_ge(sem[st_], k + 1)
                e.wait_ge(sem[sxs[g % 3]], 16 * ntt * (g // 3 + 1))
                xs = xsl[:, ((g % 3) * ntt + tt) * 128:((g % 3) * ntt + tt + 1) * 128]
                cnt[sv_] += 1
                e.scalar_tensor_tensor(out=xs,
                                       in0=ptrans[:, (k % 2) * 1024:(k % 2) * 1024 + 128],
                                       scalar=res_scale,
                                       in1=xs,
                                       op0=ALU.mult, op1=ALU.add).then_inc(sem[sv_], 1)
        block.vector(f_v)

        def f_sxo(e):
            for ws_, wc_ in _wl(wait_sync_first):
                e.wait_ge(sem[ws_], wc_)
            for g in range(nko + 2):
                if g < nko:
                    if g >= 3:
                        e.wait_ge(sem[sos[g % 3]],
                                  16 * ntt * ((g - 3) // 3 + 1))
                    for tt in range(ntt):
                        cnt[sxs[g % 3]] += 16
                        e.dma_start(out=xsl[:, ((g % 3) * ntt + tt) * 128:
                                            ((g % 3) * ntt + tt + 1) * 128],
                                    in_=x_in_dram[tt * 128:(tt + 1) * 128,
                                                  g * 128:(g + 1) * 128]
                                    ).then_inc(sem[sxs[g % 3]], 16)
                if g >= 2:
                    go = g - 2
                    for tt in range(ntt):
                        ko = go * ntt + tt
                        e.wait_ge(sem[sv_], ko + 1)
                        cnt[sos[go % 3]] += 16
                        e.dma_start(out=x_out_dram[tt * 128:(tt + 1) * 128,
                                                   go * 128:(go + 1) * 128],
                                    in_=xsl[:, ((go % 3) * ntt + tt) * 128:
                                            ((go % 3) * ntt + tt + 1) * 128]
                                    ).then_inc(sem[sos[go % 3]], 16)
        block.sync(f_sxo)
        if scoped:
            fscope.close()
        else:
            ctx.es.enter_context(fscope.pop_all())
        return [(s_, cnt[s_]) for s_ in sos]

    elif evac_mode == "sbuf":
        def f_sc_(e):
            for ot in range(nko):
                e.wait_ge(sem[sp], ot + 1)
                cnt[sc] += 1
                e.activation(out_sb[:, ot * TW:(ot + 1) * TW],
                             pss[:, (ot % 2) * TW:(ot % 2 + 1) * TW],
                             act if act is not None else AF.Copy).then_inc(sem[sc], 1)
        block.scalar(f_sc_)
        if scoped:
            fscope.close()
        else:
            ctx.es.enter_context(fscope.pop_all())
        return sc, cnt[sc]

    elif evac_mode == "dram_bf":
        def f_sc_(e):
            for ot in range(nko):
                e.wait_ge(sem[sp], ot + 1)
                if ot >= 2:
                    e.wait_ge(sem[sos2[ot % 2]], 16 * ((ot - 2) // 2 + 1))
                cnt[sc] += 1
                e.activation(obf[:, (ot % 2) * TW:(ot % 2 + 1) * TW],
                             pss[:, (ot % 2) * TW:(ot % 2 + 1) * TW],
                             act if act is not None else AF.Copy).then_inc(sem[sc], 1)
        block.scalar(f_sc_)

        def f_so(e):
            for ot in range(nko):
                e.wait_ge(sem[sc], ot + 1)
                cnt[sos2[ot % 2]] += 16
                e.dma_start(out=out_dram_bf[ot * 128:(ot + 1) * 128, :],
                            in_=obf[:, (ot % 2) * TW:(ot % 2 + 1) * TW]
                            ).then_inc(sem[sos2[ot % 2]], 16)
        block.sync(f_so)
        if scoped:
            fscope.close()
        else:
            ctx.es.enter_context(fscope.pop_all())
        return [(s_, cnt[s_]) for s_ in sos2]
    raise ValueError(evac_mode)


def emit_l0ffn(ctx, block, pfx, *, x_in, x_out, wbT, MTc, c0, m0, c1, m1, encT_d,
               bb_d, ident, ptrans, lnb, build_G, G_sb, g_dram_in=None,
               g_store=None, wait_sync=None, level=9, xnT_shared=None,
               psX=None, wait_vec=None, wait_gpsimd0=None, fscoped=False,
               fshared=None, fgates=None):
    nc = ctx.nc
    sem, cnt = ctx.sem, ctx.cnt
    xnT = xnT_shared if xnT_shared is not None else ctx.sb(pfx + "_xnT", [128, NI * TPC], BF16)
    eT = ctx.sb(pfx + "_eT", [41, TPC], BF16)
    encb = ctx.sb(pfx + "_encb", [128, NI * 40], BF16)
    ones = ctx.sb(pfx + "_ones", [1, TPC], BF16)
    psG = (psX[0:40, 0:1024] if psX is not None else None)
    if psG is None:
        psG_t = ctx.ps(pfx + "_psG", [40, 1024], F32)
        psG = psG_t[0:40, 0:1024]
    for s_ in ("ld", "vb", "gm", "pg", "sg", "eo", "gb", "gs"):
        ctx.new_sem(pfx + s_)

    gscope = ExitStack()
    if build_G:
        cwork = gscope.enter_context(nc.sbuf_tensor(pfx + "_cw", [128, 3 * 1024 + 3 * 256], F32))
        mwork = gscope.enter_context(nc.sbuf_tensor(pfx + "_mw", [128, 3 * 1024 + 3 * 256], U8))
        w01 = gscope.enter_context(nc.sbuf_tensor(pfx + "_w01", [128, 1024 + 256 + 1024], F32))
        wsum = gscope.enter_context(nc.sbuf_tensor(pfx + "_ws", [128, NI * 40], BF16))
        mtc = gscope.enter_context(nc.sbuf_tensor(pfx + "_mtc", [128, NI * 512], BF16))
    encf_tmp = gscope.enter_context(nc.sbuf_tensor(pfx + "_encf", [128, NI * 40], F32))

    def f_ld0(e):
        def dd(o, i_):
            cnt[pfx + "ld"] += 16
            e.dma_start(out=o, in_=i_).then_inc(sem[pfx + "ld"], 16)
        if build_G:
            for ee in range(3):
                dd(cwork[:, ee * 1024:(ee + 1) * 1024], c0.ap()[ee])
                dd(mwork[:, ee * 1024:(ee + 1) * 1024], m0.ap()[ee])
                dd(cwork[:, 3072 + ee * 256:3072 + (ee + 1) * 256], c1.ap()[ee])
                dd(mwork[:, 3072 + ee * 256:3072 + (ee + 1) * 256], m1.ap()[ee])
        else:
            dd(G_sb[:, :], g_dram_in[:, :])
        dd(encf_tmp[:, :], encT_d[:, :])
    block.sync(f_ld0)

    if build_G:
        def f_gb(e):
            cnt[pfx + "gb"] += 16
            e.dma_start(out=G_sb[40:41, :], in_=bb_d[:, :]).then_inc(sem[pfx + "gb"], 16)
        block.gpsimd(f_gb)

    def f_vb(e):
        vv = pfx + "vb"
        e.wait_ge(sem[pfx + "ld"], cnt[pfx + "ld"])
        if build_G:
            e.wait_ge(sem[pfx + "gb"], 16)
        ctx.serm(e.memset(ones[:, :], 1.0), vv)
        if build_G:
            for ee in range(3):
                ctx.serw(e, vv)
                ctx.serm(e.tensor_copy(w01[:, 1280:2304],
                                       mwork[:, ee * 1024:(ee + 1) * 1024]), vv)
                ctx.serw(e, vv)
                if ee == 0:
                    ctx.serm(e.tensor_mul(w01[:, 0:1024], cwork[:, 0:1024],
                                          w01[:, 1280:2304]), vv)
                else:
                    ctx.serm(e.tensor_mul(w01[:, 1280:2304],
                                          cwork[:, ee * 1024:(ee + 1) * 1024],
                                          w01[:, 1280:2304]), vv)
                    ctx.serw(e, vv)
                    ctx.serm(e.tensor_add(w01[:, 0:1024], w01[:, 0:1024],
                                          w01[:, 1280:2304]), vv)
            for ee in range(3):
                ctx.serw(e, vv)
                ctx.serm(e.tensor_copy(w01[:, 1280:1536],
                                       mwork[:, 3072 + ee * 256:3072 + (ee + 1) * 256]), vv)
                ctx.serw(e, vv)
                if ee == 0:
                    ctx.serm(e.tensor_mul(w01[:, 1024:1280], cwork[:, 3072:3328],
                                          w01[:, 1280:1536]), vv)
                else:
                    ctx.serm(e.tensor_mul(w01[:, 1280:1536],
                                          cwork[:, 3072 + ee * 256:3072 + (ee + 1) * 256],
                                          w01[:, 1280:1536]), vv)
                    ctx.serw(e, vv)
                    ctx.serm(e.tensor_add(w01[:, 1024:1280], w01[:, 1024:1280],
                                          w01[:, 1280:1536]), vv)
            for j in range(NI):
                ctx.serw(e, vv)
                ctx.serm(e.tensor_copy(wsum[:, j * 40:j * 40 + 32],
                                       w01[:, j * 32:(j + 1) * 32]), vv)
                ctx.serm(e.tensor_copy(wsum[:, j * 40 + 32:j * 40 + 40],
                                       w01[:, 1024 + j * 8:1024 + (j + 1) * 8]), vv)
        ctx.serm(e.tensor_copy(encb[:, :], encf_tmp[:, :]), vv)
    block.vector(f_vb)
    vb_done = cnt[pfx + "vb"]

    def f_ones(e):
        e.wait_ge(sem[pfx + "vb"], vb_done)
        cnt[pfx + "eo"] += 16
        e.dma_start(out=eT[40:41, :], in_=ones[:, :]).then_inc(sem[pfx + "eo"], 16)
    block.sync(f_ones)

    if build_G:
        def f_gm(e):
            for oc in range(8):
                if oc >= 1:
                    e.wait_ge(sem[pfx + "pg"], oc)
                cnt[pfx + "gm"] += 16
                e.dma_start(out=mtc[:, :],
                            in_=MTc.ap()[oc]).then_inc(sem[pfx + "gm"], 16)
        block.gpsimd(f_gm)

        def f_pg(e):
            e.wait_ge(sem[pfx + "vb"], vb_done)
            for oc in range(8):
                e.wait_ge(sem[pfx + "gm"], 16 * (oc + 1))
                if oc >= 2:
                    e.wait_ge(sem[pfx + "sg"], oc - 1)
                for j in range(NI):
                    mm = e.matmul(psG[:, (oc % 2) * 512:(oc % 2 + 1) * 512],
                                  wsum[:, j * 40:(j + 1) * 40],
                                  mtc[:, j * 512:(j + 1) * 512],
                                  start=(j == 0), stop=(j == NI - 1))
                cnt[pfx + "pg"] += 1
                mm.then_inc(sem[pfx + "pg"], 1)
        block.tensor(f_pg)

        def f_sg(e):
            for oc in range(8):
                e.wait_ge(sem[pfx + "pg"], oc + 1)
                cnt[pfx + "sg"] += 1
                e.activation(G_sb[0:40, oc * 512:(oc + 1) * 512],
                             psG[:, (oc % 2) * 512:(oc % 2 + 1) * 512],
                             AF.Copy, scale=0.5).then_inc(sem[pfx + "sg"], 1)
        block.scalar(f_sg)

        if g_store is not None:
            def f_gst(e):
                e.wait_ge(sem[pfx + "sg"], cnt[pfx + "sg"])
                e.wait_ge(sem[pfx + "gb"], 16)
                cnt[pfx + "gs"] += 16
                e.dma_start(out=g_store[:, :], in_=G_sb[:, :]).then_inc(sem[pfx + "gs"], 16)
            block.sync(f_gst)
    g_groups = cnt[pfx + "pg"]
    g_done = cnt[pfx + "sg"]
    if level <= 1:
        gscope.close()
        def f_l1(e):
            e.wait_ge(sem[pfx + "gs"], cnt[pfx + "gs"])
            e.wait_ge(sem[pfx + "eo"], cnt[pfx + "eo"])
        block.sync(f_l1)
        return [(pfx + "gs", cnt[pfx + "gs"])]

    lns, lnc = emit_ln(ctx, block, pfx + "n", x_in, xnT, ident, lnb,
                       build_xnT=True, ptrans=ptrans, wait_sync=wait_sync,
                       wait_vec=wait_vec)

    if level <= 2:
        gscope.close()
        def f_l2(e):
            e.wait_ge(sem[lns], lnc)
            e.wait_ge(sem[pfx + "gs"], cnt[pfx + "gs"])
        block.sync(f_l2)
        return [(pfx + "gs", cnt[pfx + "gs"])]

    def f_pe_eT(e):
        e.wait_ge(sem[lns], lnc)
        e.wait_ge(sem[pfx + "vb"], vb_done)
        if build_G:
            e.wait_ge(sem[pfx + "sg"], g_done)
        for tc in range(2):
            for j in range(NI):
                mm = e.matmul(psG[:, tc * 512:(tc + 1) * 512],
                              encb[:, j * 40:(j + 1) * 40],
                              xnT[:, j * TPC + tc * 512:j * TPC + (tc + 1) * 512],
                              start=(j == 0), stop=(j == NI - 1))
            cnt[pfx + "pg"] += 1
            mm.then_inc(sem[pfx + "pg"], 1)
    block.tensor(f_pe_eT)

    def f_sg_eT(e):
        for tc in range(2):
            e.wait_ge(sem[pfx + "pg"], g_groups + tc + 1)
            cnt[pfx + "sg"] += 1
            e.copy(eT[0:40, tc * 512:(tc + 1) * 512],
                   psG[:, tc * 512:(tc + 1) * 512]).then_inc(sem[pfx + "sg"], 1)
    block.scalar(f_sg_eT)
    eT_done = cnt[pfx + "sg"]

    if level <= 3:
        gscope.close()
        def f_l3(e):
            e.wait_ge(sem[pfx + "sg"], cnt[pfx + "sg"])
            e.wait_ge(sem[pfx + "gs"], cnt[pfx + "gs"])
        block.sync(f_l3)
        return [(pfx + "gs", cnt[pfx + "gs"])]

    gscope.close()

    def extras(e, ot, tc, psl):
        return e.matmul(psl, G_sb[:, ot * 128:(ot + 1) * 128],
                        eT[:, tc * 512:(tc + 1) * 512],
                        start=False, stop=True)

    bias_ready = 16 if build_G else 0
    fg = fgates or {}
    wt = [(lns, lnc), (pfx + "sg", eT_done), (pfx + "eo", 16), (pfx + "gb", bias_ready)]
    wt += fg.get("tensor", [])
    wg = [(pfx + "pg", g_groups)] if wait_gpsimd0 is None else [wait_gpsimd0]
    wg += fg.get("gpsimd", [])
    wsf = [(pfx + "vb", vb_done)] + fg.get("sync", [])
    return emit_fproj(ctx, block, pfx + "f",
                      w_dram=wbT, nko=NI, kt=NI,
                      rhs_slice=lambda i, tc: xnT[:, i * TPC + tc * 512:i * TPC + (tc + 1) * 512],
                      n_tc=2, extras=extras,
                      wait_tensor=tuple(wt),
                      evac_mode="resadd", ident=ident, ptrans=ptrans,
                      x_in_dram=x_in, x_out_dram=x_out, res_scale=0.5,
                      wait_gpsimd=wg,
                      wait_sync_first=wsf, scoped=fscoped,
                      shared_bufs=fshared, wait_scalar_first=fg.get("scalar", []))


def build_launchA(level=9):
    ctx = new_ctx("lA")
    nc, es = ctx.nc, ctx.es

    x0 = nc.dram_tensor("x0", [TPC, D], F32, kind="ExternalInput")
    wbT = nc.dram_tensor("wbT", [NI, 128, D], F32, kind="ExternalInput")
    bb = nc.dram_tensor("bb", [1, D], F32, kind="ExternalInput")
    encT = nc.dram_tensor("encT", [128, NI * 40], F32, kind="ExternalInput")
    MTc = nc.dram_tensor("MTc", [8, 128, NI * 512], BF16, kind="ExternalInput")
    c0 = nc.dram_tensor("c0", [3, 128, 1024], F32, kind="ExternalInput")
    m0 = nc.dram_tensor("m0", [3, 128, 1024], U8, kind="ExternalInput")
    c1 = nc.dram_tensor("c1", [3, 128, 256], F32, kind="ExternalInput")
    m1 = nc.dram_tensor("m1", [3, 128, 256], U8, kind="ExternalInput")
    identD = nc.dram_tensor("identD", [128, 128], BF16, kind="ExternalInput")

    x1 = nc.dram_tensor("x1", [TPC, D], F32, kind="ExternalOutput")
    xn2 = nc.dram_tensor("xn2", [TPC, D], BF16, kind="ExternalOutput")
    g_out = nc.dram_tensor("g_out", [41, D], BF16, kind="ExternalOutput")

    with ctx.es:
        block = es.enter_context(nc.Block())
        ident = ctx.sb("ident", [128, 128], BF16)
        ptrans = ctx.ps("ptrans", [128, 2048], BF16)
        G_sb = ctx.sb("G_sb", [41, D], BF16)
        lnb = LnBufs(ctx)
        ctx.new_sem("ii")

        def f_ident(e):
            ctx.cnt["ii"] += 16
            e.dma_start(out=ident[:, :], in_=identD[:, :]).then_inc(ctx.sem["ii"], 16)
        block.sync(f_ident)

        def f_identw(e):
            e.wait_ge(ctx.sem["ii"], 16)
        block.tensor(f_identw)

        fin_list = emit_l0ffn(ctx, block, "a", x_in=x0, x_out=x1, wbT=wbT, MTc=MTc,
                              c0=c0, m0=m0, c1=c1, m1=m1, encT_d=encT, bb_d=bb,
                              ident=ident, ptrans=ptrans, lnb=lnb,
                              build_G=True, G_sb=G_sb, g_store=g_out, level=level)

        if level >= 5:
            emit_ln(ctx, block, "z", x1, None, ident, lnb, build_xnT=False,
                    xn_bf_store=xn2, wait_sync=fin_list)

        def f_fin(e):
            if level >= 5:
                e.wait_ge(ctx.sem["zo0"], ctx.cnt["zo0"])
                e.wait_ge(ctx.sem["zo1"], ctx.cnt["zo1"])
            for s_, c_ in fin_list:
                e.wait_ge(ctx.sem[s_], c_)
            e.wait_ge(ctx.sem["ags"], ctx.cnt["ags"])
        block.sync(f_fin)

    return ctx.nc




def emit_l1ffn(ctx, block, pfx, *, x_in, x_out, l1wT, wqkhD, BmD, bbD,
               ident, ptrans, lnb, xnT_shared, psX, wait_sync=None,
               wait_vec=None, fshared=None, fgates=None):
    nc = ctx.nc
    sem, cnt = ctx.sem, ctx.cnt
    xnT = xnT_shared
    wqkh = ctx.sb(pfx + "_wqkh", [128, NI * 640], BF16)
    Bm_sb = ctx.sb(pfx + "_Bm", [128, D], BF16)
    bbP = ctx.sb(pfx + "_bbP", [128, NI], F32)
    qkh_f = ctx.sb(pfx + "_qkh", [128, 640], F32)
    s8 = ctx.sb(pfx + "_s8", [128, 16], F32)
    w8 = ctx.sb(pfx + "_w8", [128, 8], F32)
    wh_f = ctx.sb(pfx + "_whf", [128, 128], F32)
    wh_b = ctx.sb(pfx + "_whb", [128, 128], BF16)
    whT = ctx.sb(pfx + "_whT", [128, NT * 128], BF16)
    psQ = psX[0:128, 0:640]
    fg = fgates or {}
    for s_ in ("ld", "qp", "qs", "qv"):
        ctx.new_sem(pfx + s_)

    def f_ld(e):
        for ws_, wc_ in fg.get("l1ld", []):
            e.wait_ge(sem[ws_], wc_)
        for (o, i_) in ((wqkh[:, :], wqkhD[:, :]), (Bm_sb[:, :], BmD[:, :]),
                        (bbP[:, :], bbD[:, :])):
            cnt[pfx + "ld"] += 16
            e.dma_start(out=o, in_=i_).then_inc(sem[pfx + "ld"], 16)
    block.gpsimd(f_ld)

    lns, lnc = emit_ln(ctx, block, pfx + "n", x_in, xnT, ident, lnb,
                       build_xnT=True, ptrans=ptrans, wait_sync=wait_sync,
                       wait_vec=wait_vec, wait_pe=(fgates or {}).get("lnpe"))

    bqp, bqs, bqv = cnt[pfx + "qp"], cnt[pfx + "qs"], cnt[pfx + "qv"]
    grp = [bqp + 2 * t + 1 for t in range(NT)]
    trq = [bqp + 2 * t + 2 for t in range(NT)]
    ev = [bqs + 3 * t + 1 for t in range(NT)]
    ex = [bqs + 3 * t + 2 for t in range(NT)]
    wc = [bqs + 3 * t + 3 for t in range(NT)]
    vneg = [bqv + 7 * t + 4 for t in range(NT)]
    vwh = [bqv + 7 * t + 7 for t in range(NT)]

    def f_pe(e):
        e.wait_ge(sem[lns], lnc)
        e.wait_ge(sem[pfx + "ld"], 48)
        for ws_, wc_ in fg.get("l1pe", []):
            e.wait_ge(sem[ws_], wc_)
        for tt in range(NT):
            if tt >= 1:
                e.wait_ge(sem[pfx + "qs"], ev[tt - 1])
            for i in range(NI):
                e.matmul(psQ[:, 0:512],
                         xnT[:, i * TPC + tt * 128:i * TPC + (tt + 1) * 128],
                         wqkh[:, i * 640:i * 640 + 512],
                         start=(i == 0), stop=(i == NI - 1))
                mm = e.matmul(psQ[:, 512:640],
                              xnT[:, i * TPC + tt * 128:i * TPC + (tt + 1) * 128],
                              wqkh[:, i * 640 + 512:(i + 1) * 640],
                              start=(i == 0), stop=(i == NI - 1))
            cnt[pfx + "qp"] += 1
            mm.then_inc(sem[pfx + "qp"], 1)
            e.wait_ge(sem[pfx + "qv"], vwh[tt])
            if tt >= 2:
                e.wait_ge(sem[pfx + "qs"], wc[tt - 2])
            cnt[pfx + "qp"] += 1
            e.transpose(ptrans[:, (tt % 2) * 1024:(tt % 2) * 1024 + 128],
                        wh_b[:, :], ident[:, :]).then_inc(sem[pfx + "qp"], 1)
    block.tensor(f_pe)

    def f_sc(e):
        for tt in range(NT):
            e.wait_ge(sem[pfx + "qp"], grp[tt])
            if tt >= 1:
                e.wait_ge(sem[pfx + "qv"], vwh[tt - 1])
            cnt[pfx + "qs"] += 1
            e.copy(qkh_f[:, :], psQ[:, :]).then_inc(sem[pfx + "qs"], 1)
            e.wait_ge(sem[pfx + "qv"], vneg[tt])
            cnt[pfx + "qs"] += 1
            e.activation(w8[:, :], s8[:, 0:8], AF.Exp, bias=s8[:, 8:9],
                         accum_out=s8[:, 9:10]).then_inc(sem[pfx + "qs"], 1)
            e.wait_ge(sem[pfx + "qp"], trq[tt])
            cnt[pfx + "qs"] += 1
            e.copy(whT[:, tt * 128:(tt + 1) * 128],
                   ptrans[:, (tt % 2) * 1024:(tt % 2) * 1024 + 128]
                   ).then_inc(sem[pfx + "qs"], 1)
    block.scalar(f_sc)

    ISQ32 = 1.0 / math.sqrt(32.0)

    def f_ve(e):
        for tt in range(NT):
            e.wait_ge(sem[pfx + "qs"], ev[tt])
            ctx.serm(e.tensor_mul(qkh_f[:, 0:256], qkh_f[:, 0:256], qkh_f[:, 256:512]),
                     pfx + "qv")
            ctx.serw(e, pfx + "qv")
            ctx.serm(e.reduce_sum(s8[:, 0:8],
                                  qkh_f[:, 0:256].rearrange("p (e r) -> p e r", e=8),
                                  axis=AX.X), pfx + "qv")
            ctx.serw(e, pfx + "qv")
            ctx.serm(e.tensor_scalar(out=s8[:, 0:8], in0=s8[:, 0:8], scalar1=ISQ32,
                                     scalar2=None, op0=ALU.mult), pfx + "qv")
            ctx.serw(e, pfx + "qv")
            ctx.serm(e.reduce_max(s8[:, 8:9], s8[:, 0:8], axis=AX.X, negate=True),
                     pfx + "qv")
            e.wait_ge(sem[pfx + "qs"], ex[tt])
            ctx.serm(e.reciprocal(s8[:, 10:11], s8[:, 9:10]), pfx + "qv")
            ctx.serw(e, pfx + "qv")
            ctx.serm(e.tensor_mul(wh_f.ap().rearrange("p (e r) -> p e r", e=8),
                                  qkh_f[:, 512:640].rearrange("p (e r) -> p e r", e=8),
                                  w8.ap().unsqueeze(2).broadcast_to([128, 8, 16])),
                     pfx + "qv")
            ctx.serw(e, pfx + "qv")
            ctx.serm(e.tensor_scalar(out=wh_b[:, :], in0=wh_f[:, :],
                                     scalar1=s8[:, 10:11], scalar2=None,
                                     op0=ALU.mult), pfx + "qv")
            assert cnt[pfx + "qv"] == vwh[tt]
    block.vector(f_ve)

    def extras(e, ot, tc, psl):
        return e.matmul(psl, Bm_sb[:, ot * 128:(ot + 1) * 128],
                        whT[:, tc * 512:(tc + 1) * 512], start=False, stop=True)

    wt = [(lns, lnc), (pfx + "qs", wc[NT - 1]), (pfx + "ld", 48)]
    wt += fg.get("tensor", [])
    return emit_fproj(ctx, block, pfx + "f",
                      w_dram=l1wT, nko=NI, kt=NI,
                      rhs_slice=lambda i, tc: xnT[:, i * TPC + tc * 512:i * TPC + (tc + 1) * 512],
                      n_tc=2, extras=extras, wait_tensor=tuple(wt),
                      evac_mode="resadd", ident=ident, ptrans=ptrans,
                      x_in_dram=x_in, x_out_dram=x_out, res_scale=0.5,
                      wait_gpsimd=fg.get("gpsimd", []),
                      wait_sync_first=fg.get("sync", []),
                      wait_scalar_first=list(fg.get("scalar", []))
                      + [(pfx + "ld", 48)],
                      evac_bias=lambda ot: bbP[:, ot:ot + 1],
                      shared_bufs=fshared)


def build_launchC():
    ctx = new_ctx("lC")
    nc, es = ctx.nc, ctx.es
    sem, cnt = ctx.sem, ctx.cnt

    x1 = nc.dram_tensor("x1", [TPC, D], F32, kind="ExternalInput")
    attnT = nc.dram_tensor("attnT", [128, NI * TPC], BF16, kind="ExternalInput")
    woT = nc.dram_tensor("woT", [NI, 128, D], F32, kind="ExternalInput")
    g_in = nc.dram_tensor("g_in", [41, D], BF16, kind="ExternalInput")
    wbT = nc.dram_tensor("wbT", [NI, 128, D], F32, kind="ExternalInput")
    encT = nc.dram_tensor("encT", [128, NI * 40], F32, kind="ExternalInput")
    l1wT = nc.dram_tensor("l1wT", [NI, 128, D], F32, kind="ExternalInput")
    wqkhD = nc.dram_tensor("wqkhD", [128, NI * 640], F32, kind="ExternalInput")
    BmD = nc.dram_tensor("BmD", [128, D], F32, kind="ExternalInput")
    bbD = nc.dram_tensor("bbD", [128, NI], F32, kind="ExternalInput")
    identD = nc.dram_tensor("identD", [128, 128], BF16, kind="ExternalInput")

    x4 = nc.dram_tensor("x4", [TPC, D], F32, kind="ExternalOutput")
    xn5 = nc.dram_tensor("xn5", [TPC, D], BF16, kind="ExternalOutput")
    x2 = nc.dram_tensor("x2", [TPC, D], F32)
    x3 = nc.dram_tensor("x3", [TPC, D], F32)

    with ctx.es:
        block = es.enter_context(nc.Block())
        ident = ctx.sb("ident", [128, 128], BF16)
        ptrans = ctx.ps("ptrans", [128, 2048], BF16)
        G_sb = ctx.sb("G_sb", [41, D], BF16)
        rhs_sb = ctx.sb("rhs_sb", [128, NI * TPC], BF16)
        lnb = LnBufs(ctx, xa2=False)
        psX = ctx.ps("psX", [128, 1024], F32)
        fw = ctx.sb("fw", [128, 2 * D], BF16)
        fps = ctx.ps("fps", [128, 2 * 1024], F32)
        fobf = ctx.sb("fobf", [128, 2 * 1024], BF16)
        fxsl = ctx.sb("fxsl", [128, 3 * 8 * 128], F32)
        fshared = (fw, fps, fobf, fxsl)
        for s_ in ("ii", "ax"):
            ctx.new_sem(s_)

        def f_ii(e):
            cnt["ii"] += 16
            e.dma_start(out=ident[:, :], in_=identD[:, :]).then_inc(sem["ii"], 16)
            cnt["ax"] += 16
            e.dma_start(out=rhs_sb[:, :], in_=attnT[:, :]).then_inc(sem["ax"], 16)
        block.sync(f_ii)

        def f_iiw(e):
            e.wait_ge(sem["ii"], 16)
        block.tensor(f_iiw)

        st1 = emit_fproj(ctx, block, "o",
                         w_dram=woT, nko=NI, kt=NI,
                         rhs_slice=lambda i, tc: rhs_sb[:, i * TPC + tc * 512:
                                                        i * TPC + (tc + 1) * 512],
                         n_tc=2, extras=None,
                         wait_tensor=(("ax", 16),),
                         evac_mode="resadd", ident=ident, ptrans=ptrans,
                         x_in_dram=x1, x_out_dram=x2, res_scale=1.0,
                         shared_bufs=fshared)

        st2 = emit_l0ffn(ctx, block, "c", x_in=x2, x_out=x3, wbT=wbT, MTc=None,
                         c0=None, m0=None, c1=None, m1=None, encT_d=encT,
                         bb_d=None, ident=ident, ptrans=ptrans, lnb=lnb,
                         build_G=False, G_sb=G_sb, g_dram_in=g_in,
                         wait_sync=st1, xnT_shared=rhs_sb, psX=psX,
                         wait_vec=[("op", NI)],
                         fshared=fshared,
                         fgates={"tensor": [("oc", NI)],
                                 "gpsimd": [("op", NI)],
                                 "sync": st1,
                                 "scalar": [("ot", NI * NT)],
                                 "lnpe": [("ov", NI * NT)]})

        st3 = emit_l1ffn(ctx, block, "d", x_in=x3, x_out=x4, l1wT=l1wT,
                         wqkhD=wqkhD, BmD=BmD, bbD=bbD, ident=ident,
                         ptrans=ptrans, lnb=lnb, xnT_shared=rhs_sb, psX=psX,
                         wait_sync=st2, wait_vec=[("cfp", NI)],
                         fshared=fshared,
                         fgates={"tensor": [("cfc", NI)],
                                 "gpsimd": [("cfp", NI)],
                                 "sync": st2,
                                 "scalar": [("cft", NI * NT)],
                                 "lnpe": [("cfv", NI * NT)]})

        emit_ln(ctx, block, "z", x4, None, ident, lnb, build_xnT=False,
                xn_bf_store=xn5, wait_sync=st3)

        def f_fin(e):
            e.wait_ge(sem["zo0"], cnt["zo0"])
            e.wait_ge(sem["zo1"], cnt["zo1"])
            for s_, c_ in st3:
                e.wait_ge(sem[s_], c_)
        block.sync(f_fin)
    return nc

def build_launchB():
    ctx = new_ctx("lB")
    nc, es = ctx.nc, ctx.es
    sem, cnt = ctx.sem, ctx.cnt

    xnT_g = nc.dram_tensor("xnT_g", [4, 128, NI * 1024], BF16, kind="ExternalInput")
    wqkv = nc.dram_tensor("wqkv", [24, 128, D], F32, kind="ExternalInput")
    cosD = nc.dram_tensor("cosD", [128, S], F32, kind="ExternalInput")
    sinD = nc.dram_tensor("sinD", [128, S], F32, kind="ExternalInput")
    identD = nc.dram_tensor("identD", [128, 128], BF16, kind="ExternalInput")
    triD = nc.dram_tensor("triD", [128, 2048], BF16, kind="ExternalInput")
    OT = nc.dram_tensor("OT", [1024, S], BF16, kind="ExternalOutput")
    qkvT = nc.dram_tensor("qkvT", [3072, S], BF16)

    NQ, TQ = 4, 1024

    with ctx.es:
        block = es.enter_context(nc.Block())
        ident = ctx.sb("ident", [128, 128], BF16)
        cosb = ctx.sb("cosb", [128, S], F32)
        sinb = ctx.sb("sinb", [128, S], F32)
        triq4 = ctx.sb("triq4", [128, 4 * 512], BF16)
        ones128 = ctx.sb("ones128", [128, 1], BF16)
        onesF = ctx.sb("onesF", [128, 128], BF16)
        ones1 = ctx.sb("ones1", [1, 128], BF16)
        for s_ in ("ii", "vi"):
            ctx.new_sem(s_)

        def f_consts(e):
            for o, i_ in ((ident, identD), (cosb, cosD), (sinb, sinD),
                          (triq4, triD)):
                cnt["ii"] += 16
                e.dma_start(out=o[:, :], in_=i_[:, :]).then_inc(sem["ii"], 16)
        block.sync(f_consts)

        def f_vinit(e):
            ctx.serm(e.memset(ones128[:, :], 1.0), "vi")
            ctx.serm(e.memset(ones1[:, :], 1.0), "vi")
            ctx.serm(e.memset(onesF[:, :], 1.0), "vi")
        block.vector(f_vinit)
        vi_done = cnt["vi"]

        pscope = ExitStack()
        xnq = pscope.enter_context(nc.sbuf_tensor("xnq", [128, NI * TQ], BF16))
        w_sb = pscope.enter_context(nc.sbuf_tensor("pw", [128, 2 * D], BF16))
        obf = pscope.enter_context(nc.sbuf_tensor("pobf", [128, 2 * TQ], BF16))
        psp = pscope.enter_context(nc.psum_tensor("psp", [128, 2 * TQ], F32))
        for s_ in ("qx", "qw0", "qw1", "qp", "qc", "qo0", "qo1"):
            ctx.new_sem(s_)
        NG = NQ * 24



        def f_qw(e):
            for g in range(NG):
                ot = g % 24
                if g >= 2:
                    e.wait_ge(sem["qc"], g - 1)
                swp = "qw0" if g % 2 == 0 else "qw1"
                cnt[swp] += 16
                e.dma_start(out=w_sb[:, (g % 2) * D:(g % 2 + 1) * D],
                            in_=wqkv.ap()[ot]).then_inc(sem[swp], 16)
        block.gpsimd(f_qw)

        def f_qt(e):
            e.wait_ge(sem["ii"], 64)
            for g in range(NG):
                tq = g // 24
                e.wait_ge(sem["qx"], 16 * (tq + 1))
                e.wait_ge(sem["qw0" if g % 2 == 0 else "qw1"], 16 * (g // 2 + 1))
                if g >= 2:
                    e.wait_ge(sem["qc"], g - 1)
                for tc in range(2):
                    for i in range(NI):
                        mm = e.matmul(psp[:, (g % 2) * TQ + tc * 512:
                                          (g % 2) * TQ + (tc + 1) * 512],
                                      w_sb[:, (g % 2) * D + i * 128:
                                           (g % 2) * D + (i + 1) * 128],
                                      xnq[:, i * TQ + tc * 512:i * TQ + (tc + 1) * 512],
                                      start=(i == 0), stop=(i == NI - 1))
                cnt["qp"] += 1
                mm.then_inc(sem["qp"], 1)
        block.tensor(f_qt)

        def f_qc(e):
            for g in range(NG):
                e.wait_ge(sem["qp"], g + 1)
                if g >= 2:
                    e.wait_ge(sem["qo0" if g % 2 == 0 else "qo1"],
                              16 * ((g - 2) // 2 + 1))
                cnt["qc"] += 1
                e.copy(obf[:, (g % 2) * TQ:(g % 2 + 1) * TQ],
                       psp[:, (g % 2) * TQ:(g % 2 + 1) * TQ]).then_inc(sem["qc"], 1)
        block.scalar(f_qc)

        def _qstore(e, g):
            tq, ot = g // 24, g % 24
            e.wait_ge(sem["qc"], g + 1)
            sop = "qo0" if g % 2 == 0 else "qo1"
            cnt[sop] += 16
            e.dma_start(out=qkvT[ot * 128:(ot + 1) * 128, tq * TQ:(tq + 1) * TQ],
                        in_=obf[:, (g % 2) * TQ:(g % 2 + 1) * TQ]
                        ).then_inc(sem[sop], 16)

        def f_qxs(e):
            for g in range(NG):
                if g % 24 == 0:
                    tq = g // 24
                    if tq >= 1:
                        e.wait_ge(sem["qp"], tq * 24)
                    cnt["qx"] += 16
                    e.dma_start(out=xnq[:, :], in_=xnT_g.ap()[tq]
                                ).then_inc(sem["qx"], 16)
                if g >= 1:
                    _qstore(e, g - 1)
            _qstore(e, NG - 1)
        block.sync(f_qxs)
        proj_done = [("qo0", cnt["qo0"]), ("qo1", cnt["qo1"])]
        pscope.close()

        vT_all = ctx.sb("vT_all", [128, 8 * S], BF16)
        vscope = ExitStack()
        vload = vscope.enter_context(nc.sbuf_tensor("vload", [128, 2 * S], BF16))
        ptr2 = vscope.enter_context(nc.psum_tensor("ptr2", [128, 2048], BF16))
        for s_ in ("vx0", "vx1", "vt", "vc"):
            ctx.new_sem(s_)

        def f_vx(e):
            for s_, c_ in proj_done:
                e.wait_ge(sem[s_], c_)
            for hh in range(HPC):
                if hh >= 2:
                    e.wait_ge(sem["vt"], 32 * (hh - 1))
                sx = "vx0" if hh % 2 == 0 else "vx1"
                cnt[sx] += 16
                e.dma_start(out=vload[:, (hh % 2) * S:(hh % 2 + 1) * S],
                            in_=qkvT[(16 + hh) * 128:(17 + hh) * 128, :]
                            ).then_inc(sem[sx], 16)
        block.sync(f_vx)

        def f_vt(e):
            e.wait_ge(sem["ii"], 64)
            for k in range(32 * HPC):
                hh, kt = k // 32, k % 32
                if kt == 0:
                    e.wait_ge(sem["vx0" if hh % 2 == 0 else "vx1"],
                              16 * (hh // 2 + 1))
                if k >= 2:
                    e.wait_ge(sem["vc"], k - 1)
                cnt["vt"] += 1
                e.transpose(ptr2[:, (k % 2) * 1024:(k % 2) * 1024 + 128],
                            vload[:, (hh % 2) * S + kt * 128:
                                  (hh % 2) * S + (kt + 1) * 128],
                            ident[:, :]).then_inc(sem["vt"], 1)
        block.tensor(f_vt)

        def f_vc(e):
            for k in range(32 * HPC):
                e.wait_ge(sem["vt"], k + 1)
                cnt["vc"] += 1
                e.copy(vT_all[:, k * 128:(k + 1) * 128],
                       ptr2[:, (k % 2) * 1024:(k % 2) * 1024 + 128]
                       ).then_inc(sem["vc"], 1)
        block.scalar(f_vc)
        vc_total = cnt["vc"]
        vt_total = cnt["vt"]
        vscope.close()

        bufs = dict(
            qraw=ctx.sb("qraw", [128, S], BF16),
            kraw=ctx.sb("kraw", [128, S], BF16),
            qr=ctx.sb("qr", [128, S], BF16),
            kr=ctx.sb("kr", [128, S], BF16),
            qsh=ctx.sb("qsh", [128, S], BF16),
            ksh=ctx.sb("ksh", [128, S], BF16),
            P_sb=ctx.sb("P_sb", [128, S], BF16),
            o_sb=ctx.sb("o_sb", [128, 2 * 512], BF16),
            rb_f=ctx.sb("rb_f", [128, 2 * 512], F32),
            rb_t=ctx.sb("rb_t", [128, 512], F32),
            triq4=triq4,
            nrm=ctx.sb("nrm", [1, 24], F32),
            nbias=ctx.sb("nbias", [128, 1], F32),
            nbb=ctx.sb("nbb", [1, 1], BF16),
            psS=ctx.ps("psS", [128, 4 * 512], F32),
            psO=ctx.ps("psO", [128, 2 * 512], F32),
            psNB=ctx.ps("psNB", [128, 2 * 512], F32),
        )
        for s_ in ("hx", "hv", "hs", "hp", "ho0", "ho1"):
            ctx.new_sem(s_)

        for h in range(HPC):
            _emit_B_head(ctx, block, h, qkvT, OT, bufs, cosb, sinb, ident,
                         ones128, onesF, ones1, vT_all,
                         (proj_done + [("vt", vt_total)]) if h == 0 else None,
                         vi_done, vc_total)

        def f_fin(e):
            e.wait_ge(sem["ho0"], cnt["ho0"])
            e.wait_ge(sem["ho1"], cnt["ho1"])
        block.sync(f_fin)
    return nc


def _emit_B_head(ctx, block, h, qkvT, OT, B, cosb, sinb, ident,
                 ones128, onesF, ones1, vT, first_wait, vi_done, vc_total):
    sem, cnt = ctx.sem, ctx.cnt
    bhp, bhs, bhv, bhx = cnt["hp"], cnt["hs"], cnt["hv"], cnt["hx"]
    bho0, bho1 = cnt["ho0"], cnt["ho1"]

    tiles = []
    for qb in range(8):
        for t in range(4 * (qb + 1)):
            tiles.append((qb, t))
    NTILE = len(tiles)

    hp = hs = hv = 0
    pos = {}
    hv += 3; pos["ropeq"] = bhv + hv
    hv += 3; pos["ropek"] = bhv + hv
    hv += 1; pos["qsq"] = bhv + hv
    bm = [0] * 16; bred = [0] * 16
    for b in range(8):
        hp += 1; bm[b] = bhp + hp
        hv += 1; bred[b] = bhv + hv
    hv += 1; pos["qmax"] = bhv + hv
    hv += 1; pos["ksq"] = bhv + hv
    for b in range(8, 16):
        hp += 1; bm[b] = bhp + hp
        hv += 1; bred[b] = bhv + hv
    hv += 1; pos["kmax"] = bhv + hv
    hv += 1; pos["prod"] = bhv + hv
    hs += 1; pos["sqb"] = bhs + hs
    hv += 1; pos["neg"] = bhv + hv
    hv += 1; pos["nbbc"] = bhv + hv
    hp += 1; pos["bcast"] = bhp + hp
    hs += 1; pos["cbias"] = bhs + hs
    smm = [0] * NTILE; summ = [0] * NTILE; avmm = [0] * NTILE
    pe_order = []
    for g in range(min(3, NTILE)):
        pe_order.append(("S", g))
    for g in range(NTILE):
        if g + 3 < NTILE:
            pe_order.append(("S", g + 3))
        pe_order.append(("sum", g))
        pe_order.append(("av", g))
    for kind, g in pe_order:
        hp += 1
        if kind == "S":
            smm[g] = bhp + hp
        elif kind == "sum":
            summ[g] = bhp + hp
        else:
            avmm[g] = bhp + hp
    expp = [0] * NTILE; rcp = [0] * 8; onor = [0] * 8; odiv = [0] * 8
    for g, (qb, t) in enumerate(tiles):
        hs += 1; expp[g] = bhs + hs
        if t == 4 * (qb + 1) - 1:
            hs += 2; rcp[qb] = bhs + hs
            hs += 1; onor[qb] = bhs + hs
    for qb in range(8):
        hv += 1; odiv[qb] = bhv + hv

    def _loads(e, hh):
        for r, t in ((hh, "qraw"), (8 + hh, "kraw")):
            cnt["hx"] += 16
            e.dma_start(out=B[t][:, :], in_=qkvT[r * 128:(r + 1) * 128, :]
                        ).then_inc(sem["hx"], 16)
        e.wait_ge(sem["hx"], cnt["hx"])
        for (dst, srcb) in ((B["qsh"], B["qraw"]), (B["ksh"], B["kraw"])):
            cnt["hx"] += 16
            e.dma_start(out=dst[0:64, :], in_=srcb[64:128, :]).then_inc(sem["hx"], 16)
            cnt["hx"] += 16
            e.dma_start(out=dst[64:128, :], in_=srcb[0:64, :]).then_inc(sem["hx"], 16)

    def f_hx(e):
        if first_wait:
            for s_, c_ in first_wait:
                e.wait_ge(sem[s_], c_)
        _loads(e, h)
    if h == 0:
        block.sync(f_hx)
    loads_done = cnt["hx"]

    def f_hv(e):
        e.wait_ge(sem["hx"], loads_done)
        if h == 0:
            e.wait_ge(sem["vi"], vi_done)
            e.wait_ge(sem["ii"], 64)
        for (dst, srcb, shb) in ((B["qr"], B["qraw"], B["qsh"]),
                                 (B["kr"], B["kraw"], B["ksh"])):
            ctx.serm(e.tensor_mul(dst[:, :], srcb[:, :], cosb[:, :]), "hv")
            ctx.serm(e.tensor_mul(B["P_sb"][:, 0:S], shb[:, :], sinb[:, :]), "hv")
            ctx.serw(e, "hv")
            ctx.serm(e.tensor_add(dst[:, :], dst[:, :], B["P_sb"][:, 0:S]), "hv")
            ctx.serw(e, "hv")
        ctx.serm(e.tensor_mul(B["P_sb"][:, 0:S], B["qr"][:, :], B["qr"][:, :]), "hv")
        for b in range(16):
            e.wait_ge(sem["hp"], bm[b])
            cnt["hv"] += 1
            e.reduce_max(B["nrm"][0:1, b:b + 1],
                         B["psO"][0:1, (b % 2) * 512:(b % 2) * 512 + 512],
                         axis=AX.X).then_inc(sem["hv"], 1)
            if b == 7:
                ctx.serw(e, "hv")
                ctx.serm(e.reduce_max(B["nrm"][0:1, 16:17], B["nrm"][0:1, 0:8],
                                      axis=AX.X), "hv")
                ctx.serw(e, "hv")
                ctx.serm(e.tensor_mul(B["P_sb"][:, 0:S], B["kr"][:, :],
                                      B["kr"][:, :]), "hv")
        ctx.serw(e, "hv")
        ctx.serm(e.reduce_max(B["nrm"][0:1, 17:18], B["nrm"][0:1, 8:16],
                              axis=AX.X), "hv")
        ctx.serw(e, "hv")
        ctx.serm(e.tensor_mul(B["nrm"][0:1, 18:19], B["nrm"][0:1, 16:17],
                              B["nrm"][0:1, 17:18]), "hv")
        e.wait_ge(sem["hs"], pos["sqb"])
        ctx.serm(e.tensor_scalar(out=B["nrm"][0:1, 20:21], in0=B["nrm"][0:1, 19:20],
                                 scalar1=-1.0, scalar2=None, op0=ALU.mult), "hv")
        ctx.serw(e, "hv")
        ctx.serm(e.tensor_copy(B["nbb"][0:1, 0:1], B["nrm"][0:1, 20:21]), "hv")
        assert cnt["hv"] == pos["nbbc"]
        for qb in range(8):
            e.wait_ge(sem["hs"], onor[qb])
            e.wait_ge(sem["hs"], rcp[qb])
            cnt["hv"] += 1
            e.tensor_mul(B["o_sb"][:, (qb % 2) * 512:(qb % 2 + 1) * 512],
                         B["o_sb"][:, (qb % 2) * 512:(qb % 2 + 1) * 512],
                         B["rb_f"][:, (qb % 2) * 512:(qb % 2 + 1) * 512]
                         ).then_inc(sem["hv"], 1)
            assert cnt["hv"] == odiv[qb]
    block.vector(f_hv)

    def f_ht(e):
        if h == 0:
            e.wait_ge(sem["vc"], vc_total)
        else:
            e.wait_ge(sem["hs"], bhs)
        e.wait_ge(sem["hv"], pos["qsq"])
        for b in range(16):
            if b == 8:
                e.wait_ge(sem["hv"], pos["ksq"])
            if b >= 2:
                e.wait_ge(sem["hv"], bred[b - 2])
            cnt["hp"] += 1
            e.matmul(B["psO"][0:1, (b % 2) * 512:(b % 2) * 512 + 512],
                     ones128[:, 0:1],
                     B["P_sb"][:, (b % 8) * 512:(b % 8 + 1) * 512],
                     start=True, stop=True).then_inc(sem["hp"], 1)
            assert cnt["hp"] == bm[b]
        e.wait_ge(sem["hv"], pos["nbbc"])
        cnt["hp"] += 1
        e.matmul(B["psNB"][:, 0:1], ones1[0:1, :], B["nbb"][0:1, 0:1],
                 start=True, stop=True).then_inc(sem["hp"], 1)
        assert cnt["hp"] == pos["bcast"]

        def _emit_S(g):
            qb, t = tiles[g]
            if g >= 4:
                e.wait_ge(sem["hs"], expp[g - 4])
            masked = (t >= 4 * qb)
            mm = e.matmul(B["psS"][:, (g % 4) * 512:(g % 4 + 1) * 512],
                          B["kr"][:, t * 128:(t + 1) * 128],
                          B["qr"][:, qb * 512:(qb + 1) * 512],
                          start=True, stop=not masked)
            if masked:
                s_ = t - 4 * qb
                mm = e.matmul(B["psS"][:, (g % 4) * 512:(g % 4 + 1) * 512],
                              ident[:, :], B["triq4"][:, s_ * 512:(s_ + 1) * 512],
                              start=False, stop=True)
            cnt["hp"] += 1
            mm.then_inc(sem["hp"], 1)
            assert cnt["hp"] == smm[g]

        for g in range(min(3, NTILE)):
            _emit_S(g)
        for g in range(NTILE):
            qb, t = tiles[g]
            nkt = 4 * (qb + 1)
            if g + 3 < NTILE:
                _emit_S(g + 3)
            e.wait_ge(sem["hs"], expp[g])
            if t == 0:
                if qb == 0:
                    e.wait_ge(sem["hs"], pos["cbias"])
                if qb >= 2:
                    e.wait_ge(sem["hs"], rcp[qb - 2])
            cnt["hp"] += 1
            e.matmul(B["psNB"][:, (qb % 2) * 512:(qb % 2 + 1) * 512],
                     onesF[:, :],
                     B["P_sb"][:, (g % 4) * 512:(g % 4 + 1) * 512],
                     start=(t == 0), stop=(t == nkt - 1)).then_inc(sem["hp"], 1)
            assert cnt["hp"] == summ[g]
            if t == 0 and qb >= 2:
                e.wait_ge(sem["hs"], onor[qb - 2])
            cnt["hp"] += 1
            e.matmul(B["psO"][:, (qb % 2) * 512:(qb % 2 + 1) * 512],
                     vT[:, h * S + t * 128:h * S + (t + 1) * 128],
                     B["P_sb"][:, (g % 4) * 512:(g % 4 + 1) * 512],
                     start=(t == 0), stop=(t == nkt - 1)).then_inc(sem["hp"], 1)
            assert cnt["hp"] == avmm[g]
    block.tensor(f_ht)

    def f_hs(e):
        e.wait_ge(sem["hv"], pos["prod"])
        cnt["hs"] += 1
        e.activation(B["nrm"][0:1, 19:20], B["nrm"][0:1, 18:19], AF.Sqrt,
                     scale=1.0 / 128.0).then_inc(sem["hs"], 1)
        assert cnt["hs"] == pos["sqb"]
        e.wait_ge(sem["hp"], pos["bcast"])
        cnt["hs"] += 1
        e.copy(B["nbias"][:, :], B["psNB"][:, 0:1]).then_inc(sem["hs"], 1)
        assert cnt["hs"] == pos["cbias"]
        for g, (qb, t) in enumerate(tiles):
            e.wait_ge(sem["hp"], smm[g])
            if g == 0:
                e.wait_ge(sem["hs"], pos["cbias"])
            if g >= 4:
                e.wait_ge(sem["hp"], avmm[g - 4])
            cnt["hs"] += 1
            e.activation(B["P_sb"][:, (g % 4) * 512:(g % 4 + 1) * 512],
                         B["psS"][:, (g % 4) * 512:(g % 4 + 1) * 512],
                         AF.Exp, bias=B["nbias"][:, 0:1], scale=ISQ
                         ).then_inc(sem["hs"], 1)
            assert cnt["hs"] == expp[g]
            if t == 4 * (qb + 1) - 1:
                e.wait_ge(sem["hp"], summ[g])
                if qb >= 2:
                    e.wait_ge(sem["hv"], odiv[qb - 2])
                cnt["hs"] += 1
                e.activation(B["rb_t"][:, :],
                             B["psNB"][:, (qb % 2) * 512:(qb % 2 + 1) * 512],
                             AF.Ln).then_inc(sem["hs"], 1)
                cnt["hs"] += 1
                e.wait_ge(sem["hs"], rcp[qb] - 1)
                e.activation(B["rb_f"][:, (qb % 2) * 512:(qb % 2 + 1) * 512],
                             B["rb_t"][:, :], AF.Exp,
                             scale=-1.0).then_inc(sem["hs"], 1)
                assert cnt["hs"] == rcp[qb]
                e.wait_ge(sem["hp"], avmm[g])
                if qb >= 2:
                    hosem = "ho0" if qb % 2 == 0 else "ho1"
                    hob = bho0 if qb % 2 == 0 else bho1
                    e.wait_ge(sem[hosem], hob + 16 * ((qb - 2) // 2 + 1))
                elif h > 0:
                    e.wait_ge(sem["ho0" if qb % 2 == 0 else "ho1"],
                              bho0 if qb % 2 == 0 else bho1)
                cnt["hs"] += 1
                e.copy(B["o_sb"][:, (qb % 2) * 512:(qb % 2 + 1) * 512],
                       B["psO"][:, (qb % 2) * 512:(qb % 2 + 1) * 512]
                       ).then_inc(sem["hs"], 1)
                assert cnt["hs"] == onor[qb]
    block.scalar(f_hs)

    def f_ho(e):
        if h + 1 < HPC:
            e.wait_ge(sem["hv"], pos["ropek"])
            _loads(e, h + 1)
        for qb in range(8):
            e.wait_ge(sem["hv"], odiv[qb])
            hop = "ho0" if qb % 2 == 0 else "ho1"
            cnt[hop] += 16
            e.dma_start(out=OT[h * 128:(h + 1) * 128, qb * 512:(qb + 1) * 512],
                        in_=B["o_sb"][:, (qb % 2) * 512:(qb % 2 + 1) * 512]
                        ).then_inc(sem[hop], 16)
    block.sync(f_ho)


def build_launchD():
    ctx = new_ctx("lD")
    nc, es = ctx.nc, ctx.es
    sem, cnt = ctx.sem, ctx.cnt

    xnT_g = nc.dram_tensor("xnT_g", [4, 128, NI * 1024], BF16, kind="ExternalInput")
    wqkv12 = nc.dram_tensor("wqkv12", [24, 128, D], F32, kind="ExternalInput")
    qdecD = nc.dram_tensor("qdecD", [8, 128, 256], F32, kind="ExternalInput")
    kdecD = nc.dram_tensor("kdecD", [128, 16], F32, kind="ExternalInput")
    DdecD = nc.dram_tensor("DdecD", [8, 128, 512], BF16, kind="ExternalInput")
    bdBD = nc.dram_tensor("bdBD", [128, 8], F32, kind="ExternalInput")
    identD = nc.dram_tensor("identD", [128, 128], BF16, kind="ExternalInput")
    OT = nc.dram_tensor("OT", [1024, S], BF16, kind="ExternalOutput")
    qkvT = nc.dram_tensor("qkvT", [3072, S], BF16)

    NQ, TQ = 4, 1024
    NB = 16

    with ctx.es:
        block = es.enter_context(nc.Block())
        ident = ctx.sb("ident", [128, 128], BF16)
        kdec = ctx.sb("kdec", [128, 16], F32)
        bdB = ctx.sb("bdB", [128, 8], F32)
        ctx.new_sem("ii")

        def f_consts(e):
            for o, i_ in ((ident, identD), (kdec, kdecD), (bdB, bdBD)):
                cnt["ii"] += 16
                e.dma_start(out=o[:, :], in_=i_[:, :]).then_inc(sem["ii"], 16)
        block.sync(f_consts)

        pscope = ExitStack()
        xnq = pscope.enter_context(nc.sbuf_tensor("xnq", [128, NI * TQ], BF16))
        w_sb = pscope.enter_context(nc.sbuf_tensor("pw", [128, 2 * D], BF16))
        obf = pscope.enter_context(nc.sbuf_tensor("pobf", [128, 2 * TQ], BF16))
        psp = pscope.enter_context(nc.psum_tensor("psp", [128, 2 * TQ], F32))
        for s_ in ("qx", "qw0", "qw1", "qp", "qc", "qo0", "qo1"):
            ctx.new_sem(s_)
        NG = NQ * 24

        def f_qw(e):
            for g in range(NG):
                ot = g % 24
                if g >= 2:
                    e.wait_ge(sem["qc"], g - 1)
                swp = "qw0" if g % 2 == 0 else "qw1"
                cnt[swp] += 16
                e.dma_start(out=w_sb[:, (g % 2) * D:(g % 2 + 1) * D],
                            in_=wqkv12.ap()[ot]).then_inc(sem[swp], 16)
        block.gpsimd(f_qw)

        def f_qt(e):
            e.wait_ge(sem["ii"], 48)
            for g in range(NG):
                tq = g // 24
                e.wait_ge(sem["qx"], 16 * (tq + 1))
                e.wait_ge(sem["qw0" if g % 2 == 0 else "qw1"], 16 * (g // 2 + 1))
                if g >= 2:
                    e.wait_ge(sem["qc"], g - 1)
                for tc in range(2):
                    for i in range(NI):
                        mm = e.matmul(psp[:, (g % 2) * TQ + tc * 512:
                                          (g % 2) * TQ + (tc + 1) * 512],
                                      w_sb[:, (g % 2) * D + i * 128:
                                           (g % 2) * D + (i + 1) * 128],
                                      xnq[:, i * TQ + tc * 512:i * TQ + (tc + 1) * 512],
                                      start=(i == 0), stop=(i == NI - 1))
                cnt["qp"] += 1
                mm.then_inc(sem["qp"], 1)
        block.tensor(f_qt)

        def f_qc(e):
            for g in range(NG):
                e.wait_ge(sem["qp"], g + 1)
                if g >= 2:
                    e.wait_ge(sem["qo0" if g % 2 == 0 else "qo1"],
                              16 * ((g - 2) // 2 + 1))
                cnt["qc"] += 1
                e.activation(obf[:, (g % 2) * TQ:(g % 2 + 1) * TQ],
                             psp[:, (g % 2) * TQ:(g % 2 + 1) * TQ],
                             AF.Silu).then_inc(sem["qc"], 1)
        block.scalar(f_qc)

        def _qstore(e, g):
            tq, ot = g // 24, g % 24
            e.wait_ge(sem["qc"], g + 1)
            sop = "qo0" if g % 2 == 0 else "qo1"
            cnt[sop] += 16
            e.dma_start(out=qkvT[ot * 128:(ot + 1) * 128, tq * TQ:(tq + 1) * TQ],
                        in_=obf[:, (g % 2) * TQ:(g % 2 + 1) * TQ]
                        ).then_inc(sem[sop], 16)

        def f_qxs(e):
            for g in range(NG):
                if g % 24 == 0:
                    tq = g // 24
                    if tq >= 1:
                        e.wait_ge(sem["qp"], tq * 24)
                    cnt["qx"] += 16
                    e.dma_start(out=xnq[:, :], in_=xnT_g.ap()[tq]
                                ).then_inc(sem["qx"], 16)
                if g >= 1:
                    _qstore(e, g - 1)
            _qstore(e, NG - 1)
        block.sync(f_qxs)
        proj_done = [("qo0", cnt["qo0"]), ("qo1", cnt["qo1"])]
        pscope.close()

        bufs = dict(
            qT=ctx.sb("qT", [128, 2 * S], BF16),
            kT=ctx.sb("kT", [128, 2 * S], BF16),
            vraw=ctx.sb("vraw", [128, 2 * S], BF16),
            v_sb=ctx.sb("v_sb", [128, S], BF16),
            ktk=ctx.sb("ktk", [128, S], BF16),
            qdq=ctx.sb("qdq", [128, 2 * S], BF16),
            qdc=ctx.sb("qdc", [128, 2 * 256], F32),
            Dd=ctx.sb("Dd", [128, 2 * 512], BF16),
            P_b=ctx.sb("P_b", [128, 4 * 256], BF16),
            PT_sb=ctx.sb("PT_sb", [128, 2 * 256], BF16),
            o_sb=ctx.sb("o_sb", [128, 2 * 256], BF16),
            S_f=ctx.sb("S_f", [128, 128], F32),
            S_b=ctx.sb("S_b", [128, 128], BF16),
            psS=ctx.ps("psS", [128, 4 * 512], F32),
            ptr2=ctx.ps("ptr2", [128, 2048], BF16),
            psA=ctx.ps("psA", [128, 256], F32),
            psK=ctx.ps("psK", [128, 128], F32),
        )
        for s_ in ("hx", "hv", "hs", "hp", "ho0", "ho1"):
            ctx.new_sem(s_)

        for h in range(HPC):
            _emit_D_head(ctx, block, h, qkvT, OT, bufs, qdecD, DdecD, kdec, bdB,
                         ident, proj_done if h == 0 else None)

        def f_fin(e):
            e.wait_ge(sem["ho0"], cnt["ho0"])
            e.wait_ge(sem["ho1"], cnt["ho1"])
        block.sync(f_fin)
    return nc


def _emit_D_head(ctx, block, h, qkvT, OT, B, qdecD, DdecD, kdec, bdB, ident,
                 first_wait):
    sem, cnt = ctx.sem, ctx.cnt
    NKT = 32
    NB = 16
    bhp, bhs, bhv, bhx = cnt["hp"], cnt["hs"], cnt["hv"], cnt["hx"]
    bho0, bho1 = cnt["ho0"], cnt["ho1"]
    sl = (h % 2) * S
    sl2 = ((h + 1) % 2) * S

    hp = hs = hv = 0
    vtr = [bhp + i + 1 for i in range(NKT)]; hp = NKT
    ktr = [bhp + NKT + i + 1 for i in range(NKT)]; hp = 2 * NKT
    vev = [bhs + i + 1 for i in range(NKT)]; hs = NKT
    kev = [bhs + NKT + i + 1 for i in range(NKT)]; hs = 2 * NKT
    qdqp = [bhv + i + 1 for i in range(NB)]; hv = NB
    hv += 2
    sz = bhv + hv
    smm = [[0, 0] for _ in range(NB)]
    dml = [[0, 0] for _ in range(NB)]
    ptr_ = [[0] * 4 for _ in range(NB)]
    ptc = [[0] * 4 for _ in range(NB)]
    avm = [[0, 0, 0] for _ in range(NB)]
    kmm = [[0, 0] for _ in range(NB)]
    oevp = [0] * NB
    supd = [0] * NB
    sbcp = [0] * NB
    for blk in (0, 1):
        for qsub in range(2):
            hp += 1; smm[blk][qsub] = bhp + hp
    for blk in range(NB):
        for j in range(4):
            hp += 1; ptr_[blk][j] = bhp + hp
        hp += 1; avm[blk][0] = bhp + hp
        hp += 1; avm[blk][1] = bhp + hp
        hp += 1; kmm[blk][0] = bhp + hp
        hp += 1; kmm[blk][1] = bhp + hp
        if blk + 2 < NB:
            for qsub in range(2):
                hp += 1; smm[blk + 2][qsub] = bhp + hp
        hp += 1; avm[blk][2] = bhp + hp
    for blk in range(NB):
        for qsub in range(2):
            hv += 1; dml[blk][qsub] = bhv + hv
        if blk >= 1:
            hv += 1; supd[blk - 1] = bhv + hv
            hv += 1; sbcp[blk - 1] = bhv + hv
    hv += 1; supd[NB - 1] = bhv + hv
    hv += 1; sbcp[NB - 1] = bhv + hv
    for blk in range(NB):
        for j in range(4):
            hs += 1; ptc[blk][j] = bhs + hs
        hs += 1; oevp[blk] = bhs + hs

    def f_hx(e):
        if first_wait:
            for s_, c_ in first_wait:
                e.wait_ge(sem[s_], c_)
        for r, t in ((h, "qT"), (8 + h, "kT"), (16 + h, "vraw")):
            cnt["hx"] += 16
            e.dma_start(out=B[t][:, sl:sl + S], in_=qkvT[r * 128:(r + 1) * 128, :]
                        ).then_inc(sem["hx"], 16)
        cnt["hx"] += 16
        e.dma_start(out=B["qdc"][:, (h % 2) * 256:(h % 2) * 256 + 256],
                    in_=qdecD.ap()[h]).then_inc(sem["hx"], 16)
        cnt["hx"] += 16
        e.dma_start(out=B["Dd"][:, (h % 2) * 512:(h % 2) * 512 + 512],
                    in_=DdecD.ap()[h]).then_inc(sem["hx"], 16)
    if h == 0:
        block.sync(f_hx)
    loads_done = cnt["hx"]

    def f_hv(e):
        e.wait_ge(sem["hx"], loads_done)
        if h == 0:
            e.wait_ge(sem["ii"], 48)
        else:
            e.wait_ge(sem["hp"], bhp)
        for blk in range(NB):
            cnt["hv"] += 1
            e.tensor_mul(B["qdq"][:, sl + blk * 256:sl + (blk + 1) * 256],
                         B["qT"][:, sl + blk * 256:sl + (blk + 1) * 256],
                         B["qdc"][:, (h % 2) * 256:(h % 2) * 256 + 256]
                         ).then_inc(sem["hv"], 1)
        ctx.serm(e.memset(B["S_f"][:, :], 0.0), "hv")
        ctx.serm(e.memset(B["S_b"][:, :], 0.0), "hv")
        assert cnt["hv"] == sz
        def _emit_state(e, blk):
            e.wait_ge(sem["hp"], kmm[blk][1])
            cnt["hv"] += 1
            e.scalar_tensor_tensor(out=B["S_f"][:, :], in0=B["S_f"][:, :],
                                   scalar=bdB[:, h:h + 1], in1=B["psK"][:, :],
                                   op0=ALU.mult, op1=ALU.add).then_inc(sem["hv"], 1)
            assert cnt["hv"] == supd[blk]
            e.wait_ge(sem["hp"], avm[blk][2])
            cnt["hv"] += 1
            e.tensor_copy(B["S_b"][:, :], B["S_f"][:, :]).then_inc(sem["hv"], 1)
            assert cnt["hv"] == sbcp[blk]

        for blk in range(NB):
            for qsub in range(2):
                e.wait_ge(sem["hp"], smm[blk][qsub])
                if blk >= 2:
                    e.wait_ge(sem["hp"], ptr_[blk - 2][3])
                s4 = (2 * blk + qsub) % 4
                cnt["hv"] += 1
                e.tensor_tensor(B["P_b"][:, s4 * 256:(s4 + 1) * 256],
                                B["psS"][:, s4 * 512:s4 * 512 + 256],
                                B["Dd"][:, (h % 2) * 512 + qsub * 256:
                                        (h % 2) * 512 + (qsub + 1) * 256],
                                op=ALU.mult).then_inc(sem["hv"], 1)
                assert cnt["hv"] == dml[blk][qsub]
            if blk >= 1:
                _emit_state(e, blk - 1)
        _emit_state(e, NB - 1)
    block.vector(f_hv)

    def f_ht(e):
        e.wait_ge(sem["hx"], loads_done)
        if h > 0:
            e.wait_ge(sem["hs"], bhs)
            e.wait_ge(sem["hv"], bhv)
        for kt in range(NKT):
            if kt >= 2:
                e.wait_ge(sem["hs"], vev[kt - 2])
            cnt["hp"] += 1
            e.transpose(B["ptr2"][:, (kt % 2) * 1024:(kt % 2) * 1024 + 128],
                        B["vraw"][:, sl + kt * 128:sl + (kt + 1) * 128],
                        ident[:, :]).then_inc(sem["hp"], 1)
            assert cnt["hp"] == vtr[kt]
        for kt in range(NKT):
            e.wait_ge(sem["hs"], vev[NKT - 1] if kt == 0 else kev[kt - 1])
            if kt == 1:
                e.wait_ge(sem["hs"], vev[NKT - 1])
            if kt >= 2:
                e.wait_ge(sem["hs"], kev[kt - 2])
            cnt["hp"] += 1
            e.transpose(B["ptr2"][:, (kt % 2) * 1024:(kt % 2) * 1024 + 128],
                        B["kT"][:, sl + kt * 128:sl + (kt + 1) * 128],
                        ident[:, :]).then_inc(sem["hp"], 1)
            assert cnt["hp"] == ktr[kt]

        def _emit_scores(e, blk):
            for qsub in range(2):
                if blk >= 2:
                    e.wait_ge(sem["hv"], dml[blk - 2][qsub])
                s4 = (2 * blk + qsub) % 4
                cnt["hp"] += 1
                e.matmul(B["psS"][:, s4 * 512:s4 * 512 + 256],
                         B["qT"][:, sl + blk * 256 + qsub * 128:
                                 sl + blk * 256 + (qsub + 1) * 128],
                         B["kT"][:, sl + blk * 256:sl + (blk + 1) * 256],
                         start=True, stop=True).then_inc(sem["hp"], 1)
                assert cnt["hp"] == smm[blk][qsub]

        for blk in (0, 1):
            _emit_scores(e, blk)
        for blk in range(NB):
            for j in range(4):
                ksub, qsub = j // 2, j % 2
                e.wait_ge(sem["hv"], dml[blk][qsub])
                if j >= 2:
                    e.wait_ge(sem["hs"], ptc[blk][j - 2])
                elif blk >= 1:
                    e.wait_ge(sem["hs"], ptc[blk - 1][j + 2])
                else:
                    e.wait_ge(sem["hs"], kev[NKT - 2 + j])
                s4 = (2 * blk + qsub) % 4
                cnt["hp"] += 1
                e.transpose(B["ptr2"][:, (j % 2) * 1024:(j % 2) * 1024 + 128],
                            B["P_b"][:, s4 * 256 + ksub * 128:
                                     s4 * 256 + (ksub + 1) * 128],
                            ident[:, :]).then_inc(sem["hp"], 1)
                assert cnt["hp"] == ptr_[blk][j]
            for ksub in range(2):
                e.wait_ge(sem["hs"], ptc[blk][ksub * 2 + 1])
                if ksub == 0 and blk >= 1:
                    e.wait_ge(sem["hs"], oevp[blk - 1])
                cnt["hp"] += 1
                e.matmul(B["psA"][:, :],
                         B["v_sb"][:, blk * 256 + ksub * 128:
                                   blk * 256 + (ksub + 1) * 128],
                         B["PT_sb"][:, ksub * 256:(ksub + 1) * 256],
                         start=(ksub == 0), stop=False).then_inc(sem["hp"], 1)
                assert cnt["hp"] == avm[blk][ksub]
            for ksub in range(2):
                if blk >= 1 and ksub == 0:
                    e.wait_ge(sem["hv"], supd[blk - 1])
                cnt["hp"] += 1
                e.matmul(B["psK"][:, :],
                         B["ktk"][:, blk * 256 + ksub * 128:
                                  blk * 256 + (ksub + 1) * 128],
                         B["v_sb"][:, blk * 256 + ksub * 128:
                                   blk * 256 + (ksub + 1) * 128],
                         start=(ksub == 0), stop=(ksub == 1)).then_inc(sem["hp"], 1)
                assert cnt["hp"] == kmm[blk][ksub]
            if blk + 2 < NB:
                _emit_scores(e, blk + 2)
            if blk == 0:
                e.wait_ge(sem["hv"], sz)
            else:
                e.wait_ge(sem["hv"], sbcp[blk - 1])
            e.wait_ge(sem["hv"], qdqp[blk])
            cnt["hp"] += 1
            e.matmul(B["psA"][:, :], B["S_b"][:, :],
                     B["qdq"][:, sl + blk * 256:sl + (blk + 1) * 256],
                     start=False, stop=True).then_inc(sem["hp"], 1)
            assert cnt["hp"] == avm[blk][2]
    block.tensor(f_ht)

    def f_hs(e):
        for kt in range(NKT):
            e.wait_ge(sem["hp"], vtr[kt])
            cnt["hs"] += 1
            e.copy(B["v_sb"][:, kt * 128:(kt + 1) * 128],
                   B["ptr2"][:, (kt % 2) * 1024:(kt % 2) * 1024 + 128]
                   ).then_inc(sem["hs"], 1)
            assert cnt["hs"] == vev[kt]
        for kt in range(NKT):
            e.wait_ge(sem["hp"], ktr[kt])
            if h == 0 and kt == 0:
                e.wait_ge(sem["ii"], 48)
            cnt["hs"] += 1
            e.activation(B["ktk"][:, kt * 128:(kt + 1) * 128],
                         B["ptr2"][:, (kt % 2) * 1024:(kt % 2) * 1024 + 128],
                         AF.Copy, scale=kdec[:, h * 2 + (kt % 2):h * 2 + (kt % 2) + 1]
                         ).then_inc(sem["hs"], 1)
            assert cnt["hs"] == kev[kt]
        for blk in range(NB):
            for j in range(4):
                ksub, qsub = j // 2, j % 2
                e.wait_ge(sem["hp"], ptr_[blk][j])
                if blk >= 1:
                    e.wait_ge(sem["hp"], avm[blk - 1][2])
                cnt["hs"] += 1
                e.copy(B["PT_sb"][:, ksub * 256 + qsub * 128:
                                  ksub * 256 + (qsub + 1) * 128],
                       B["ptr2"][:, (j % 2) * 1024:(j % 2) * 1024 + 128]
                       ).then_inc(sem["hs"], 1)
                assert cnt["hs"] == ptc[blk][j]
            e.wait_ge(sem["hp"], avm[blk][2])
            if blk >= 2:
                e.wait_ge(sem["ho0" if blk % 2 == 0 else "ho1"],
                          (bho0 if blk % 2 == 0 else bho1) + 16 * ((blk - 2) // 2 + 1))
            elif h > 0:
                e.wait_ge(sem["ho0" if blk % 2 == 0 else "ho1"],
                          bho0 if blk % 2 == 0 else bho1)
            cnt["hs"] += 1
            e.copy(B["o_sb"][:, (blk % 2) * 256:(blk % 2 + 1) * 256],
                   B["psA"][:, :]).then_inc(sem["hs"], 1)
            assert cnt["hs"] == oevp[blk]
    block.scalar(f_hs)

    def f_ho(e):
        if h + 1 < HPC:
            if h >= 1:
                e.wait_ge(sem["hp"], bhp)
                e.wait_ge(sem["hv"], bhv)
            for r, t in ((h + 1, "qT"), (8 + h + 1, "kT"), (16 + h + 1, "vraw")):
                cnt["hx"] += 16
                e.dma_start(out=B[t][:, sl2:sl2 + S],
                            in_=qkvT[r * 128:(r + 1) * 128, :]
                            ).then_inc(sem["hx"], 16)
            cnt["hx"] += 16
            e.dma_start(out=B["qdc"][:, ((h + 1) % 2) * 256:((h + 1) % 2) * 256 + 256],
                        in_=qdecD.ap()[h + 1]).then_inc(sem["hx"], 16)
            cnt["hx"] += 16
            e.dma_start(out=B["Dd"][:, ((h + 1) % 2) * 512:((h + 1) % 2) * 512 + 512],
                        in_=DdecD.ap()[h + 1]).then_inc(sem["hx"], 16)
        for blk in range(NB):
            e.wait_ge(sem["hs"], oevp[blk])
            hop = "ho0" if blk % 2 == 0 else "ho1"
            cnt[hop] += 16
            e.dma_start(out=OT[h * 128:(h + 1) * 128, blk * 256:(blk + 1) * 256],
                        in_=B["o_sb"][:, (blk % 2) * 256:(blk % 2 + 1) * 256]
                        ).then_inc(sem[hop], 16)
    block.sync(f_ho)


def build_launchE():
    ctx = new_ctx("lE")
    nc, es = ctx.nc, ctx.es
    sem, cnt = ctx.sem, ctx.cnt

    x4 = nc.dram_tensor("x4", [TPC, D], F32, kind="ExternalInput")
    attnT = nc.dram_tensor("attnT", [128, NI * TPC], BF16, kind="ExternalInput")
    xn5T = nc.dram_tensor("xn5T", [128, NI * TPC], BF16, kind="ExternalInput")
    gwT = nc.dram_tensor("gwT", [NI, 128, D], F32, kind="ExternalInput")
    owT = nc.dram_tensor("owT", [NI, 128, D], F32, kind="ExternalInput")
    nwD = nc.dram_tensor("nwD", [128, NI], F32, kind="ExternalInput")
    l1wT = nc.dram_tensor("l1wT", [NI, 128, D], F32, kind="ExternalInput")
    wqkhD = nc.dram_tensor("wqkhD", [128, NI * 640], F32, kind="ExternalInput")
    BmD = nc.dram_tensor("BmD", [128, D], F32, kind="ExternalInput")
    bbD = nc.dram_tensor("bbD", [128, NI], F32, kind="ExternalInput")
    identD = nc.dram_tensor("identD", [128, 128], BF16, kind="ExternalInput")
    x6 = nc.dram_tensor("x6", [TPC, D], F32, kind="ExternalOutput")
    x5 = nc.dram_tensor("x5", [TPC, D], F32)

    with ctx.es:
        block = es.enter_context(nc.Block())
        ident = ctx.sb("ident", [128, 128], BF16)
        ptrans = ctx.ps("ptrans", [128, 2048], BF16)
        rhs_sb = ctx.sb("rhs_sb", [128, NI * TPC], BF16)
        lnb = LnBufs(ctx, nxn=1, xa2=False)
        psX = ctx.ps("psX", [128, 1024], F32)
        fw = ctx.sb("fw", [128, 2 * D], BF16)
        fps = ctx.ps("fps", [128, 2 * 1024], F32)
        fobf = ctx.sb("fobf", [128, 2 * 1024], BF16)
        fxsl = ctx.sb("fxsl", [128, 3 * 8 * 128], F32)
        fshared = (fw, fps, fobf, fxsl)
        nw = ctx.sb("nw", [128, NI], F32)
        rstdb = ctx.sb("rstdb", [128, TPC], F32)
        gbuf = ctx.sb("gbuf", [128, 2 * TPC], BF16)
        sqbuf = ctx.sb("sqbuf", [128, 2 * TPC], BF16)
        rsml = ctx.sb("rsml", [1, 2 * TPC], F32)
        rsb = ctx.sb("rsb", [1, TPC], BF16)
        epsb = ctx.sb("epsb", [1, 8], F32)
        ones128 = ctx.sb("ones128", [128, 1], BF16)
        ones1 = ctx.sb("ones1", [1, 128], BF16)
        for s_ in ("ii", "ax", "rv", "rp", "rs", "gw0", "gw1", "gp", "gc", "gv"):
            ctx.new_sem(s_)

        escope = ExitStack()
        xnT5 = escope.enter_context(nc.sbuf_tensor("xnT5", [128, NI * TPC], BF16))

        def f_ii(e):
            for o, i_ in ((ident[:, :], identD[:, :]), (nw[:, :], nwD[:, :]),
                          (rhs_sb[:, :], attnT[:, :]), (xnT5[:, :], xn5T[:, :])):
                cnt["ii"] += 16
                e.dma_start(out=o, in_=i_).then_inc(sem["ii"], 16)
        block.sync(f_ii)

        def f_vinit(e):
            ctx.serm(e.memset(epsb[:, :], EPS), "rv")
            ctx.serm(e.memset(ones128[:, :], 1.0), "rv")
            ctx.serm(e.memset(ones1[:, :], 1.0), "rv")
        block.vector(f_vinit)
        vinit = cnt["rv"]

        def f_sq(e):
            e.wait_ge(sem["ii"], 64)
            for ot in range(NI):
                if ot >= 2:
                    e.wait_ge(sem["rp"], ot - 1)
                cnt["rv"] += 1
                e.tensor_mul(sqbuf[:, (ot % 2) * TPC:(ot % 2 + 1) * TPC],
                             rhs_sb[:, ot * TPC:(ot + 1) * TPC],
                             rhs_sb[:, ot * TPC:(ot + 1) * TPC]).then_inc(sem["rv"], 1)
        block.vector(f_sq)
        sq_done = cnt["rv"]

        def f_rp(e):
            e.wait_ge(sem["rv"], vinit)
            for ot in range(NI):
                e.wait_ge(sem["rv"], vinit + ot + 1)
                for c in range(2):
                    mm = e.matmul(psX[0:1, c * 512:(c + 1) * 512], ones128[:, 0:1],
                                  sqbuf[:, (ot % 2) * TPC + c * 512:
                                        (ot % 2) * TPC + (c + 1) * 512],
                                  start=(ot == 0), stop=(ot == NI - 1))
                cnt["rp"] += 1
                mm.then_inc(sem["rp"], 1)
            e.wait_ge(sem["rv"], sq_done + 2)
            e.wait_ge(sem["rs"], 1)
            for c in range(2):
                cnt["rp"] += 1
                e.matmul(psX[:, c * 512:(c + 1) * 512], ones1[0:1, :],
                         rsb[0:1, c * 512:(c + 1) * 512],
                         start=True, stop=True).then_inc(sem["rp"], 1)
        block.tensor(f_rp)

        def f_rs(e):
            e.wait_ge(sem["rp"], NI)
            cnt["rs"] += 1
            e.activation(rsml[0:1, 0:TPC], psX[0:1, 0:TPC], AF.Sqrt,
                         scale=1.0 / D, bias=epsb[0:1, 0:1]).then_inc(sem["rs"], 1)
            e.wait_ge(sem["rp"], NI + 2)
            cnt["rs"] += 1
            e.copy(rstdb[:, :], psX[:, :]).then_inc(sem["rs"], 1)
        block.scalar(f_rs)

        def f_rv2(e):
            e.wait_ge(sem["rs"], 1)
            ctx.serm(e.reciprocal(rsml[0:1, TPC:2 * TPC], rsml[0:1, 0:TPC]), "rv")
            ctx.serw(e, "rv")
            ctx.serm(e.tensor_copy(rsb[0:1, :], rsml[0:1, TPC:2 * TPC]), "rv")
        block.vector(f_rv2)

        def f_gw(e):
            for ot in range(NI):
                if ot >= 2:
                    e.wait_ge(sem["gc"], ot - 1)
                swp = "gw0" if ot % 2 == 0 else "gw1"
                cnt[swp] += 16
                e.dma_start(out=fw[:, (ot % 2) * D:(ot % 2 + 1) * D],
                            in_=gwT.ap()[ot]).then_inc(sem[swp], 16)
        block.gpsimd(f_gw)

        def f_gt(e):
            for ot in range(NI):
                e.wait_ge(sem["gw0" if ot % 2 == 0 else "gw1"], 16 * (ot // 2 + 1))
                if ot >= 2:
                    e.wait_ge(sem["gc"], ot - 1)
                for tc in range(2):
                    for i in range(NI):
                        mm = e.matmul(fps[:, (ot % 2) * TPC + tc * 512:
                                          (ot % 2) * TPC + (tc + 1) * 512],
                                      fw[:, (ot % 2) * D + i * 128:
                                         (ot % 2) * D + (i + 1) * 128],
                                      xnT5[:, i * TPC + tc * 512:i * TPC + (tc + 1) * 512],
                                      start=(i == 0), stop=(i == NI - 1))
                cnt["gp"] += 1
                mm.then_inc(sem["gp"], 1)
        block.tensor(f_gt)

        def f_gc(e):
            for ot in range(NI):
                e.wait_ge(sem["gp"], ot + 1)
                if ot >= 2:
                    e.wait_ge(sem["gv"], 3 * (ot - 1))
                cnt["gc"] += 1
                e.activation(gbuf[:, (ot % 2) * TPC:(ot % 2 + 1) * TPC],
                             fps[:, (ot % 2) * TPC:(ot % 2 + 1) * TPC],
                             AF.Sigmoid).then_inc(sem["gc"], 1)
        block.scalar(f_gc)

        def f_gv(e):
            e.wait_ge(sem["rv"], sq_done + 2)
            e.wait_ge(sem["rs"], 2)
            for ot in range(NI):
                e.wait_ge(sem["gc"], ot + 1)
                sl = rhs_sb[:, ot * TPC:(ot + 1) * TPC]
                cnt["gv"] += 1
                e.tensor_scalar(out=sl, in0=sl, scalar1=nw[:, ot:ot + 1],
                                scalar2=None, op0=ALU.mult).then_inc(sem["gv"], 1)
                ctx.serw(e, "gv")
                cnt["gv"] += 1
                e.tensor_mul(sl, sl, rstdb[:, :]).then_inc(sem["gv"], 1)
                ctx.serw(e, "gv")
                cnt["gv"] += 1
                e.tensor_mul(sl, sl, gbuf[:, (ot % 2) * TPC:(ot % 2 + 1) * TPC]
                             ).then_inc(sem["gv"], 1)
        block.vector(f_gv)
        gv_done = cnt["gv"]
        escope.close()

        st1 = emit_fproj(ctx, block, "p",
                         w_dram=owT, nko=NI, kt=NI,
                         rhs_slice=lambda i, tc: rhs_sb[:, i * TPC + tc * 512:
                                                        i * TPC + (tc + 1) * 512],
                         n_tc=2, extras=None,
                         wait_tensor=(("gv", gv_done), ("gc", NI)),
                         evac_mode="resadd", ident=ident, ptrans=ptrans,
                         x_in_dram=x4, x_out_dram=x5, res_scale=1.0,
                         shared_bufs=fshared,
                         wait_gpsimd=[("gp", NI)],
                         wait_scalar_first=[("gv", gv_done)])

        st2 = emit_l1ffn(ctx, block, "d", x_in=x5, x_out=x6, l1wT=l1wT,
                         wqkhD=wqkhD, BmD=BmD, bbD=bbD, ident=ident,
                         ptrans=ptrans, lnb=lnb, xnT_shared=rhs_sb, psX=psX,
                         wait_sync=st1, wait_vec=[("pp", NI)],
                         fshared=fshared,
                         fgates={"tensor": [("pc", NI)],
                                 "gpsimd": [("pp", NI)],
                                 "sync": st1,
                                 "scalar": [("pt", NI * NT)],
                                 "lnpe": [("pv", NI * NT)],
                                 "l1ld": [("gp", NI)],
                                 "l1pe": []})

        def f_fin(e):
            for s_, c_ in st2:
                e.wait_ge(sem[s_], c_)
        block.sync(f_fin)
    return nc


def _prep_E(inputs):
    pr = {}
    pr["gwT"] = _tile_w(np.ascontiguousarray(inputs["L1_attn_gate_w"].T))
    pr["owT"] = _tile_w(np.ascontiguousarray(inputs["L1_attn_out_w"].T))
    pr["nwD"] = _pmaj(np.ascontiguousarray(inputs["L1_attn_norm_w"].reshape(D, 1)))
    return pr


def _xn_slice_pm(xn_b, j):
    sl = np.ascontiguousarray(xn_b[j * TPC:(j + 1) * TPC, :])
    a = sl.view(np.uint16).reshape(TPC, NI, 128).transpose(2, 1, 0)
    return np.ascontiguousarray(a.reshape(128, NI * TPC)).view(BF)


def run_E(inputs, x4, attnT5, xn5):
    prE = _prep_E(inputs)
    prC = _prep_C(inputs)
    common = _prep_common(inputs)
    nc = build_launchE()
    in_maps = []
    for c in range(8):
        b, j = c // 4, c % 4
        m = {"gwT": prE["gwT"], "owT": prE["owT"], "nwD": prE["nwD"],
             "l1wT": prC["l1wT"], "wqkhD": prC["wqkhD"], "BmD": prC["BmD"],
             "bbD": prC["bbD"], "identD": common["identD"]}
        m["x4"] = np.ascontiguousarray(x4[b, j * TPC:(j + 1) * TPC, :])
        m["attnT"] = _attnT_slice(attnT5[b], j)
        m["xn5T"] = _xn_slice_pm(xn5[b], j)
        in_maps.append(m)
    res = run_bass_kernel_spmd(nc, in_maps, core_ids=list(range(8)))
    x6 = np.zeros((2, S, D), np.float32)
    for c in range(8):
        b, j = c // 4, c % 4
        x6[b, j * TPC:(j + 1) * TPC] = res.results[c]["x6"]
    return x6

def _prep_common(inputs):
    p = {}
    p["identD"] = np.ascontiguousarray(np.eye(128, dtype=np.float32).astype(BF))
    return p


def _prep_A(inputs):
    pr = {}
    pr["wbT"] = _tile_w(np.ascontiguousarray(inputs["L0_ffn_W_base"].T))
    pr["bb"] = np.ascontiguousarray(inputs["L0_ffn_b_base"].reshape(1, D))
    pr["encT"] = _pmaj(np.concatenate([inputs["L0_ffn_enc0"].T,
                                       inputs["L0_ffn_enc1"].T], axis=1))
    pr["MTc"] = _build_MT_cols()
    pr["c0"] = np.stack([_pmaj(inputs["L0_ffn_coefs0"][e]) for e in range(3)])
    pr["m0"] = np.stack([_pmaj(inputs["L0_ffn_mask0"][e].astype(np.uint8)) for e in range(3)])
    pr["c1"] = np.stack([_pmaj(inputs["L0_ffn_coefs1"][e]) for e in range(3)])
    pr["m1"] = np.stack([_pmaj(inputs["L0_ffn_mask1"][e].astype(np.uint8)) for e in range(3)])
    return pr


def run_A(inputs):
    x0 = inputs["L0_attn_hidden_states"]
    shared = _prep_A(inputs)
    shared.update(_prep_common(inputs))
    nc = build_launchA()
    in_maps = []
    for c in range(8):
        b, j = c // 4, c % 4
        m = dict(shared)
        m["x0"] = np.ascontiguousarray(x0[b, j * TPC:(j + 1) * TPC, :])
        in_maps.append(m)
    res = run_bass_kernel_spmd(nc, in_maps, core_ids=list(range(8)))
    x1 = np.zeros((2, S, D), np.float32)
    xn2 = np.zeros((2, S, D), BF)
    for c in range(8):
        b, j = c // 4, c % 4
        x1[b, j * TPC:(j + 1) * TPC] = res.results[c]["x1"]
        xn2[b, j * TPC:(j + 1) * TPC] = res.results[c]["xn2"]
    return x1, xn2, res.results[0]["g_out"]


def _prep_B(inputs):
    wq, wk, wv = inputs["L0_attn_wq"], inputs["L0_attn_wk"], inputs["L0_attn_wv"]
    wqkv_hg = []
    for hg in range(4):
        sl = slice(hg * 1024, (hg + 1) * 1024)
        comb = np.concatenate([wq[:, sl], wk[:, sl], wv[:, sl]], axis=1)
        wqkv_hg.append(_tile_w(comb))
    cos, sin = _rope_tables()
    sinS = np.concatenate([-sin[0:64, :], sin[64:128, :]], axis=0)
    return wqkv_hg, cos, np.ascontiguousarray(sinS)


def _xnq_host(xn2_b):
    a = np.asarray(xn2_b).view(np.uint16).reshape(4, 1024, 32, 128)
    a = np.ascontiguousarray(a.transpose(0, 3, 2, 1).reshape(4, 128, 32 * 1024))
    return a.view(BF)


def run_B(inputs, xn2):
    wqkv_hg, cos, sin = _prep_B(inputs)
    common = _prep_common(inputs)
    tri = _tri_mask()
    nc = build_launchB()
    xnq = [_xnq_host(xn2[b]) for b in range(2)]
    in_maps = []
    for c in range(8):
        b, hg = c // 4, c % 4
        in_maps.append({"xnT_g": xnq[b], "wqkv": wqkv_hg[hg], "cosD": cos, "sinD": sin,
                        "identD": common["identD"], "triD": tri})
    res = run_bass_kernel_spmd(nc, in_maps, core_ids=list(range(8)))
    attnT = np.zeros((2, D, S), BF)
    for c in range(8):
        b, hg = c // 4, c % 4
        attnT[b, hg * 1024:(hg + 1) * 1024, :] = res.results[c]["OT"]
    return attnT


def _prep_C(inputs):
    pr = {}
    pr["woT"] = _tile_w(np.ascontiguousarray(inputs["L0_attn_wo"]))
    pr["wbT"] = _tile_w(np.ascontiguousarray(inputs["L0_ffn_W_base"].T))
    pr["encT"] = _pmaj(np.concatenate([inputs["L0_ffn_enc0"].T,
                                       inputs["L0_ffn_enc1"].T], axis=1))
    pr["l1wT"] = _tile_w(np.ascontiguousarray(inputs["L1_ffn_W"].T))
    A_flat = inputs["L1_ffn_A"].reshape(128, D)
    pr["wqkhD"] = _pmaj(np.ascontiguousarray(
        np.concatenate([inputs["L1_ffn_Wq"].T, inputs["L1_ffn_Wk"].T, A_flat.T],
                       axis=1)))
    pr["BmD"] = np.ascontiguousarray(
        inputs["L1_ffn_Bm"].transpose(0, 2, 1).reshape(128, D))
    pr["bbD"] = np.ascontiguousarray(
        inputs["L1_ffn_b"].astype(np.float32).reshape(NI, 128).T)
    return pr


def _attnT_slice(attnT_b, j):
    sl = np.ascontiguousarray(attnT_b[:, j * TPC:(j + 1) * TPC])
    a = sl.view(np.uint16).reshape(NI, 128, TPC).transpose(1, 0, 2)
    return np.ascontiguousarray(a.reshape(128, NI * TPC)).view(BF)


def run_C(inputs, x1, attnT, g):
    pr = _prep_C(inputs)
    pr.update(_prep_common(inputs))
    nc = build_launchC()
    in_maps = []
    for c in range(8):
        b, j = c // 4, c % 4
        m = dict(pr)
        m["x1"] = np.ascontiguousarray(x1[b, j * TPC:(j + 1) * TPC, :])
        m["attnT"] = _attnT_slice(attnT[b], j)
        m["g_in"] = g
        in_maps.append(m)
    res = run_bass_kernel_spmd(nc, in_maps, core_ids=list(range(8)))
    x4 = np.zeros((2, S, D), np.float32)
    xn5 = np.zeros((2, S, D), BF)
    for c in range(8):
        b, j = c // 4, c % 4
        x4[b, j * TPC:(j + 1) * TPC] = res.results[c]["x4"]
        xn5[b, j * TPC:(j + 1) * TPC] = res.results[c]["xn5"]
    return x4, xn5


def _prep_D(inputs):
    qd, kd, Ddec, bd = _l1_decays()
    qkvT_full = np.ascontiguousarray(inputs["L1_attn_qkv_w"].T)
    wqkv_hg, qdecs, kdecs, Ddecs, bds = [], [], [], [], []
    for hg in range(4):
        qc, kc, vc = [], [], []
        qdl = np.zeros((8, 128, 256), np.float32)
        kdl = np.zeros((128, 16), np.float32)
        Ddl = np.zeros((8, 128, 512), np.float32)
        bdl = np.zeros((128, 8), np.float32)
        for lh in range(8):
            gh = hg * 8 + lh
            base = gh * 384
            qc.append(qkvT_full[:, base:base + 128])
            kc.append(qkvT_full[:, base + 128:base + 256])
            vc.append(qkvT_full[:, base + 256:base + 384])
            qdl[lh] = np.broadcast_to(qd[gh][None, :], (128, 256))
            for sub in range(2):
                kdl[:, lh * 2 + sub] = kd[gh][sub * 128:(sub + 1) * 128]
            for qsub in range(2):
                Ddl[lh, :, qsub * 256:(qsub + 1) * 256] = \
                    Ddec[gh][qsub * 128:(qsub + 1) * 128, :].astype(np.float32)
            bdl[:, lh] = bd[gh]
        comb = np.concatenate(qc + kc + vc, axis=1)
        wqkv_hg.append(_tile_w(comb))
        qdecs.append(np.ascontiguousarray(qdl))
        kdecs.append(np.ascontiguousarray(kdl))
        Ddecs.append(np.ascontiguousarray(Ddl.astype(BF)))
        bds.append(np.ascontiguousarray(bdl))
    return wqkv_hg, qdecs, kdecs, Ddecs, bds


def run_D(inputs, xn5):
    wqkv_hg, qdecs, kdecs, Ddecs, bds = _prep_D(inputs)
    common = _prep_common(inputs)
    nc = build_launchD()
    xnq = [_xnq_host(xn5[b]) for b in range(2)]
    in_maps = []
    for c in range(8):
        b, hg = c // 4, c % 4
        in_maps.append({"xnT_g": xnq[b], "wqkv12": wqkv_hg[hg], "qdecD": qdecs[hg],
                        "kdecD": kdecs[hg], "DdecD": Ddecs[hg], "bdBD": bds[hg],
                        "identD": common["identD"]})
    res = run_bass_kernel_spmd(nc, in_maps, core_ids=list(range(8)))
    attnT = np.zeros((2, D, S), BF)
    for c in range(8):
        b, hg = c // 4, c % 4
        attnT[b, hg * 1024:(hg + 1) * 1024, :] = res.results[c]["OT"]
    return attnT


def kernel(**inputs):
    x1, xn2, g = run_A(inputs)
    attnT = run_B(inputs, xn2)
    x4, xn5 = run_C(inputs, x1, attnT, g)
    attnT5 = run_D(inputs, xn5)
    x6 = run_E(inputs, x4, attnT5, np.asarray(xn5))
    return x6
